# version 10
# baseline (speedup 1.0000x reference)
import numpy as np
import ml_dtypes
from contextlib import ExitStack
import concourse.bass as bass
import concourse.mybir as mybir
from concourse.bass_utils import run_bass_kernel_spmd

F32 = mybir.dt.float32
BF16 = mybir.dt.bfloat16
AF = mybir.ActivationFunctionType
ALU = mybir.AluOpType
AX = mybir.AxisListType

Q = 2048
D = 1024
NC = 8
QS = 4
EXPERTS = 8
HID = 2816
HC = 22
ATT0_SHIFT = 40.0
ATT1_H = 16
S_FULL = 8192
GROUPS_RG = [[0, 1, 2, 3], [4, 5, 6, 7]]

ENGS = ("t", "v", "s", "g", "y")
DEBUG_TAPS = []
DEBUG_PQDUMP = False
SERIAL_DMA = True


class Plan:

    def __init__(self):
        self.ops = {e: [] for e in ENGS}
        self.cnt = {}
        self.waited = {e: {} for e in ENGS}

    def mark(self, sem):
        return self.cnt.get(sem, 0)

    def emit(self, eng, fn=None, wait=(), inc=()):
        w2 = []
        for s, v in wait:
            if v <= 0:
                continue
            assert v <= self.cnt.get(s, 0), \
                f"wait {s}>={v} exceeds planned {self.cnt.get(s, 0)}"
            if v > self.waited[eng].get(s, 0):
                self.waited[eng][s] = v
                w2.append((s, v))
        incs = tuple(inc)
        assert len(incs) <= 1
        for s, a in incs:
            self.cnt[s] = self.cnt.get(s, 0) + a
        self.ops[eng].append((fn, tuple(w2), incs))
        return self.cnt.get(incs[0][0], None) if incs else None

    def t(self, fn, wait=(), inc=False):
        return self.emit("t", fn, wait, [("T", 1)] if inc else [])

    def v(self, fn, wait=(), inc=True):
        return self.emit("v", fn, wait, [("V", 1)] if inc else [])

    def s(self, fn, wait=(), inc=True):
        return self.emit("s", fn, wait, [("S", 1)] if inc else [])

    def g(self, fn, wait=(), inc=False, dma=False):
        if dma:
            sem = "DG" if dma is True else dma
            w = list(wait) + ([(sem, self.cnt.get(sem, 0))] if SERIAL_DMA else [])
            return self.emit("g", fn, w, [(sem, 16)])
        return self.emit("g", fn, wait, [("G", 1)] if inc else [])

    def y(self, fn, wait=(), dma=True):
        if dma:
            w = list(wait) + ([("DY", self.cnt.get("DY", 0))] if SERIAL_DMA else [])
            return self.emit("y", fn, w, [("DY", 16)])
        return self.emit("y", fn, wait, [])


class Ctx:
    pass


MAP_BASE = 18944
OFF_X = MAP_BASE
OFF_FSCR = MAP_BASE + 65536
FS_SQ = OFF_FSCR
FS_TMP = OFF_FSCR + 4096
FS_RB = OFF_FSCR + 8192
FS_MRB = OFF_FSCR + 12288
FS_ACC = FS_SQ
FS_ROWS = OFF_FSCR + 16384
OFF_BFA = OFF_FSCR + 24576
OFF_CONST = OFF_BFA + 114688
TOTAL_LIMIT = 229376

BFA_XB = 0
BFA_PL = 16384
BFA_DISHA = 18432
BFA_SSEL = 19456
BFA_QT0 = 16384
BFA_KTF = 32768
BFA_VF = 40960
BFA_OT = 0
BFA_E0 = 8192
BFA_WS2 = 10240
BFA_KVSTAGE = 32768
BFA_PKPV = 36864
BFA_W0T = 49152
BFA_HB = 16384
BFA_W13 = 38912
BFA_W2R = 43008
BFA_GEB = 54272
BFA_MTMP = 56320
BFA_WVR = 24576
BFA_VSTG = 22528
BFA_QT1 = 16384
BFA_KT1 = 32768
BFA_V1 = 0
BFA_HALO = 49152
BFA_E3 = 51200
BFA_E3T = 53248
BFA_QLB = 54272
BFA_CONV = 54784
BFA_WS5 = 55552
BFA_BVB = 23552


def bfa(c, elem_off, parts, width):
    return c.BFA[0:parts, elem_off:elem_off + width]


def XBc(c, dc, sl=None):
    if sl is None:
        return bfa(c, BFA_XB + 2048 * dc, 128, Q)
    return c.BFA[0:128, BFA_XB + 2048 * dc + sl.start:BFA_XB + 2048 * dc + sl.stop]


def _sem_names():
    return ["T", "V", "S", "G", "DY", "DG", "DG2", "DG3", "DG4", "DG5"] + [f"CC{i}" for i in range(8)]


def build_program(sublayers=(0, 1, 2, 3, 4, 5)):
    nc = bass.Bass("TRN2", target_bir_lowering=False, debug=False)
    p = Plan()
    c = Ctx()
    c.nc = nc
    c.p = p
    c.mk_xb_free = 0

    need_ffn0 = 0 in sublayers or 2 in sublayers
    need_attn0 = 1 in sublayers
    need_moe = 3 in sublayers or 5 in sublayers
    need_attn1 = 4 in sublayers

    din = {}

    def DIN(name, shape):
        din[name] = nc.dram_tensor(name, list(shape), F32, kind="ExternalInput").ap()
        return din[name]

    DIN("xin", (D, Q))
    c.xout_ap = nc.dram_tensor("xout", [D, Q], F32, kind="ExternalOutput").ap()
    DIN("CONSTS", (128, 384))
    if need_ffn0:
        DIN("W0T", (NC, 128, D))
        DIN("DISHA", (64, D))
        DIN("SSEL", (128, 64))
    if need_attn0:
        if DEBUG_PQDUMP:
            t = nc.dram_tensor("dbg_pq", [8, 128, 1024], BF16, kind="ExternalOutput")
            c.dbg_pq = t.ap()
        DIN("PQ", (8, NC, 128, 128))
        DIN("PK", (NC, 128, 128))
        DIN("PV", (NC, 128, 128))
        DIN("POT", (8, 128, D))
    if need_moe:
        DIN("MS", (EXPERTS, HC, 3, 128, D))
        DIN("GWT", (NC, 128, 8))
    if need_attn1:
        DIN("WQJ", (NC, NC, 128, 128))
        DIN("WKJ", (NC, NC, 128, 128))
        DIN("WV", (NC, 128, D))
        DIN("BQ", (128, 8))
        DIN("BK", (128, 8))
        DIN("BV", (1, D))
        DIN("CONVT", (ATT1_H, 3, 128, 128))
        DIN("EYES", (64, 192))
        din["HALOIDX"] = nc.dram_tensor("HALOIDX", [128, 2], mybir.dt.int32,
                                        kind="ExternalInput").ap()
    c.din = din

    c.dram = {}
    if need_attn0:
        c.dram["pqb"] = nc.dram_tensor("pqb", [8, 128, 1024], BF16, kind="Internal")
        c.dram["kv_in"] = nc.dram_tensor("kv_in", [128, 4096], BF16, kind="Internal")
        c.dram["kv_out"] = nc.dram_tensor("kv_out", [512, 4096], BF16, kind="Internal")
    if need_attn1:
        c.dram["v_dram"] = nc.dram_tensor("v_dram", [16, 128, D], BF16, kind="Internal")
        c.dram["lm_in"] = nc.dram_tensor("lm_in", [2048, 16], F32, kind="Internal")
        c.dram["lm_out"] = nc.dram_tensor("lm_out", [8192, 16], F32, kind="Internal")
        c.dram["mx_in"] = nc.dram_tensor("mx_in", [64, 16], F32, kind="Internal")
        c.dram["mx_out"] = nc.dram_tensor("mx_out", [64, 16], F32, kind="Internal")
        c.dram["uz_in"] = nc.dram_tensor("uz_in", [64, 1040], F32, kind="Internal")
        c.dram["uz_out"] = nc.dram_tensor("uz_out", [64, 1040], F32, kind="Internal")
        c.dram["vh_in"] = nc.dram_tensor("vh_in", [256, D], BF16, kind="Internal")
        c.dram["vh_out"] = nc.dram_tensor("vh_out", [1024, D], BF16, kind="Internal")

    with ExitStack() as stack:
        def sb_at(name, shape, dt, off):
            return nc.alloc_sbuf_tensor_at(name, list(shape), dt, offset=off)

        c.X = [sb_at(f"X{dc}", [128, Q], F32, OFF_X + 8192 * dc) for dc in range(NC)]
        c.fs_sq = sb_at("fs_sq", [128, 1024], F32, FS_SQ)
        c.fs_tmp = sb_at("fs_tmp", [128, 1024], F32, FS_TMP)
        c.fs_rb = sb_at("fs_rb", [128, 1024], F32, FS_RB)
        c.fs_mrb = sb_at("fs_mrb", [128, 1024], F32, FS_MRB)
        c.fs_acc = sb_at("fs_acc", [128, 1024], F32, FS_ACC)
        c.fs_rows = sb_at("fs_rows", [1, 2048], F32, FS_ROWS)
        c.BFA = sb_at("BFA", [128, 57344], BF16, OFF_BFA)
        c.ONES = sb_at("ONES", [128, 128], F32, OFF_CONST)
        c.IDENT = sb_at("IDENT", [128, 128], F32, OFF_CONST + 512)
        c.IDENTB = sb_at("IDENTB", [128, 128], BF16, OFF_CONST + 1024)
        c.NEG8 = sb_at("NEG8", [128, 8], F32, OFF_CONST + 1280)
        c.CBIAS = sb_at("CBIAS", [128, 8], F32, OFF_CONST + 1312)
        c.CEXTRA = sb_at("CEXTRA", [128, 624], F32, OFF_CONST + 1344)
        assert OFF_CONST + 1344 + 2496 <= TOTAL_LIMIT

        c.QLF = sb_at("QLF", [128, 1024], F32, FS_SQ)
        c.EYES_T = sb_at("EYES_T", [64, 192], F32, OFF_CONST + 1344 + 2496 + 32)
        c.SC64 = sb_at("SC64", [64, 1024], F32, FS_TMP)
        c.K1T = sb_at("K1T", [64, 2048], F32, FS_RB)
        c.QKL = sb_at("QKL", [128, 512], F32, FS_RB)
        c.QKL2 = sb_at("QKL2", [128, 512], F32, FS_RB + 2048)
        c.INV2 = sb_at("INV2", [64, 1024], F32, FS_ROWS)
        c.UBUF = sb_at("UBUF", [64, 1024], F32, FS_ROWS + 4096)
        c.ATTNT = sb_at("ATTNT", [64, 2048], F32, FS_SQ)
        c.HIDX = nc.alloc_sbuf_tensor_at("HIDX", [128, 2], mybir.dt.int32,
                                         offset=OFF_CONST + 1344 + 2496)
        c.PSA = stack.enter_context(nc.psum_tensor("PSA", [128, 1024], F32))
        c.PSB = stack.enter_context(nc.psum_tensor("PSB", [128, 1024], F32))
        c.PSC = stack.enter_context(nc.psum_tensor("PSC", [128, 1024], F32))
        c.PSE = stack.enter_context(nc.psum_tensor("PSE", [128, 2048], BF16))

        sems = {}
        for sn in _sem_names():
            sems[sn] = stack.enter_context(nc.semaphore(sn))
        c.sems = sems
        block = stack.enter_context(nc.Block())

        plan_all(c, sublayers)

        def runner(eng_key):
            def run(eng):
                for fn, waits, incs in p.ops[eng_key]:
                    for s, v in waits:
                        eng.wait_ge(sems[s], v)
                    inst = fn(eng) if fn is not None else None
                    for s, a in incs:
                        if inst is None:
                            inst = eng.nop()
                        inst.then_inc(sems[s], a)
            return run

        block.tensor(runner("t"))
        block.vector(runner("v"))
        block.scalar(runner("s"))
        block.gpsimd(runner("g"))
        block.sync(runner("y"))

    return nc


def plan_all(c, sublayers):
    p = c.p
    CON = c.din["CONSTS"]
    p.y(lambda e: e.dma_start(out=c.ONES[:, :], in_=CON[:, 0:128]))
    p.y(lambda e: e.dma_start(out=c.IDENT[:, :], in_=CON[:, 128:256]))
    p.g(lambda e: e.dma_start(out=c.IDENTB[:, :], in_=CON[:, 128:256]), dma=True)
    p.y(lambda e: e.dma_start(out=c.NEG8[:, :], in_=CON[:, 256:264]))
    p.y(lambda e: e.dma_start(out=c.CBIAS[:, 0:4], in_=CON[:, 264:268]))
    xin = c.din["xin"]
    for dc in range(NC):
        p.y(lambda e, dc=dc: e.dma_start(out=c.X[dc][:, :],
                                         in_=xin[128 * dc:128 * (dc + 1), :]))
    c.mk_ld = p.mark("DY")
    c.mk_ldg = p.mark("DG")

    if 0 in sublayers or 2 in sublayers:
        W0T = c.din["W0T"]
        for kc in range(NC):
            p.g(lambda e, kc=kc: e.dma_start(out=bfa(c, BFA_W0T + 1024 * kc, 128, 1024),
                                             in_=W0T[kc]), dma=True)
    c.mk_w0 = p.mark("DG")

    first = True
    for sl in sublayers:
        if sl in (0, 2):
            emit_ln(c, half=True, first=first)
            emit_ffn0(c)
        elif sl == 1:
            emit_ln(c, half=False, first=first)
            emit_attn0(c)
        elif sl in (3, 5):
            emit_ln(c, half=False, first=first)
            emit_moe(c)
        elif sl == 4:
            emit_ln(c, half=False, first=first)
            emit_attn1(c)
        first = False

    mkv = p.mark("V")
    for dc in range(NC):
        p.y(lambda e, dc=dc: e.dma_start(
            out=c.xout_ap[128 * dc:128 * (dc + 1), :], in_=c.X[dc][:, :]),
            wait=[("V", mkv)])
    for name, ap_fn, shape in DEBUG_TAPS:
        dout = c.nc.dram_tensor("dbg_" + name, list(shape), ap_fn(c).dtype,
                                kind="ExternalOutput").ap()
        p.y(lambda e, dout=dout, ap_fn=ap_fn: e.dma_start(out=dout, in_=ap_fn(c)),
            wait=[("V", mkv), ("T", p.mark("T")), ("S", p.mark("S"))])
    p.emit("y", None, wait=[("DY", p.mark("DY"))])


def emit_ln(c, half, first=False):
    p = c.p
    mkv_entry = p.mark("V")
    ROW_M = c.fs_rows[0:1, 0:1024]
    ROW_S = c.fs_rows[0:1, 1024:2048]
    ROW_T = c.fs_tmp[0:1, 0:1024]
    wait_ld = [("DY", c.mk_ld)] if first else []
    mk_prev_evac = getattr(c, "mk_ln_evac", 0)

    for qt in range(2):
        qlo = 1024 * qt
        mk_sumsq = 0
        for dc in range(NC):
            p.s(lambda e, dc=dc, qlo=qlo: e.activation(
                out=c.fs_sq[:, :], in_=c.X[dc][:, qlo:qlo + 1024], func=AF.Square),
                wait=wait_ld + [("V", mkv_entry), ("T", mk_sumsq)])
            mksq = p.mark("S")
            for hi in range(2):
                sl = slice(512 * hi, 512 * hi + 512)
                p.t(lambda e, dc=dc, qlo=qlo, sl=sl: e.matmul(
                    c.PSC[0:1, sl], lhsT=c.ONES[:, 0:1],
                    rhs=c.X[dc][:, qlo + sl.start:qlo + sl.stop],
                    start=(dc == 0), stop=(dc == NC - 1)),
                    wait=wait_ld + [("V", mkv_entry), ("S", mk_prev_evac)])
                mk_sumsq = p.t(lambda e, dc=dc, sl=sl: e.matmul(
                    c.PSA[0:1, sl], lhsT=c.ONES[:, 0:1], rhs=c.fs_sq[:, sl],
                    start=(dc == 0), stop=(dc == NC - 1)),
                    wait=[("S", mksq)], inc=True)
        mk_stats = p.mark("T")
        p.s(lambda e: e.activation(out=ROW_M, in_=c.PSC[0:1, 0:1024], func=AF.Copy,
                                   scale=1.0 / D), wait=[("T", mk_stats)])
        p.s(lambda e: e.activation(out=ROW_S, in_=c.PSA[0:1, 0:1024], func=AF.Copy,
                                   scale=1.0 / D))
        mks = p.mark("S")
        p.v(lambda e: e.tensor_tensor(out=ROW_T, in0=ROW_M, in1=ROW_M, op=ALU.mult),
            wait=[("S", mks)])
        p.v(lambda e: e.tensor_tensor(out=ROW_S, in0=ROW_S, in1=ROW_T, op=ALU.subtract))
        mkv2 = p.mark("V")
        p.s(lambda e: e.activation(out=ROW_S, in_=ROW_S, func=AF.Ln,
                                   bias=c.CBIAS[0:1, 0:1]),
            wait=[("V", mkv2)])
        p.s(lambda e: e.activation(out=ROW_S, in_=ROW_S, func=AF.Exp, scale=-0.5))
        if half:
            p.s(lambda e: e.activation(out=ROW_S, in_=ROW_S, func=AF.Copy, scale=0.5))
        mks2 = p.mark("S")
        p.v(lambda e: e.tensor_tensor(out=ROW_T, in0=ROW_M, in1=ROW_S, op=ALU.mult),
            wait=[("S", mks2)])
        mkv3 = p.mark("V")
        for hi in range(2):
            sl = slice(512 * hi, 512 * hi + 512)
            p.t(lambda e, sl=sl: e.matmul(c.PSA[:, sl], lhsT=c.ONES[0:1, :],
                                          rhs=ROW_S[:, sl], start=True, stop=True),
                wait=[("V", mkv3), ("S", mks2)])
            p.t(lambda e, sl=sl: e.matmul(c.PSB[:, sl], lhsT=c.ONES[0:1, :],
                                          rhs=ROW_T[:, sl], start=True, stop=True),
                inc=(hi == 1))
        mk_bc = p.mark("T")
        p.s(lambda e: e.copy(out=c.fs_rb[:, :], in_=c.PSA[:, :]), wait=[("T", mk_bc)])
        p.s(lambda e: e.copy(out=c.fs_mrb[:, :], in_=c.PSB[:, :]))
        mk_prev_evac = p.mark("S")
        for dc in range(NC):
            p.v(lambda e, dc=dc, qlo=qlo: e.tensor_tensor(
                out=c.fs_tmp[:, :], in0=c.X[dc][:, qlo:qlo + 1024],
                in1=c.fs_rb[:, :], op=ALU.mult),
                wait=[("S", mk_prev_evac), ("T", c.mk_xb_free)])
            p.v(lambda e, dc=dc, qlo=qlo: e.tensor_tensor(
                out=c.BFA[0:128, BFA_XB + 2048 * dc + qlo:BFA_XB + 2048 * dc + qlo + 1024],
                in0=c.fs_tmp[:, :], in1=c.fs_mrb[:, :], op=ALU.subtract))
    c.mk_ln_evac = mk_prev_evac
    c.mk_ln_done = p.mark("V")


def emit_ffn0(c):
    p = c.p
    mkln = c.mk_ln_done
    p.g(lambda e: e.dma_start(out=bfa(c, BFA_DISHA, 64, 1024), in_=c.din["DISHA"]),
        wait=[("T", c.mk_xb_free)], dma=True)
    p.g(lambda e: e.dma_start(out=bfa(c, BFA_SSEL, 128, 64), in_=c.din["SSEL"]),
        dma=True)
    mk_small = p.mark("DG")
    DISHAB = bfa(c, BFA_DISHA, 64, 1024)
    SSELB = bfa(c, BFA_SSEL, 128, 64)

    for qs in range(QS):
        sl = slice(512 * qs, 512 * (qs + 1))
        psl = slice(512 * (qs % 2), 512 * (qs % 2) + 512)
        for kc in range(NC):
            p.t(lambda e, kc=kc, sl=sl, psl=psl: e.matmul(
                c.PSC[0:64, psl], lhsT=SSELB, rhs=XBc(c, kc, sl),
                start=(kc == 0), stop=(kc == NC - 1)),
                wait=[("V", mkln), ("DG", max(c.mk_w0, mk_small)),
                      ("S", p.mark("S"))],
                inc=(kc == NC - 1))
        mkt = p.mark("T")
        p.s(lambda e, sl=sl, psl=psl: e.copy(
            out=c.BFA[0:64, BFA_PL + sl.start:BFA_PL + sl.stop],
            in_=c.PSC[0:64, psl]), wait=[("T", mkt)])
    mk_pool = p.mark("S")

    slot_free = [0, 0, 0, 0]
    sidx = 0
    for oc in range(NC):
        for qs in range(QS):
            sl = slice(512 * qs, 512 * (qs + 1))
            ps_t = c.PSA if (sidx % 4) < 2 else c.PSB
            psl = slice(512 * (sidx % 2), 512 * (sidx % 2) + 512)
            for kc in range(NC):
                p.t(lambda e, kc=kc, oc=oc, sl=sl, ps_t=ps_t, psl=psl: e.matmul(
                    ps_t[:, psl],
                    lhsT=bfa(c, BFA_W0T + 1024 * kc + 128 * oc, 128, 128),
                    rhs=XBc(c, kc, sl), start=(kc == 0), stop=False),
                    wait=[("V", slot_free[sidx % 4])] if kc == 0 else [])
            p.t(lambda e, oc=oc, sl=sl, psl=psl, ps_t=ps_t: e.matmul(
                ps_t[:, psl], lhsT=DISHAB[:, 128 * oc:128 * (oc + 1)],
                rhs=c.BFA[0:64, BFA_PL + sl.start:BFA_PL + sl.stop],
                start=False, stop=True),
                wait=[("S", mk_pool)], inc=True)
            mkt = p.mark("T")
            slot_free[sidx % 4] = p.v(
                lambda e, oc=oc, sl=sl, ps_t=ps_t, psl=psl: e.tensor_tensor(
                    out=c.X[oc][:, sl], in0=c.X[oc][:, sl], in1=ps_t[:, psl],
                    op=ALU.add),
                wait=[("T", mkt)])
            sidx += 1
    c.mk_xb_free = p.mark("T")


def emit_attn0(c):
    p = c.p
    mkln = c.mk_ln_done
    PQ = c.din["PQ"]
    POT = c.din["POT"]

    PKB = 36864
    PVB = 37888
    PQBUF = 38912
    POTBUF = 12288
    STG_K = 32768
    STG_V = 34816
    for h0 in range(8):
        p.g(lambda e, h0=h0: e.dma_start(
            out=c.dram["pqb"].ap()[h0].rearrange("p (k j) -> p k j", k=8),
            in_=PQ[h0].rearrange("k p j -> p k j")), dma="DG2")
    mk_pqcast = p.mark("DG2")
    p.g(lambda e: e.dma_start(out=bfa(c, PKB, 128, 1024).rearrange("p (k j) -> p k j", k=8),
                              in_=c.din["PK"].rearrange("k p j -> p k j")), dma=True)
    p.g(lambda e: e.dma_start(out=bfa(c, PVB, 128, 1024).rearrange("p (k j) -> p k j", k=8),
                              in_=c.din["PV"].rearrange("k p j -> p k j")), dma=True)
    mk_pkv = p.mark("DG")

    for qs in range(QS):
        sl = slice(512 * qs, 512 * (qs + 1))
        ps_t = c.PSA if qs % 2 == 0 else c.PSB
        psl = slice(512, 1024)
        for kc in range(NC):
            p.t(lambda e, kc=kc, sl=sl, ps_t=ps_t, psl=psl: e.matmul(
                ps_t[:, psl], lhsT=bfa(c, PKB + 128 * kc, 128, 128),
                rhs=XBc(c, kc, sl), start=(kc == 0), stop=(kc == NC - 1)),
                wait=([("DG", mk_pkv), ("S", p.mark("S")), ("V", mkln)]
                      if kc == 0 else []),
                inc=(kc == NC - 1))
        mkt = p.mark("T")
        p.s(lambda e, sl=sl, ps_t=ps_t, psl=psl: e.copy(
            out=c.BFA[0:128, STG_K + sl.start:STG_K + sl.stop], in_=ps_t[:, psl]),
            wait=[("T", mkt)])

    for tc in range(16):
        tsl = slice(128 * tc, 128 * (tc + 1))
        psl = slice(512 * (tc % 2), 512 * (tc % 2) + 128)
        for kc in range(NC):
            p.t(lambda e, kc=kc, tsl=tsl, psl=psl: e.matmul(
                c.PSC[:, psl], lhsT=XBc(c, kc, tsl),
                rhs=bfa(c, PVB + 128 * kc, 128, 128),
                start=(kc == 0), stop=(kc == NC - 1)),
                wait=[("S", p.mark("S"))] if kc == 0 else [],
                inc=(kc == NC - 1))
        mkt = p.mark("T")
        p.s(lambda e, tc=tc, psl=psl: e.copy(
            out=bfa(c, STG_V + 128 * tc, 128, 128), in_=c.PSC[:, psl]),
            wait=[("T", mkt)])
    c.mk_xb_free = p.mark("T")
    mk_stage = p.mark("S")

    kv_in = c.dram["kv_in"]
    kv_out = c.dram["kv_out"]
    p.y(lambda e: e.dma_start(out=kv_in.ap(), in_=bfa(c, STG_K, 128, 4096)),
        wait=[("S", mk_stage)])
    mk_kvout = p.mark("DY")
    p.emit("g", lambda e: e.collective_compute(
        "AllGather", ALU.bypass, replica_groups=GROUPS_RG,
        ins=[kv_in.ap().opt()], outs=[kv_out.ap().opt()]),
        wait=[("DY", mk_kvout)], inc=[("CC0", 1)])
    pq_mk = [0, 0]
    qt_mks = []
    for h in range(8):
        buf = h % 2
        p.y(lambda e, h=h, buf=buf: e.dma_start(
            out=bfa(c, PQBUF + 1024 * buf, 128, 1024),
            in_=c.dram["pqb"].ap()[h]),
            wait=[("T", pq_mk[buf]), ("DG2", mk_pqcast)])
        mk_dma = p.mark("DY")
        for qs in range(QS):
            sl = slice(512 * qs, 512 * (qs + 1))
            ps_t = c.PSA if qs % 2 == 0 else c.PSB
            psl = slice(0, 512)
            for kc in range(NC):
                p.t(lambda e, h=h, buf=buf, kc=kc, sl=sl, ps_t=ps_t, psl=psl: e.matmul(
                    ps_t[:, psl], lhsT=bfa(c, PQBUF + 1024 * buf + 128 * kc, 128, 128),
                    rhs=XBc(c, kc, sl), start=(kc == 0), stop=(kc == NC - 1)),
                    wait=([("V", mkln), ("DY", mk_dma), ("S", p.mark("S"))]
                          if kc == 0 else []),
                    inc=(kc == NC - 1))
            mkt = p.mark("T")
            if qs == QS - 1:
                pq_mk[buf] = mkt
                if DEBUG_PQDUMP:
                    p.y(lambda e, h=h, buf=buf: e.dma_start(
                        out=c.dbg_pq[h], in_=bfa(c, PQBUF + 1024 * buf, 128, 1024)),
                        wait=[("T", mkt)])
            p.s(lambda e, h=h, sl=sl, ps_t=ps_t, psl=psl: e.copy(
                out=c.BFA[0:128, BFA_QT0 + 2048 * h + sl.start:BFA_QT0 + 2048 * h + sl.stop],
                in_=ps_t[:, psl]), wait=[("T", mkt)])

    c.mk_xb_free = p.mark("T")
    for r in range(4):
        p.y(lambda e, r=r: e.dma_start(
            out=bfa(c, BFA_KTF + 2048 * r, 128, 2048),
            in_=kv_out.ap()[128 * r:128 * (r + 1), 0:2048]),
            wait=[("CC0", 1), ("DY", mk_kvout), ("T", c.mk_xb_free)])
        p.y(lambda e, r=r: e.dma_start(
            out=bfa(c, BFA_VF + 2048 * r, 128, 2048),
            in_=kv_out.ap()[128 * r:128 * (r + 1), 2048:4096]))
    mk_kv = p.mark("DY")

    mk_ot_free = [0] * 8
    pot_mk = [0, 0]
    for qt in range(2):
        qlo = 1024 * qt
        for h in range(8):
            hstart_t = p.mark("T")
            hstart_s = p.mark("S")
            hstart_v = p.mark("V")
            mk_exp = [0, 0]
            mk_add = 0
            mk_addp = [0, 0, 0, 0]
            for mc in range(65):
                if mc < 64:
                    ps_t = c.PSA if mc % 2 == 0 else c.PSB
                    for hi in range(2):
                        p.t(lambda e, mc=mc, hi=hi, ps_t=ps_t, h=h, qlo=qlo: e.matmul(
                            ps_t[:, 512 * hi:512 * hi + 512],
                            lhsT=bfa(c, BFA_KTF + 128 * mc, 128, 128),
                            rhs=c.BFA[0:128, BFA_QT0 + 2048 * h + qlo + 512 * hi:
                                      BFA_QT0 + 2048 * h + qlo + 512 * hi + 512],
                            start=True, stop=True),
                            wait=([("DY", mk_kv), ("S", max(mk_exp[mc % 2], hstart_s)),
                                   ("V", hstart_v)]
                                  if hi == 0 else []),
                            inc=(hi == 1))
                    mk_l = p.mark("T")
                    mk_exp[mc % 2] = p.s(
                        lambda e, mc=mc, ps_t=ps_t: e.activation(
                            out=bfa(c, BFA_E0 + 1024 * (mc % 4), 128, 1024),
                            in_=ps_t[:, :], func=AF.Exp, bias=c.CBIAS[:, 1:2]),
                        wait=[("T", max(mk_l, c.mk_xb_free)), ("V", mk_addp[mc % 4])])
                    if mc == 0:
                        mk_add = p.v(lambda e: e.tensor_copy(
                            out=c.fs_acc[:, :], in_=bfa(c, BFA_E0, 128, 1024)),
                            wait=[("S", mk_exp[0]), ("T", hstart_t)])
                    else:
                        mk_add = p.v(lambda e, mc=mc: e.tensor_tensor(
                            out=c.fs_acc[:, :], in0=c.fs_acc[:, :],
                            in1=bfa(c, BFA_E0 + 1024 * (mc % 4), 128, 1024), op=ALU.add),
                            wait=[("S", mk_exp[mc % 2])])
                    mk_addp[mc % 4] = mk_add
                if mc >= 1:
                    m = mc - 1
                    for hi in range(2):
                        p.t(lambda e, m=m, hi=hi: e.matmul(
                            c.PSC[:, 512 * hi:512 * hi + 512],
                            lhsT=bfa(c, BFA_VF + 128 * m, 128, 128),
                            rhs=c.BFA[0:128, BFA_E0 + 1024 * (m % 4) + 512 * hi:
                                      BFA_E0 + 1024 * (m % 4) + 512 * hi + 512],
                            start=(m == 0), stop=(m == 63)),
                            wait=([("S", mk_exp[m % 2])] if hi == 0 else []),
                            inc=(hi == 1 and m == 63))
            mk_av = p.mark("T")
            for hi in range(2):
                p.t(lambda e, hi=hi: e.matmul(
                    c.PSA[0:1, 512 * hi:512 * hi + 512], lhsT=c.ONES[:, 0:1],
                    rhs=c.fs_acc[:, 512 * hi:512 * hi + 512], start=True, stop=True),
                    wait=[("V", mk_add), ("S", mk_exp[0])], inc=(hi == 1))
            mk_sum = p.mark("T")
            ROW = c.fs_rows[0:1, 0:1024]
            p.v(lambda e: e.reciprocal(out=ROW, in_=c.PSA[0:1, 0:1024]),
                wait=[("T", mk_sum)])
            mk_rec = p.mark("V")
            for hi in range(2):
                p.t(lambda e, hi=hi: e.matmul(
                    c.PSB[:, 512 * hi:512 * hi + 512], lhsT=c.ONES[0:1, :],
                    rhs=ROW[:, 512 * hi:512 * hi + 512], start=True, stop=True),
                    wait=[("V", mk_rec), ("S", mk_exp[1])], inc=(hi == 1))
            mk_bc = p.mark("T")
            p.s(lambda e: e.copy(out=c.fs_rb[:, :], in_=c.PSB[:, :]),
                wait=[("T", mk_bc)])
            mk_rbe = p.mark("S")
            p.v(lambda e, h=h: e.tensor_tensor(
                out=bfa(c, BFA_OT + 1024 * h, 128, 1024), in0=c.PSC[:, :],
                in1=c.fs_rb[:, :], op=ALU.mult),
                wait=[("S", mk_rbe), ("T", max(mk_av, mk_ot_free[h]))])
        mk_norm = p.mark("V")

        yslot_free = [0, 0]
        for dc in range(NC):
            buf = dc % 2
            p.g(lambda e, dc=dc, buf=buf: e.dma_start(
                out=bfa(c, POTBUF + 1024 * buf, 128, 1024).rearrange("p (h j) -> p h j", h=8),
                in_=POT[:, :, 128 * dc:128 * (dc + 1)].rearrange("h p j -> p h j")),
                wait=[("T", max(pot_mk[buf], c.mk_xb_free))], dma="DG2")
            mk_dma = p.mark("DG2")
            for qs in range(2):
                si = (dc * 2 + qs) % 2
                ps_t = c.PSA if si == 0 else c.PSB
                for h in range(8):
                    p.t(lambda e, dc=dc, qs=qs, h=h, buf=buf, ps_t=ps_t: e.matmul(
                        ps_t[:, 0:512],
                        lhsT=bfa(c, POTBUF + 1024 * buf + 128 * h, 128, 128),
                        rhs=bfa(c, BFA_OT + 1024 * h + 512 * qs, 128, 512),
                        start=(h == 0), stop=(h == 7)),
                        wait=[("V", max(mk_norm, yslot_free[si])), ("DG2", mk_dma)]
                        if h == 0 else [],
                        inc=(h == 7))
                mkt = p.mark("T")
                if qs == 1:
                    pot_mk[buf] = mkt
                yslot_free[si] = p.v(
                    lambda e, dc=dc, qs=qs, qlo=qlo, ps_t=ps_t: e.tensor_tensor(
                        out=c.X[dc][:, qlo + 512 * qs:qlo + 512 * qs + 512],
                        in0=c.X[dc][:, qlo + 512 * qs:qlo + 512 * qs + 512],
                        in1=ps_t[:, 0:512], op=ALU.add),
                    wait=[("T", mkt)])
        mk_y = p.mark("T")
        for h in range(8):
            mk_ot_free[h] = mk_y


def emit_moe(c):
    p = c.p
    mkln = c.mk_ln_done
    MSD = c.din["MS"]
    GWTB = bfa(c, BFA_MTMP, 128, 64)
    GTOK = c.fs_tmp[:, 0:128]
    GROW = c.fs_rows[0:1, 0:2048]
    SC = c.fs_mrb

    p.g(lambda e: e.dma_start(out=GWTB.rearrange("p (k e) -> p k e", k=8),
                              in_=c.din["GWT"].rearrange("k p e -> p k e")),
        wait=[("S", p.mark("S"))], dma="DG3")
    mk_gw = p.mark("DG3")

    ev_mks = [0, 0]
    for tc in range(16):
        psl = slice(512 * (tc % 2), 512 * (tc % 2) + 8)
        tsl = slice(128 * tc, 128 * (tc + 1))
        for kc in range(NC):
            p.t(lambda e, kc=kc, tsl=tsl, psl=psl: e.matmul(
                c.PSA[:, psl], lhsT=XBc(c, kc, tsl),
                rhs=GWTB[:, 8 * kc:8 * kc + 8],
                start=(kc == 0), stop=(kc == NC - 1)),
                wait=([("V", mkln), ("DG3", mk_gw), ("S", ev_mks[tc % 2])]
                      if kc == 0 else []),
                inc=(kc == NC - 1))
        mkt = p.mark("T")
        ev_mks[tc % 2] = p.s(lambda e, tc=tc, psl=psl: e.copy(
            out=GTOK[:, 8 * tc:8 * tc + 8], in_=c.PSA[:, psl]), wait=[("T", mkt)])
    mk_glog = p.mark("S")

    M1N = SC[:, 0:1]
    M1P = SC[:, 1:2]
    M2N = SC[:, 2:3]
    M2P = SC[:, 3:4]
    ZS = SC[:, 4:5]
    RS = SC[:, 5:6]
    SEXP = SC[:, 8:16]
    L2 = SC[:, 16:24]
    KEEP = SC[:, 24:32]
    for tc in range(16):
        G8 = GTOK[:, 8 * tc:8 * tc + 8]
        mv = p.v(lambda e, G8=G8: e.tensor_reduce(out=M1N, in_=G8, axis=AX.X,
                                                  op=ALU.max, negate=True),
                 wait=[("S", mk_glog)])
        p.s(lambda e, G8=G8: e.activation(out=SEXP, in_=G8, func=AF.Exp,
                                          bias=M1N), wait=[("V", mv)])
        mv = p.v(lambda e: e.tensor_scalar_mul(out=M1P, in0=M1N, scalar1=-1.0),
                 wait=[("V", mv)])
        mv = p.v(lambda e, G8=G8: e.tensor_scalar(out=KEEP, in0=G8, scalar1=M1P,
                                                  scalar2=None, op0=ALU.is_lt),
                 wait=[("V", mv)])
        mv = p.v(lambda e: e.tensor_scalar(out=L2, in0=KEEP, scalar1=1e30,
                                           scalar2=-1e30, op0=ALU.mult, op1=ALU.add),
                 wait=[("V", mv)])
        mv = p.v(lambda e, G8=G8: e.tensor_tensor(out=L2, in0=L2, in1=G8, op=ALU.add),
                 wait=[("V", mv)])
        mv = p.v(lambda e: e.tensor_reduce(out=M2N, in_=L2, axis=AX.X, op=ALU.max,
                                           negate=True), wait=[("V", mv)])
        mv = p.v(lambda e: e.tensor_scalar_mul(out=M2P, in0=M2N, scalar1=-1.0),
                 wait=[("V", mv)])
        mv = p.v(lambda e, G8=G8: e.tensor_scalar(out=KEEP, in0=G8, scalar1=M2P,
                                                  scalar2=None, op0=ALU.is_ge),
                 wait=[("V", mv)])
        mks = p.mark("S")
        mv = p.v(lambda e: e.tensor_tensor(out=SEXP, in0=SEXP, in1=KEEP, op=ALU.mult),
                 wait=[("S", mks), ("V", mv)])
        mv = p.v(lambda e: e.tensor_reduce(out=ZS, in_=SEXP, axis=AX.X, op=ALU.add),
                 wait=[("V", mv)])
        mv = p.v(lambda e: e.reciprocal(out=RS, in_=ZS), wait=[("V", mv)])
        p.v(lambda e, G8=G8: e.tensor_scalar(out=G8, in0=SEXP, scalar1=RS,
                                             scalar2=0.5, op0=ALU.mult, op1=ALU.mult),
            wait=[("V", mv)])
    mk_gtok = p.mark("V")

    w13_mk = [0, 0]
    w2r_mk = 0
    mk_geb_read = 0
    mk_hb_free = 0
    for ex in range(EXPERTS):
        mk_ps_a = p.mark("S")
        mk_vv = p.mark("V")
        tr_ev = [mk_ps_a, mk_ps_a]
        for tc in range(16):
            po = 128 * (tc % 2)
            po = 512 * (tc % 2)
            p.t(lambda e, ex=ex, tc=tc, po=po: e.transpose(
                c.PSA[0:1, po:po + 128],
                in_=GTOK[:, 8 * tc + ex:8 * tc + ex + 1], identity=c.IDENT[:, :]),
                wait=[("V", max(mk_gtok, mk_vv)), ("S", tr_ev[tc % 2])], inc=True)
            mkt = p.mark("T")
            tr_ev[tc % 2] = p.s(lambda e, tc=tc, po=po: e.copy(
                out=GROW[:, 128 * tc:128 * (tc + 1)],
                in_=c.PSA[0:1, po:po + 128]), wait=[("T", mkt)])
        mk_grow = p.mark("S")
        bc_ev = [mk_grow, mk_grow]
        for hi in range(4):
            p.t(lambda e, hi=hi: e.matmul(
                c.PSA[:, 512 * (hi % 2):512 * (hi % 2) + 512], lhsT=c.ONES[0:1, :],
                rhs=GROW[:, 512 * hi:512 * hi + 512], start=True, stop=True),
                wait=[("S", max(mk_grow, bc_ev[hi % 2]))], inc=True)
            mkt = p.mark("T")
            bc_ev[hi % 2] = p.s(lambda e, hi=hi: e.copy(
                out=bfa(c, BFA_GEB + 512 * hi, 128, 512),
                in_=c.PSA[:, 512 * (hi % 2):512 * (hi % 2) + 512]),
                wait=[("T", mkt), ("V", mk_geb_read)])
        mk_geb = p.mark("S")

        for half in range(2):
            hcs = list(range(11 * half, 11 * half + 11))
            silu_mks = [0, 0]
            t3_mks = [0, 0]
            mtmp_v = [0, 0, 0, 0]
            pv0 = p.mark("V")
            sidx = 0
            for i, hc in enumerate(hcs):
                buf = hc % 2
                p.g(lambda e, ex=ex, hc=hc, buf=buf: e.dma_start(
                    out=bfa(c, BFA_W13 + 2048 * buf, 128, 2048).rearrange(
                        "p (m f) -> p m f", m=2),
                    in_=MSD[ex, hc, 0:2].rearrange("m p f -> p m f")),
                    wait=[("T", w13_mk[buf])], dma="DG3")
                mk_dma = p.mark("DG3")
                p.g(lambda e, ex=ex, hc=hc, i=i: e.dma_start(
                    out=bfa(c, BFA_W2R + 1024 * i, 128, 1024),
                    in_=MSD[ex, hc, 2]),
                    wait=[("T", w2r_mk)], dma="DG4")
                for sl4 in range(4):
                    sl = slice(512 * sl4, 512 * (sl4 + 1))
                    par = sidx % 2
                    psl = slice(512 * par, 512 * par + 512)
                    for kc in range(NC):
                        p.t(lambda e, buf=buf, kc=kc, sl=sl, psl=psl: e.matmul(
                            c.PSA[:, psl], lhsT=bfa(c, BFA_W13 + 2048 * buf + 128 * kc, 128, 128),
                            rhs=XBc(c, kc, sl), start=(kc == 0), stop=(kc == NC - 1)),
                            wait=([("DG3", mk_dma), ("S", silu_mks[par]), ("V", mkln)]
                                  if kc == 0 else []))
                    for kc in range(NC):
                        p.t(lambda e, buf=buf, kc=kc, sl=sl, psl=psl: e.matmul(
                            c.PSB[:, psl],
                            lhsT=bfa(c, BFA_W13 + 2048 * buf + 1024 + 128 * kc, 128, 128),
                            rhs=XBc(c, kc, sl), start=(kc == 0), stop=(kc == NC - 1)),
                            wait=[("V", max(t3_mks[par], pv0))] if kc == 0 else [],
                            inc=(kc == NC - 1))
                    mkt = p.mark("T")
                    ms = sidx % 4
                    silu_mks[par] = p.s(lambda e, psl=psl: e.activation(
                        out=bfa(c, BFA_MTMP, 128, 512), in_=c.PSA[:, psl],
                        func=AF.Silu), wait=[("T", mkt), ("V", mtmp_v[0])])
                    t3_mks[par] = p.v(lambda e, sl4=sl4, psl=psl: e.tensor_tensor(
                        out=bfa(c, BFA_MTMP + 512, 128, 512), in0=c.PSB[:, psl],
                        in1=bfa(c, BFA_GEB + 512 * sl4, 128, 512), op=ALU.mult),
                        wait=[("T", mkt), ("S", mk_geb)])
                    mtmp_v[0] = p.v(lambda e, i=i, sl4=sl4: e.tensor_tensor(
                        out=bfa(c, BFA_HB + 2048 * i + 512 * sl4, 128, 512),
                        in0=bfa(c, BFA_MTMP, 128, 512),
                        in1=bfa(c, BFA_MTMP + 512, 128, 512), op=ALU.mult),
                        wait=[("S", silu_mks[par]), ("T", mk_hb_free)])
                    sidx += 1
                w13_mk[buf] = p.mark("T")
            mk_w2dma = p.mark("DG4")
            mk_h = p.mark("V")

            oslot_mk = [0, 0]
            ti = 0
            for dc in range(NC):
                for qh in range(2):
                    ps_t = c.PSB if ti % 2 == 0 else c.PSC
                    for i, hc in enumerate(hcs):
                        for s2 in range(2):
                            p.t(lambda e, dc=dc, qh=qh, i=i, s2=s2, ps_t=ps_t: e.matmul(
                                ps_t[:, 512 * s2:512 * s2 + 512],
                                lhsT=bfa(c, BFA_W2R + 1024 * i + 128 * dc, 128, 128),
                                rhs=bfa(c, BFA_HB + 2048 * i + 1024 * qh + 512 * s2,
                                        128, 512),
                                start=(i == 0), stop=(i == 10)),
                                wait=([("DG4", mk_w2dma), ("V", max(mk_h, oslot_mk[ti % 2]))]
                                      if (i == 0 and s2 == 0) else []),
                                inc=(i == 10 and s2 == 1))
                    mkt = p.mark("T")
                    oslot_mk[ti % 2] = p.v(
                        lambda e, dc=dc, qh=qh, ps_t=ps_t: e.tensor_tensor(
                            out=c.X[dc][:, 1024 * qh:1024 * qh + 1024],
                            in0=c.X[dc][:, 1024 * qh:1024 * qh + 1024],
                            in1=ps_t[:, :], op=ALU.add),
                        wait=[("T", mkt)])
                    ti += 1
            w2r_mk = p.mark("T")
            mk_hb_free = w2r_mk
        mk_geb_read = max(t3_mks)
    c.mk_xb_free = p.mark("T")


ATT1_SCALE = 0.35355339059327373


def emit_attn1(c):
    from concourse.bass_core_rust import IndirectOffsetOnAxis
    p = c.p
    mkln = c.mk_ln_done
    CE = c.CEXTRA
    BQ_, BK_ = CE[:, 0:8], CE[:, 8:16]
    BQS, BKS = CE[:, 16:24], CE[:, 24:32]
    MX = CE[0:64, 32:48]
    Z3 = CE[0:64, 48:64]
    MXG = CE[0:64, 64:80]
    NEGM = CE[0:64, 96:112]
    Z3R = CE[0:64, 112:128]
    ZPART = CE[0:64, 116 + 12:116 + 14]
    ZPART = CE[0:64, 126:128]
    K1F = CE[:, 128:192]
    K1N = CE[:, 192:256]
    K1ROW = CE[:, 256:260]
    NM64 = CE[0:64, 260:262]
    SM64 = CE[0:64, 262:264]
    RC64 = CE[0:64, 264:266]
    CS64 = CE[0:64, 266:268]
    RSC = CE[0:64, 268:270]

    WQ_BUF = [49152, 50176]
    VSTG = [22528, 21504]
    CONVB = [55296, 55680]
    HALO_L, HALO_R = 49152, 50176

    p.y(lambda e: e.dma_start(out=BQ_, in_=c.din["BQ"]))
    p.y(lambda e: e.dma_start(out=BK_, in_=c.din["BK"]))
    p.y(lambda e: e.dma_start(out=c.fs_rows[0:1, 0:1024], in_=c.din["BV"]),
        wait=[("V", mkln)])
    p.y(lambda e: e.dma_start(out=c.HIDX[:, :], in_=c.din["HALOIDX"]))
    p.y(lambda e: e.dma_start(out=c.EYES_T[:, :], in_=c.din["EYES"]))
    mk_bload = p.mark("DY")
    p.s(lambda e: e.activation(out=BQS, in_=BQ_, func=AF.Copy, scale=ATT1_SCALE),
        wait=[("DY", mk_bload)])
    p.s(lambda e: e.activation(out=BKS, in_=BK_, func=AF.Copy, scale=ATT1_SCALE))
    mk_bscale = p.mark("S")
    for kc in range(NC):
        p.g(lambda e, kc=kc: e.dma_start(out=bfa(c, BFA_WVR + 1024 * kc, 128, 1024),
                                         in_=c.din["WV"][kc]), dma="DG5")
    mk_wv = p.mark("DG5")
    for hi in range(2):
        p.t(lambda e, hi=hi: e.matmul(
            c.PSA[:, 512 * hi:512 * hi + 512], lhsT=c.ONES[0:1, :],
            rhs=c.fs_rows[0:1, 512 * hi:512 * hi + 512], start=True, stop=True),
            wait=[("DY", mk_bload), ("S", p.mark("S"))], inc=(hi == 1))
    mk_bvb = p.mark("T")
    p.s(lambda e: e.copy(out=bfa(c, BFA_BVB, 128, 1024), in_=c.PSA[:, :]),
        wait=[("T", mk_bvb)])
    mk_bvbe = p.mark("S")

    vstg_mk = [0, 0]
    for tc in range(16):
        tsl = slice(128 * tc, 128 * (tc + 1))
        ps_t = c.PSB if tc % 2 == 0 else c.PSC
        for jh in range(2):
            for kc in range(NC):
                p.t(lambda e, kc=kc, tsl=tsl, jh=jh, ps_t=ps_t: e.matmul(
                    ps_t[:, 512 * jh:512 * jh + 512], lhsT=XBc(c, kc, tsl),
                    rhs=bfa(c, BFA_WVR + 1024 * kc + 512 * jh, 128, 512),
                    start=(kc == 0), stop=(kc == NC - 1)),
                    wait=([("V", max(mkln, vstg_mk[tc % 2])), ("DG5", mk_wv)]
                          if kc == 0 else []),
                    inc=(kc == NC - 1 and jh == 1))
        mkt = p.mark("T")
        vstg_mk[tc % 2] = p.v(lambda e, tc=tc, ps_t=ps_t: e.tensor_tensor(
            out=bfa(c, VSTG[tc % 2], 128, 1024), in0=ps_t[:, :],
            in1=bfa(c, BFA_BVB, 128, 1024), op=ALU.add),
            wait=[("T", mkt), ("S", mk_bvbe), ("DY", p.mark("DY"))])
        mkv = p.mark("V")
        p.y(lambda e, tc=tc: e.dma_start(out=c.dram["v_dram"].ap()[tc],
                                         in_=bfa(c, VSTG[tc % 2], 128, 1024)),
            wait=[("V", mkv)])
        if tc == 0:
            p.y(lambda e: e.dma_start(out=c.dram["vh_in"].ap()[0:128, :],
                                      in_=bfa(c, VSTG[0], 128, 1024)))
        if tc == 15:
            p.y(lambda e: e.dma_start(out=c.dram["vh_in"].ap()[128:256, :],
                                      in_=bfa(c, VSTG[1], 128, 1024)))
    mk_vdma = p.mark("DY")
    p.emit("g", lambda e: e.collective_compute(
        "AllGather", ALU.bypass, replica_groups=GROUPS_RG,
        ins=[c.dram["vh_in"].ap().opt()], outs=[c.dram["vh_out"].ap().opt()]),
        wait=[("DY", mk_vdma)], inc=[("CC4", 1)])

    wbuf_mk = [0, 0]
    for z, (WJ, BS, base) in enumerate(
            [("WQJ", BQS, BFA_QT1), ("WKJ", BKS, BFA_KT1)]):
        for jc in range(NC):
            buf = jc % 2
            p.g(lambda e, WJ=WJ, jc=jc, buf=buf: e.dma_start(
                out=bfa(c, WQ_BUF[buf], 128, 1024).rearrange("p (k j) -> p k j", k=8),
                in_=c.din[WJ][jc].rearrange("k p j -> p k j")),
                wait=[("T", wbuf_mk[buf])], dma="DG5")
            mk_dma = p.mark("DG5")
            for qs in range(QS):
                sl = slice(512 * qs, 512 * (qs + 1))
                ps_t = c.PSA if qs % 2 == 0 else c.PSB
                for kc in range(NC):
                    p.t(lambda e, buf=buf, kc=kc, sl=sl, ps_t=ps_t: e.matmul(
                        ps_t[:, 0:512], lhsT=bfa(c, WQ_BUF[buf] + 128 * kc, 128, 128),
                        rhs=XBc(c, kc, sl), start=(kc == 0), stop=(kc == NC - 1)),
                        wait=([("DG5", mk_dma), ("S", p.mark("S")), ("V", mkln)]
                              if kc == 0 else []),
                        inc=(kc == NC - 1))
                mkt = p.mark("T")
                if qs == QS - 1:
                    wbuf_mk[buf] = mkt
                p.s(lambda e, base=base, jc=jc, sl=sl, ps_t=ps_t, BS=BS: e.activation(
                    out=c.BFA[0:128, base + 2048 * jc + sl.start:base + 2048 * jc + sl.stop],
                    in_=ps_t[:, 0:512], func=AF.Identity, scale=ATT1_SCALE,
                    bias=BS[:, jc:jc + 1]),
                    wait=[("T", mkt), ("DY", mk_vdma if z == 0 else 0),
                          ("S", mk_bscale)])
    c.mk_xb_free = p.mark("T")
    mk_qk = p.mark("S")

    for tc in range(16):
        p.y(lambda e, tc=tc: e.dma_start(out=bfa(c, BFA_V1 + 1024 * tc, 128, 1024),
                                         in_=c.dram["v_dram"].ap()[tc]),
            wait=[("T", c.mk_xb_free), ("S", mk_qk), ("DY", mk_vdma)])
    p.v(lambda e: e.memset(bfa(c, HALO_L, 128, 1024), 0.0),
        wait=[("T", c.mk_xb_free)])
    p.v(lambda e: e.memset(bfa(c, HALO_R, 128, 1024), 0.0))
    mk_hz = p.mark("V")
    p.g(lambda e: e.indirect_dma_start(
        out=bfa(c, HALO_L, 128, 1024), out_offset=None,
        in_=c.dram["vh_out"].ap(),
        in_offset=IndirectOffsetOnAxis(c.HIDX[:, 0:1], 0),
        bounds_check=1023, oob_is_err=False),
        wait=[("CC4", 1), ("V", mk_hz), ("DY", mk_bload)], dma="DG5")
    p.g(lambda e: e.indirect_dma_start(
        out=bfa(c, HALO_R, 128, 1024), out_offset=None,
        in_=c.dram["vh_out"].ap(),
        in_offset=IndirectOffsetOnAxis(c.HIDX[:, 1:2], 0),
        bounds_check=1023, oob_is_err=False), dma="DG5")
    mk_v1 = p.mark("DY")
    mk_halo = p.mark("DG5")

    mv = 0
    for z, base in enumerate([BFA_QT1, BFA_KT1]):
        for dc in range(NC):
            mv = p.v(lambda e, z=z, dc=dc, base=base: e.tensor_reduce(
                out=c.QKL[:, 128 * z + 16 * dc:128 * z + 16 * dc + 16],
                in_=c.BFA[0:128, base + 2048 * dc:
                          base + 2048 * dc + 2048].rearrange(
                    "p (s j) -> p s j", s=16),
                axis=AX.X, op=ALU.add),
                wait=[("S", mk_qk)])
    p.s(lambda e: e.activation(out=c.QKL2[:, 0:256], in_=c.QKL[:, 0:256], func=AF.Copy,
                               scale=1.0 / 128.0), wait=[("V", mv)])
    mk_l = p.mark("S")
    p.y(lambda e: e.dma_start(
        out=c.dram["lm_in"].ap().rearrange("(p z c) l -> p z c l", p=128, z=2, c=8),
        in_=c.QKL2[:, 0:256].rearrange("p (z c l) -> p z c l", z=2, c=8)),
        wait=[("S", mk_l)])
    mk_lmin = p.mark("DY")
    p.emit("g", lambda e: e.collective_compute(
        "AllGather", ALU.bypass, replica_groups=GROUPS_RG,
        ins=[c.dram["lm_in"].ap().opt()], outs=[c.dram["lm_out"].ap().opt()]),
        wait=[("DY", mk_lmin)], inc=[("CC1", 1)])
    for r in range(4):
        p.y(lambda e, r=r: e.dma_start(
            out=c.QLF[:, :].rearrange("p (z c l) -> p z c l", z=2, c=8)[
                :, :, :, 16 * r:16 * (r + 1)],
            in_=c.dram["lm_out"].ap().rearrange(
                "(r p z c) l -> r p z c l", r=4, p=128, z=2, c=8)[r]),
            wait=[("CC1", 1)])
    mk_qlf = p.mark("DY")
    p.s(lambda e: e.copy(out=bfa(c, BFA_QLB, 128, 1024), in_=c.QLF[:, :]),
        wait=[("DY", mk_qlf)])
    mk_qlb = p.mark("S")

    pa_s = [0, 0]
    pa_v = [0, 0]
    for h in range(ATT1_H):
        dc, pb = h // 2, 64 * (h % 2)
        par = h % 2
        pcol = 512 * par
        PSLOT = c.PSA[0:64, pcol:pcol + 64]
        PROW = c.PSA[0:1, pcol:pcol + 64]
        PCOL1 = c.PSA[0:64, pcol:pcol + 1]

        def sct(i, par=par):
            off = 512 * par + 64 * i
            return c.SC64[0:64, off:off + 64]
        T_K2E, T_K2N, T_K2NT, T_V, T_KVS, T_KVT, T_A, T_B = \
            (sct(i) for i in range(8))
        nm = NM64[:, par:par + 1]
        sm = SM64[:, par:par + 1]
        rc = RC64[:, par:par + 1]
        cs = CS64[0:1, par:par + 1]
        rsc = RSC[:, par:par + 1]

        def mm_small(lhsT_ap, rhs_ap, wait=(), out=PSLOT):
            return p.t(lambda e, lhsT_ap=lhsT_ap, rhs_ap=rhs_ap, out=out: e.matmul(
                out, lhsT=lhsT_ap, rhs=rhs_ap, start=True, stop=True),
                wait=wait, inc=True)

        def tr_small(in_ap, wait=(), out=PSLOT):
            return p.t(lambda e, in_ap=in_ap, out=out: e.transpose(
                out, in_=in_ap, identity=c.IDENT[0:64, 0:64]),
                wait=wait, inc=True)

        def evac(dst, wait_t, scale=None, src=PSLOT):
            if scale is None:
                return p.s(lambda e, dst=dst, src=src: e.copy(out=dst, in_=src),
                           wait=[("T", wait_t)])
            return p.s(lambda e, dst=dst, scale=scale, src=src: e.activation(
                out=dst, in_=src, func=AF.Copy, scale=scale),
                wait=[("T", wait_t)])

        mkt = mm_small(c.QLF[pb:pb + 64, 64 * dc:64 * dc + 64],
                       c.QLF[pb:pb + 64, 512 + 64 * dc:512 + 64 * dc + 64],
                       wait=[("DY", mk_qlf), ("S", pa_s[par]), ("V", pa_v[par])])
        mv = p.v(lambda e, nm=nm, PSLOT=PSLOT: e.tensor_reduce(
            out=nm, in_=PSLOT, axis=AX.X, op=ALU.max, negate=True),
            wait=[("T", mkt)])
        mks = p.s(lambda e, nm=nm, PSLOT=PSLOT, T_K2E=T_K2E: e.activation(
            out=T_K2E, in_=PSLOT, func=AF.Exp, bias=nm), wait=[("V", mv)])
        mv = p.v(lambda e, sm=sm, T_K2E=T_K2E: e.tensor_reduce(
            out=sm, in_=T_K2E, axis=AX.X, op=ALU.add), wait=[("S", mks), ("V", mv)])
        mv = p.v(lambda e, rc=rc, sm=sm: e.reciprocal(out=rc, in_=sm), wait=[("V", mv)])
        mv = p.v(lambda e, rc=rc, T_K2E=T_K2E, T_K2N=T_K2N: e.tensor_scalar(
            out=T_K2N, in0=T_K2E, scalar1=rc, scalar2=None, op0=ALU.mult),
            wait=[("V", mv)])
        mkt = mm_small(c.ONES[0:64, 0:1], T_K2N, wait=[("V", mv), ("S", mks)],
                       out=PROW)
        mv2 = p.v(lambda e, cs=cs, PROW=PROW: e.tensor_reduce(
            out=cs, in_=PROW, axis=AX.X, op=ALU.max), wait=[("T", mkt)])
        mv2 = p.v(lambda e, cs=cs: e.reciprocal(out=cs, in_=cs), wait=[("V", mv2)])
        mkt = p.t(lambda e, cs=cs, PCOL1=PCOL1: e.matmul(
            PCOL1, lhsT=c.ONES[0:1, 0:64], rhs=cs, start=True, stop=True),
            wait=[("V", mv2)], inc=True)
        mks = p.s(lambda e, rsc=rsc, PCOL1=PCOL1: e.copy(out=rsc, in_=PCOL1),
                  wait=[("T", mkt)])
        mkt = tr_small(T_K2N, wait=[("S", mks)])
        mks = evac(T_K2NT, mkt)
        mv = p.v(lambda e, T_V=T_V, T_K2NT=T_K2NT, rsc=rsc: e.tensor_scalar(
            out=T_V, in0=T_K2NT, scalar1=rsc, scalar2=None, op0=ALU.mult),
            wait=[("S", mks), ("V", mv)])
        for it in range(6):
            mkt = mm_small(T_K2NT, T_V, wait=[("V", mv), ("S", mks)])
            mks = evac(T_KVS, mkt)
            mkt = tr_small(T_KVS, wait=[("S", mks)])
            mks = evac(T_KVT, mkt)
            mv = p.v(lambda e, T_A=T_A, T_KVS=T_KVS: e.tensor_tensor(
                out=T_A, in0=c.EYES_T[0:64, 0:64], in1=T_KVS, op=ALU.subtract),
                wait=[("S", mks), ("V", mv)])
            mkt = mm_small(T_KVT, T_A, wait=[("V", mv)])
            mks = evac(T_B, mkt)
            mv = p.v(lambda e, T_A=T_A, T_B=T_B: e.tensor_tensor(
                out=T_A, in0=c.EYES_T[0:64, 64:128], in1=T_B, op=ALU.subtract),
                wait=[("S", mks), ("V", mv)])
            mkt = mm_small(T_KVT, T_A, wait=[("V", mv)])
            mks = evac(T_B, mkt)
            mv = p.v(lambda e, T_A=T_A, T_B=T_B: e.tensor_tensor(
                out=T_A, in0=c.EYES_T[0:64, 128:192], in1=T_B, op=ALU.subtract),
                wait=[("S", mks), ("V", mv)])
            mkt = tr_small(T_V, wait=[("V", mv)])
            mks = evac(T_KVS, mkt)
            mkt = mm_small(T_KVS, T_A, wait=[("S", mks)])
            dst = c.INV2[0:64, 64 * h:64 * h + 64] if it == 5 else T_V
            mks = evac(dst, mkt, scale=0.25)
            if it < 5:
                mv = p.mark("V")
        pa_s[par] = mks
        pa_v[par] = p.mark("V")
    mk_inv = p.mark("S")

    l3_done_s = 0
    for h in range(ATT1_H):
        dc, pb = h // 2, 64 * (h % 2)
        for qs in range(QS):
            ps_t = c.PSB if qs < 2 else c.PSC
            p.t(lambda e, h=h, dc=dc, pb=pb, qs=qs, ps_t=ps_t: e.matmul(
                ps_t[0:64, 512 * (qs % 2):512 * (qs % 2) + 512],
                lhsT=c.BFA[pb:pb + 64, BFA_QLB + 64 * dc:BFA_QLB + 64 * dc + 64],
                rhs=c.BFA[pb:pb + 64, BFA_KT1 + 2048 * dc + 512 * qs:
                          BFA_KT1 + 2048 * dc + 512 * qs + 512],
                start=True, stop=True),
                wait=([("S", max(mk_qlb, l3_done_s)), ("V", p.mark("V"))]
                      if qs == 0 else []),
                inc=(qs == 3))
        mkt = p.mark("T")
        mv = p.v(lambda e: e.tensor_reduce(out=ZPART[:, 0:1], in_=c.PSB[0:64, :],
                                           axis=AX.X, op=ALU.max), wait=[("T", mkt)])
        mv = p.v(lambda e: e.tensor_reduce(out=ZPART[:, 1:2], in_=c.PSC[0:64, :],
                                           axis=AX.X, op=ALU.max), wait=[("V", mv)])
        mv = p.v(lambda e, h=h: e.tensor_tensor(out=MX[:, h:h + 1], in0=ZPART[:, 0:1],
                                                in1=ZPART[:, 1:2], op=ALU.max),
                 wait=[("V", mv)])
        l3_done_s = p.s(lambda e: e.copy(out=ZPART[:, 0:1], in_=ZPART[:, 0:1]),
                        wait=[("V", mv)])
    mk_mx_v = p.mark("V")
    p.y(lambda e: e.dma_start(out=c.dram["mx_in"].ap(), in_=MX), wait=[("S", l3_done_s)])
    p.emit("g", lambda e: e.collective_compute(
        "AllReduce", ALU.max, replica_groups=GROUPS_RG,
        ins=[c.dram["mx_in"].ap().opt()], outs=[c.dram["mx_out"].ap().opt()]),
        wait=[("DY", p.mark("DY"))], inc=[("CC2", 1)])
    p.y(lambda e: e.dma_start(out=MXG, in_=c.dram["mx_out"].ap()), wait=[("CC2", 1)])
    mk_mxg = p.mark("DY")
    mv = p.v(lambda e: e.tensor_scalar_mul(out=NEGM, in0=MXG, scalar1=-1.0),
             wait=[("DY", mk_mxg)])
    mk_negm_v = mv

    ub_s = 0
    for h in range(ATT1_H):
        dc, pb = h // 2, 64 * (h % 2)
        for qs in range(QS):
            ps_t = c.PSB if qs < 2 else c.PSC
            p.t(lambda e, dc=dc, pb=pb, qs=qs, ps_t=ps_t: e.matmul(
                ps_t[0:64, 512 * (qs % 2):512 * (qs % 2) + 512],
                lhsT=c.BFA[pb:pb + 64, BFA_QLB + 64 * dc:BFA_QLB + 64 * dc + 64],
                rhs=c.BFA[pb:pb + 64, BFA_KT1 + 2048 * dc + 512 * qs:
                          BFA_KT1 + 2048 * dc + 512 * qs + 512],
                start=True, stop=True),
                wait=([("S", p.mark("S")), ("V", mk_negm_v)] if qs == 0 else []),
                inc=(qs == 3))
        mkt = p.mark("T")
        mks = p.s(lambda e, h=h: e.activation(
            out=bfa(c, BFA_E3, 64, 1024), in_=c.PSB[0:64, :], func=AF.Exp,
            bias=NEGM[:, h:h + 1], accum_out=ZPART[:, 0:1]), wait=[("T", mkt)])
        mks = p.s(lambda e, h=h: e.activation(
            out=bfa(c, BFA_E3 + 1024, 64, 1024), in_=c.PSC[0:64, :], func=AF.Exp,
            bias=NEGM[:, h:h + 1], accum_out=ZPART[:, 1:2]))
        mv = p.v(lambda e, h=h: e.tensor_tensor(out=Z3[:, h:h + 1], in0=ZPART[:, 0:1],
                                                in1=ZPART[:, 1:2], op=ALU.add),
                 wait=[("S", mks)])
        tr_s = [mks, mks]
        for grp in range(2):
            po = 1024 * grp
            mkt = 0
            for j in range(8):
                s16 = 8 * grp + j
                mkt = p.t(lambda e, s16=s16, po=po, j=j: e.transpose(
                    c.PSE[:, po + 64 * j:po + 64 * j + 64],
                    in_=c.BFA[0:64, BFA_E3 + 128 * s16:BFA_E3 + 128 * s16 + 128],
                    identity=c.IDENTB[0:64, 0:64]),
                    wait=[("S", tr_s[grp])] if j == 0 else [], inc=(j == 7))
            tr_s[grp] = p.s(lambda e, grp=grp, po=po: e.copy(
                out=bfa(c, BFA_E3T + 512 * grp, 128, 512),
                in_=c.PSE[:, po:po + 512]),
                wait=[("T", mkt)])
        mk_e3t = p.mark("S")
        for s16 in range(16):
            p.t(lambda e, s16=s16, h=h: e.matmul(
                c.PSA[0:64, 0:64], lhsT=bfa(c, BFA_E3T + 64 * s16, 128, 64),
                rhs=bfa(c, BFA_V1 + 1024 * s16 + 64 * h, 128, 64),
                start=(s16 == 0), stop=(s16 == 15)),
                wait=([("S", max(mk_e3t, ub_s)), ("DY", mk_v1)] if s16 == 0 else []),
                inc=(s16 == 15))
        mkt = p.mark("T")
        ub_s = p.s(lambda e, h=h: e.copy(out=c.UBUF[0:64, 64 * h:64 * h + 64],
                                         in_=c.PSA[0:64, 0:64]), wait=[("T", mkt)])
    mk_uz_v = p.mark("V")
    p.y(lambda e: e.dma_start(out=c.dram["uz_in"].ap()[:, 0:1024], in_=c.UBUF[:, :]),
        wait=[("S", ub_s), ("V", mk_uz_v)])
    p.y(lambda e: e.dma_start(out=c.dram["uz_in"].ap()[:, 1024:1040], in_=Z3))
    mk_uzin = p.mark("DY")
    p.emit("g", lambda e: e.collective_compute(
        "AllReduce", ALU.add, replica_groups=GROUPS_RG,
        ins=[c.dram["uz_in"].ap().opt()], outs=[c.dram["uz_out"].ap().opt()]),
        wait=[("DY", mk_uzin)], inc=[("CC3", 1)])
    p.y(lambda e: e.dma_start(out=c.UBUF[:, :], in_=c.dram["uz_out"].ap()[:, 0:1024]),
        wait=[("CC3", 1)])
    p.y(lambda e: e.dma_start(out=Z3, in_=c.dram["uz_out"].ap()[:, 1024:1040]))
    mk_uzout = p.mark("DY")
    mv = p.v(lambda e: e.reciprocal(out=Z3R, in_=Z3), wait=[("DY", mk_uzout)])
    for h in range(ATT1_H):
        mv = p.v(lambda e, h=h: e.tensor_scalar(
            out=c.UBUF[0:64, 64 * h:64 * h + 64],
            in0=c.UBUF[0:64, 64 * h:64 * h + 64],
            scalar1=Z3R[:, h:h + 1], scalar2=None, op0=ALU.mult), wait=[("V", mv)])
    mk_k3v = mv

    convb_mk = [0, 0]
    pc_s = p.mark("S")
    pc_v = p.mark("V")
    for h in range(ATT1_H):
        dc, pb = h // 2, 64 * (h % 2)
        cbuf = h % 2
        p.g(lambda e, h=h, cbuf=cbuf: e.dma_start(
            out=bfa(c, CONVB[cbuf], 128, 384).rearrange("p (b j) -> p b j", b=3),
            in_=c.din["CONVT"][h].rearrange("b p j -> p b j")),
            wait=[("T", convb_mk[cbuf])], dma="DG5")
        mk_cb = p.mark("DG5")
        NMX16 = CE[:, 256:272]
        RCP16 = CE[:, 288:304]
        K1FB = 51200
        K1NB = 52224
        for tc in range(16):
            p.t(lambda e, dc=dc, pb=pb, tc=tc: e.matmul(
                c.PSA[:, 64 * tc:64 * tc + 64],
                lhsT=c.BFA[pb:pb + 64, BFA_QT1 + 2048 * dc + 128 * tc:
                           BFA_QT1 + 2048 * dc + 128 * tc + 128],
                rhs=c.BFA[pb:pb + 64, BFA_QLB + 512 + 64 * dc:BFA_QLB + 512 + 64 * dc + 64],
                start=True, stop=True),
                wait=([("S", pc_s), ("V", pc_v)] if tc == 0 else []),
                inc=(tc == 15))
        mk_k1mm = p.mark("T")
        mk_exp1 = p.s(lambda e: e.activation(
            out=bfa(c, K1FB, 128, 1024), in_=c.PSA[:, :], func=AF.Exp),
            wait=[("T", mk_k1mm)])
        sum_v = p.v(lambda e: e.tensor_reduce(
            out=RCP16, in_=bfa(c, K1FB, 128, 1024).rearrange(
                "p (s j) -> p s j", s=16),
            axis=AX.X, op=ALU.add),
            wait=[("S", mk_exp1)])
        sum_v = p.v(lambda e: e.reciprocal(out=RCP16, in_=RCP16), wait=[("V", sum_v)])
        mult_vs = []
        for tc in range(16):
            sum_v = p.v(lambda e, tc=tc: e.tensor_scalar(
                out=bfa(c, K1NB + 64 * tc, 128, 64),
                in0=bfa(c, K1FB + 64 * tc, 128, 64),
                scalar1=RCP16[:, tc:tc + 1], scalar2=None, op0=ALU.mult),
                wait=[("V", sum_v)])
            mult_vs.append(sum_v)
        k1t_s = [pc_s, pc_s]
        for grp in range(2):
            pt = 1024 * grp
            mkt = 0
            for j in range(8):
                tc = 8 * grp + j
                mkt = p.t(lambda e, tc=tc, pt=pt, j=j: e.transpose(
                    c.PSE[0:64, pt + 128 * j:pt + 128 * j + 128],
                    in_=bfa(c, K1NB + 64 * tc, 128, 64),
                    identity=c.IDENTB[:, :]),
                    wait=[("V", mult_vs[tc])] + ([("S", k1t_s[grp])] if j == 0 else []),
                    inc=(j == 7))
            k1t_s[grp] = p.s(lambda e, grp=grp, pt=pt: e.copy(
                out=c.K1T[0:64, 1024 * grp:1024 * grp + 1024],
                in_=c.PSE[0:64, pt:pt + 1024]),
                wait=[("T", mkt)])
        mk_k1t = p.mark("S")
        for qs in range(QS):
            ps_t = c.PSB if qs < 2 else c.PSC
            p.t(lambda e, h=h, qs=qs, ps_t=ps_t: e.matmul(
                ps_t[0:64, 512 * (qs % 2):512 * (qs % 2) + 512],
                lhsT=c.INV2[0:64, 64 * h:64 * h + 64],
                rhs=c.K1T[0:64, 512 * qs:512 * qs + 512], start=True, stop=True),
                wait=([("S", max(mk_k1t, mk_inv)), ("V", pc_v)] if qs == 0 else []),
                inc=(qs == 3))
        mkt = p.mark("T")
        p.s(lambda e: e.copy(out=c.ATTNT[0:64, 0:1024], in_=c.PSB[0:64, :]),
            wait=[("T", mkt)])
        mks = p.s(lambda e: e.copy(out=c.ATTNT[0:64, 1024:2048], in_=c.PSC[0:64, :]))
        mk_at = p.mark("S")
        for qs in range(QS):
            ps_t = c.PSB if qs < 2 else c.PSC
            p.t(lambda e, h=h, pb=pb, qs=qs, ps_t=ps_t: e.matmul(
                ps_t[pb:pb + 64, 512 * (qs % 2):512 * (qs % 2) + 512],
                lhsT=c.UBUF[0:64, 64 * h:64 * h + 64],
                rhs=c.ATTNT[0:64, 512 * qs:512 * qs + 512], start=True, stop=False),
                wait=([("S", mk_at), ("V", max(mk_k3v, pc_v))] if qs == 0 else []))
        for tc in range(16):
            ps_t = c.PSB if tc < 8 else c.PSC
            pco = 512 * ((tc // 4) % 2) + 128 * (tc % 4)
            for b, delta in enumerate((-1, 0, 1)):
                src = tc + delta
                if src < 0:
                    lhs = bfa(c, HALO_L + 64 * h, 128, 64)
                elif src > 15:
                    lhs = bfa(c, HALO_R + 64 * h, 128, 64)
                else:
                    lhs = bfa(c, BFA_V1 + 1024 * src + 64 * h, 128, 64)
                p.t(lambda e, lhs=lhs, b=b, cbuf=cbuf, pb=pb, ps_t=ps_t, pco=pco: e.matmul(
                    ps_t[pb:pb + 64, pco:pco + 128], lhsT=lhs,
                    rhs=bfa(c, CONVB[cbuf] + 128 * b, 128, 128),
                    start=False, stop=(b == 2)),
                    wait=([("DG5", max(mk_cb, mk_halo)), ("DY", mk_v1)]
                          if (tc == 0 and b == 0) else []),
                    inc=(tc == 15 and b == 2))
        mkt = p.mark("T")
        if h == ATT1_H - 1:
            convb_mk[cbuf] = mkt
        else:
            convb_mk[cbuf] = mkt
        for qs in range(QS):
            ps_t = c.PSB if qs < 2 else c.PSC
            mv = p.v(lambda e, dc=dc, pb=pb, qs=qs, ps_t=ps_t: e.tensor_tensor(
                out=c.X[dc][pb:pb + 64, 512 * qs:512 * qs + 512],
                in0=c.X[dc][pb:pb + 64, 512 * qs:512 * qs + 512],
                in1=ps_t[pb:pb + 64, 512 * (qs % 2):512 * (qs % 2) + 512],
                op=ALU.add), wait=[("T", mkt)])
        pc_v = p.mark("V")
        pc_s = p.mark("S")
    c.mk_xb_free = max(c.mk_xb_free, p.mark("T"))


def host_prep(inputs, sublayers=(0, 1, 2, 3, 4, 5)):
    rep = {}
    consts = np.zeros((128, 384), dtype=np.float32)
    consts[:, 0:128] = 1.0
    consts[:, 128:256] = np.eye(128, dtype=np.float32)
    consts[:, 256:264] = -1e30
    consts[:, 264] = 1e-5
    consts[:, 265] = -ATT0_SHIFT
    rep["CONSTS"] = consts
    need_ffn0 = 0 in sublayers or 2 in sublayers
    need_attn0 = 1 in sublayers
    need_moe = 3 in sublayers or 5 in sublayers
    need_attn1 = 4 in sublayers
    if need_ffn0:
        w = np.asarray(inputs["L0_ffn_weight"], dtype=np.float32)
        rep["W0T"] = np.ascontiguousarray(w.T).reshape(NC, 128, D)
        rep["DISHA"] = np.asarray(inputs["L0_ffn_disha"], dtype=np.float32)
        ssel = np.zeros((128, 64), dtype=np.float32)
        for pp in range(128):
            ssel[pp, pp % 64] = 1.0
        rep["SSEL"] = ssel
    if need_attn0:
        pq = np.asarray(inputs["L0_attn_proj_q"], dtype=np.float32)
        rep["PQ"] = np.ascontiguousarray(pq.reshape(8, NC, 128, 128))
        rep["PK"] = np.ascontiguousarray(
            np.asarray(inputs["L0_attn_proj_k"], dtype=np.float32).reshape(NC, 128, 128))
        rep["PV"] = np.ascontiguousarray(
            np.asarray(inputs["L0_attn_proj_v"], dtype=np.float32).reshape(NC, 128, 128))
        po = np.asarray(inputs["L0_attn_proj_o"], dtype=np.float32)
        rep["POT"] = np.ascontiguousarray(po.transpose(0, 2, 1))
    if need_moe:
        w1 = np.asarray(inputs["L1_ffn_w1"], dtype=np.float32)
        w3 = np.asarray(inputs["L1_ffn_w3"], dtype=np.float32)
        w2 = np.asarray(inputs["L1_ffn_w2"], dtype=np.float32)
        p1 = w1.reshape(EXPERTS, HC, 128, NC, 128).transpose(0, 1, 4, 3, 2).reshape(
            EXPERTS, HC, 128, D)
        p3 = w3.reshape(EXPERTS, HC, 128, NC, 128).transpose(0, 1, 4, 3, 2).reshape(
            EXPERTS, HC, 128, D)
        p2 = w2.reshape(EXPERTS, NC, 128, HC, 128).transpose(0, 3, 4, 1, 2).reshape(
            EXPERTS, HC, 128, D)
        rep["MS"] = np.ascontiguousarray(np.stack([p1, p3, p2], axis=2))
        rep["GWT"] = np.ascontiguousarray(
            np.asarray(inputs["L1_ffn_gate_w"], dtype=np.float32).T.reshape(NC, 128, 8))
    if need_attn1:
        wq = np.asarray(inputs["L1_attn_Wq"], dtype=np.float32)
        wk = np.asarray(inputs["L1_attn_Wk"], dtype=np.float32)
        wv = np.asarray(inputs["L1_attn_Wv"], dtype=np.float32)
        rep["WQJ"] = np.ascontiguousarray(wq.reshape(NC, 128, NC, 128).transpose(2, 0, 1, 3))
        rep["WKJ"] = np.ascontiguousarray(wk.reshape(NC, 128, NC, 128).transpose(2, 0, 1, 3))
        rep["WV"] = np.ascontiguousarray(wv.reshape(NC, 128, D))
        rep["BQ"] = np.ascontiguousarray(
            np.asarray(inputs["L1_attn_bq"], dtype=np.float32).reshape(8, 128).T)
        rep["BK"] = np.ascontiguousarray(
            np.asarray(inputs["L1_attn_bk"], dtype=np.float32).reshape(8, 128).T)
        rep["BV"] = np.asarray(inputs["L1_attn_bv"], dtype=np.float32).reshape(1, D)
        cw = np.asarray(inputs["L1_attn_conv_w"], dtype=np.float32).reshape(ATT1_H, 65)
        convt = np.zeros((ATT1_H, 3, 128, 128), dtype=np.float32)
        jj = np.arange(128)
        for h in range(ATT1_H):
            for b, delta in enumerate((-1, 0, 1)):
                for pp in range(128):
                    t = delta * 128 + pp - jj + 32
                    m = (t >= 0) & (t < 65)
                    convt[h, b, pp, m] = cw[h, t[m]]
        rep["CONVT"] = convt
        eyes = np.zeros((64, 192), dtype=np.float32)
        eyes[:, 0:64] = 7.0 * np.eye(64)
        eyes[:, 64:128] = 15.0 * np.eye(64)
        eyes[:, 128:192] = 13.0 * np.eye(64)
        rep["EYES"] = eyes

    x = np.asarray(inputs["L0_attn_x"], dtype=np.float32)
    per_core = []
    for core in range(8):
        g, r = core // 4, core % 4
        d = dict(rep)
        d["xin"] = np.ascontiguousarray(x[g, Q * r:Q * (r + 1), :].T)
        if need_attn1:
            hi = np.full((128, 2), 99999, dtype=np.int32)
            if r > 0:
                hi[:, 0] = 256 * (r - 1) + 128 + np.arange(128)
            if r < 3:
                hi[:, 1] = 256 * (r + 1) + np.arange(128)
            d["HALOIDX"] = hi
        per_core.append(d)
    return per_core


def run_launch(sublayers, per_core_inputs):
    nc = build_program(tuple(sublayers))
    res = run_bass_kernel_spmd(nc, per_core_inputs, core_ids=list(range(8)))
    return [res.results[cc]["xout"] for cc in range(8)]


LAUNCH_GROUPS = [(0, 1, 2, 3, 4, 5)]


def kernel(**inputs):
    x = np.asarray(inputs["L0_attn_x"], dtype=np.float32)
    cur = [None] * 8
    for gi, grp in enumerate(LAUNCH_GROUPS):
        per_core = host_prep(inputs, grp)
        if gi > 0:
            for core in range(8):
                per_core[core]["xin"] = cur[core]
        outs = run_launch(grp, per_core)
        cur = [np.ascontiguousarray(o) for o in outs]
    y = np.empty((2, S_FULL, D), dtype=np.float32)
    for core in range(8):
        g, r = core // 4, core % 4
        y[g, Q * r:Q * (r + 1), :] = cur[core].T
    return y


# version 14
# speedup vs baseline: 1.0475x; 1.0475x over previous
import numpy as np
import ml_dtypes
from contextlib import ExitStack
import concourse.bass as bass
import concourse.mybir as mybir
from concourse.bass_utils import run_bass_kernel_spmd

F32 = mybir.dt.float32
BF16 = mybir.dt.bfloat16
AF = mybir.ActivationFunctionType
ALU = mybir.AluOpType
AX = mybir.AxisListType

Q = 2048
D = 1024
NC = 8
QS = 4
EXPERTS = 8
HID = 2816
HC = 22
ATT0_SHIFT = 40.0
ATT1_H = 16
S_FULL = 8192
GROUPS_RG = [[0, 1, 2, 3], [4, 5, 6, 7]]

ENGS = ("t", "v", "s", "g", "y")
DEBUG_TAPS = []
DEBUG_PQDUMP = False
SERIAL_DMA = True


class Plan:

    def __init__(self):
        self.ops = {e: [] for e in ENGS}
        self.cnt = {}
        self.waited = {e: {} for e in ENGS}

    def mark(self, sem):
        return self.cnt.get(sem, 0)

    def emit(self, eng, fn=None, wait=(), inc=()):
        w2 = []
        for s, v in wait:
            if v <= 0:
                continue
            assert v <= self.cnt.get(s, 0), \
                f"wait {s}>={v} exceeds planned {self.cnt.get(s, 0)}"
            if v > self.waited[eng].get(s, 0):
                self.waited[eng][s] = v
                w2.append((s, v))
        incs = tuple(inc)
        assert len(incs) <= 1
        for s, a in incs:
            self.cnt[s] = self.cnt.get(s, 0) + a
        self.ops[eng].append((fn, tuple(w2), incs))
        return self.cnt.get(incs[0][0], None) if incs else None

    def t(self, fn, wait=(), inc=False):
        return self.emit("t", fn, wait, [("T", 1)] if inc else [])

    def v(self, fn, wait=(), inc=True):
        return self.emit("v", fn, wait, [("V", 1)] if inc else [])

    def s(self, fn, wait=(), inc=True):
        return self.emit("s", fn, wait, [("S", 1)] if inc else [])

    def g(self, fn, wait=(), inc=False, dma=False):
        if dma:
            sem = "DG" if dma is True else dma
            w = list(wait) + ([(sem, self.cnt.get(sem, 0))] if SERIAL_DMA else [])
            return self.emit("g", fn, w, [(sem, 16)])
        return self.emit("g", fn, wait, [("G", 1)] if inc else [])

    def y(self, fn, wait=(), dma=True):
        if dma:
            w = list(wait) + ([("DY", self.cnt.get("DY", 0))] if SERIAL_DMA else [])
            return self.emit("y", fn, w, [("DY", 16)])
        return self.emit("y", fn, wait, [])


class Ctx:
    pass


MAP_BASE = 18944
OFF_X = MAP_BASE
OFF_FSCR = MAP_BASE + 65536
FS_SQ = OFF_FSCR
FS_TMP = OFF_FSCR + 4096
FS_RB = OFF_FSCR + 8192
FS_MRB = OFF_FSCR + 12288
FS_ACC = FS_SQ
FS_ROWS = OFF_FSCR + 16384
OFF_BFA = OFF_FSCR + 24576
OFF_CONST = OFF_BFA + 114688
TOTAL_LIMIT = 229376

BFA_XB = 0
BFA_PL = 16384
BFA_DISHA = 18432
BFA_SSEL = 19456
BFA_QT0 = 16384
BFA_KTF = 32768
BFA_VF = 40960
BFA_OT = 0
BFA_E0 = 8192
BFA_WS2 = 10240
BFA_KVSTAGE = 32768
BFA_PKPV = 36864
BFA_W0T = 49152
BFA_HB = 16384
BFA_W13 = 38912
BFA_W2R = 43008
BFA_GEB = 54272
BFA_MTMP = 56320
BFA_WVR = 24576
BFA_VSTG = 22528
BFA_QT1 = 16384
BFA_KT1 = 32768
BFA_V1 = 0
BFA_HALO = 49152
BFA_E3 = 51200
BFA_E3T = 53248
BFA_QLB = 54272
BFA_CONV = 54784
BFA_WS5 = 55552
BFA_BVB = 23552


def bfa(c, elem_off, parts, width):
    return c.BFA[0:parts, elem_off:elem_off + width]


def XBc(c, dc, sl=None):
    if sl is None:
        return bfa(c, BFA_XB + 2048 * dc, 128, Q)
    return c.BFA[0:128, BFA_XB + 2048 * dc + sl.start:BFA_XB + 2048 * dc + sl.stop]


def _sem_names():
    return ["T", "V", "S", "G", "DY", "DG", "DG2", "DG3", "DG4", "DG5"] + [f"CC{i}" for i in range(8)]


def build_program(sublayers=(0, 1, 2, 3, 4, 5)):
    nc = bass.Bass("TRN2", target_bir_lowering=False, debug=False)
    p = Plan()
    c = Ctx()
    c.nc = nc
    c.p = p
    c.mk_xb_free = 0

    need_ffn0 = 0 in sublayers or 2 in sublayers
    need_attn0 = 1 in sublayers
    need_moe = 3 in sublayers or 5 in sublayers
    need_attn1 = 4 in sublayers

    din = {}

    def DIN(name, shape):
        din[name] = nc.dram_tensor(name, list(shape), F32, kind="ExternalInput").ap()
        return din[name]

    DIN("xin", (D, Q))
    c.xout_ap = nc.dram_tensor("xout", [D, Q], F32, kind="ExternalOutput").ap()
    DIN("CONSTS", (128, 384))
    if need_ffn0:
        DIN("W0T", (NC, 128, D))
        DIN("DISHA", (64, D))
        DIN("SSEL", (128, 64))
    if need_attn0:
        if DEBUG_PQDUMP:
            t = nc.dram_tensor("dbg_pq", [8, 128, 1024], BF16, kind="ExternalOutput")
            c.dbg_pq = t.ap()
        DIN("PQ", (8, NC, 128, 128))
        DIN("PK", (NC, 128, 128))
        DIN("PV", (NC, 128, 128))
        DIN("POT", (8, 128, D))
    if need_moe:
        DIN("MS", (EXPERTS, HC, 3, 128, D))
        DIN("GWT", (NC, 128, 8))
    if need_attn1:
        DIN("WQJ", (NC, NC, 128, 128))
        DIN("WKJ", (NC, NC, 128, 128))
        DIN("WV", (NC, 128, D))
        DIN("BQ", (128, 8))
        DIN("BK", (128, 8))
        DIN("BV", (1, D))
        DIN("CONVT", (ATT1_H, 3, 128, 128))
        DIN("EYES", (64, 3072))
        din["HALOIDX"] = nc.dram_tensor("HALOIDX", [128, 2], mybir.dt.int32,
                                        kind="ExternalInput").ap()
    c.din = din

    c.dram = {}
    if need_attn0:
        c.dram["pqb"] = nc.dram_tensor("pqb", [8, 128, 1024], BF16, kind="Internal")
        c.dram["kv_in"] = nc.dram_tensor("kv_in", [128, 4096], BF16, kind="Internal")
        c.dram["kv_out"] = nc.dram_tensor("kv_out", [512, 4096], BF16, kind="Internal")
    if need_attn1:
        c.dram["v_dram"] = nc.dram_tensor("v_dram", [16, 128, D], BF16, kind="Internal")
        c.dram["lm_in"] = nc.dram_tensor("lm_in", [2048, 16], F32, kind="Internal")
        c.dram["lm_out"] = nc.dram_tensor("lm_out", [8192, 16], F32, kind="Internal")
        c.dram["mx_in"] = nc.dram_tensor("mx_in", [64, 16], F32, kind="Internal")
        c.dram["mx_out"] = nc.dram_tensor("mx_out", [64, 16], F32, kind="Internal")
        c.dram["uz_in"] = nc.dram_tensor("uz_in", [64, 1040], F32, kind="Internal")
        c.dram["uz_out"] = nc.dram_tensor("uz_out", [64, 1040], F32, kind="Internal")
        c.dram["vh_in"] = nc.dram_tensor("vh_in", [256, D], BF16, kind="Internal")
        c.dram["vh_out"] = nc.dram_tensor("vh_out", [1024, D], BF16, kind="Internal")

    with ExitStack() as stack:
        def sb_at(name, shape, dt, off):
            return nc.alloc_sbuf_tensor_at(name, list(shape), dt, offset=off)

        c.X = [sb_at(f"X{dc}", [128, Q], F32, OFF_X + 8192 * dc) for dc in range(NC)]
        c.fs_sq = sb_at("fs_sq", [128, 1024], F32, FS_SQ)
        c.fs_tmp = sb_at("fs_tmp", [128, 1024], F32, FS_TMP)
        c.fs_rb = sb_at("fs_rb", [128, 1024], F32, FS_RB)
        c.fs_mrb = sb_at("fs_mrb", [128, 1024], F32, FS_MRB)
        c.fs_acc = sb_at("fs_acc", [128, 1024], F32, FS_ACC)
        c.fs_rows = sb_at("fs_rows", [1, 2048], F32, FS_ROWS)
        c.BFA = sb_at("BFA", [128, 57344], BF16, OFF_BFA)
        c.ONES = sb_at("ONES", [128, 128], F32, OFF_CONST)
        c.IDENT = sb_at("IDENT", [128, 128], F32, OFF_CONST + 512)
        c.IDENTB = sb_at("IDENTB", [128, 128], BF16, OFF_CONST + 1024)
        c.NEG8 = sb_at("NEG8", [128, 8], F32, OFF_CONST + 1280)
        c.CBIAS = sb_at("CBIAS", [128, 8], F32, OFF_CONST + 1312)
        c.CEXTRA = sb_at("CEXTRA", [128, 624], F32, OFF_CONST + 1344)
        assert OFF_CONST + 1344 + 2496 <= TOTAL_LIMIT

        c.QLF = sb_at("QLF", [128, 1024], F32, FS_SQ)
        c.EYES_T = sb_at("EYES_T", [64, 192], F32, OFF_CONST + 1344 + 2496 + 32)
        c.SC64 = sb_at("SC64", [64, 1024], F32, FS_TMP)
        c.K1T = sb_at("K1T", [64, 2048], F32, FS_RB)
        c.QKL = sb_at("QKL", [128, 512], F32, FS_RB)
        c.QKL2 = sb_at("QKL2", [128, 512], F32, FS_RB + 2048)
        c.INV2 = sb_at("INV2", [64, 1024], F32, FS_ROWS)
        c.UBUF = sb_at("UBUF", [64, 1024], F32, FS_ROWS + 4096)
        c.ATTNT = sb_at("ATTNT", [64, 2048], F32, FS_SQ)
        c.HIDX = nc.alloc_sbuf_tensor_at("HIDX", [128, 2], mybir.dt.int32,
                                         offset=OFF_CONST + 1344 + 2496)
        c.PSA = stack.enter_context(nc.psum_tensor("PSA", [128, 1024], F32))
        c.PSB = stack.enter_context(nc.psum_tensor("PSB", [128, 1024], F32))
        c.PSC = stack.enter_context(nc.psum_tensor("PSC", [128, 1024], F32))
        c.PSE = stack.enter_context(nc.psum_tensor("PSE", [128, 2048], BF16))

        sems = {}
        for sn in _sem_names():
            sems[sn] = stack.enter_context(nc.semaphore(sn))
        c.sems = sems
        block = stack.enter_context(nc.Block())

        plan_all(c, sublayers)

        def runner(eng_key):
            def run(eng):
                for fn, waits, incs in p.ops[eng_key]:
                    for s, v in waits:
                        eng.wait_ge(sems[s], v)
                    inst = fn(eng) if fn is not None else None
                    for s, a in incs:
                        if inst is None:
                            inst = eng.nop()
                        inst.then_inc(sems[s], a)
            return run

        block.tensor(runner("t"))
        block.vector(runner("v"))
        block.scalar(runner("s"))
        block.gpsimd(runner("g"))
        block.sync(runner("y"))

    return nc


def plan_all(c, sublayers):
    p = c.p
    CON = c.din["CONSTS"]
    p.y(lambda e: e.dma_start(out=c.ONES[:, :], in_=CON[:, 0:128]))
    p.y(lambda e: e.dma_start(out=c.IDENT[:, :], in_=CON[:, 128:256]))
    p.g(lambda e: e.dma_start(out=c.IDENTB[:, :], in_=CON[:, 128:256]), dma=True)
    p.y(lambda e: e.dma_start(out=c.NEG8[:, :], in_=CON[:, 256:264]))
    p.y(lambda e: e.dma_start(out=c.CBIAS[:, 0:4], in_=CON[:, 264:268]))
    xin = c.din["xin"]
    for dc in range(NC):
        p.y(lambda e, dc=dc: e.dma_start(out=c.X[dc][:, :],
                                         in_=xin[128 * dc:128 * (dc + 1), :]))
    c.mk_ld = p.mark("DY")
    c.mk_ldg = p.mark("DG")

    if 0 in sublayers or 2 in sublayers:
        W0T = c.din["W0T"]
        for kc in range(NC):
            p.g(lambda e, kc=kc: e.dma_start(out=bfa(c, BFA_W0T + 1024 * kc, 128, 1024),
                                             in_=W0T[kc]), dma=True)
    c.mk_w0 = p.mark("DG")

    first = True
    for sl in sublayers:
        if sl in (0, 2):
            emit_ln(c, half=True, first=first)
            emit_ffn0(c)
        elif sl == 1:
            emit_ln(c, half=False, first=first)
            emit_attn0(c)
        elif sl in (3, 5):
            emit_ln(c, half=False, first=first)
            emit_moe(c)
        elif sl == 4:
            emit_ln(c, half=False, first=first)
            emit_attn1(c)
        first = False

    mkv = p.mark("V")
    for dc in range(NC):
        p.y(lambda e, dc=dc: e.dma_start(
            out=c.xout_ap[128 * dc:128 * (dc + 1), :], in_=c.X[dc][:, :]),
            wait=[("V", mkv)])
    for name, ap_fn, shape in DEBUG_TAPS:
        dout = c.nc.dram_tensor("dbg_" + name, list(shape), ap_fn(c).dtype,
                                kind="ExternalOutput").ap()
        p.y(lambda e, dout=dout, ap_fn=ap_fn: e.dma_start(out=dout, in_=ap_fn(c)),
            wait=[("V", mkv), ("T", p.mark("T")), ("S", p.mark("S"))])
    p.emit("y", None, wait=[("DY", p.mark("DY"))])


def emit_ln(c, half, first=False):
    p = c.p
    mkv_entry = p.mark("V")
    ROW_M = c.fs_rows[0:1, 0:1024]
    ROW_S = c.fs_rows[0:1, 1024:2048]
    ROW_T = c.fs_tmp[0:1, 0:1024]
    wait_ld = [("DY", c.mk_ld)] if first else []
    mk_prev_evac = getattr(c, "mk_ln_evac", 0)

    for qt in range(2):
        qlo = 1024 * qt
        mk_sumsq = 0
        for dc in range(NC):
            p.s(lambda e, dc=dc, qlo=qlo: e.activation(
                out=c.fs_sq[:, :], in_=c.X[dc][:, qlo:qlo + 1024], func=AF.Square),
                wait=wait_ld + [("V", mkv_entry), ("T", mk_sumsq)])
            mksq = p.mark("S")
            for hi in range(2):
                sl = slice(512 * hi, 512 * hi + 512)
                p.t(lambda e, dc=dc, qlo=qlo, sl=sl: e.matmul(
                    c.PSC[0:1, sl], lhsT=c.ONES[:, 0:1],
                    rhs=c.X[dc][:, qlo + sl.start:qlo + sl.stop],
                    start=(dc == 0), stop=(dc == NC - 1)),
                    wait=wait_ld + [("V", mkv_entry), ("S", mk_prev_evac)])
                mk_sumsq = p.t(lambda e, dc=dc, sl=sl: e.matmul(
                    c.PSA[0:1, sl], lhsT=c.ONES[:, 0:1], rhs=c.fs_sq[:, sl],
                    start=(dc == 0), stop=(dc == NC - 1)),
                    wait=[("S", mksq)], inc=True)
        mk_stats = p.mark("T")
        p.s(lambda e: e.activation(out=ROW_M, in_=c.PSC[0:1, 0:1024], func=AF.Copy,
                                   scale=1.0 / D), wait=[("T", mk_stats)])
        p.s(lambda e: e.activation(out=ROW_S, in_=c.PSA[0:1, 0:1024], func=AF.Copy,
                                   scale=1.0 / D))
        mks = p.mark("S")
        p.v(lambda e: e.tensor_tensor(out=ROW_T, in0=ROW_M, in1=ROW_M, op=ALU.mult),
            wait=[("S", mks)])
        p.v(lambda e: e.tensor_tensor(out=ROW_S, in0=ROW_S, in1=ROW_T, op=ALU.subtract))
        mkv2 = p.mark("V")
        p.s(lambda e: e.activation(out=ROW_S, in_=ROW_S, func=AF.Ln,
                                   bias=c.CBIAS[0:1, 0:1]),
            wait=[("V", mkv2)])
        p.s(lambda e: e.activation(out=ROW_S, in_=ROW_S, func=AF.Exp, scale=-0.5))
        if half:
            p.s(lambda e: e.activation(out=ROW_S, in_=ROW_S, func=AF.Copy, scale=0.5))
        mks2 = p.mark("S")
        p.v(lambda e: e.tensor_tensor(out=ROW_T, in0=ROW_M, in1=ROW_S, op=ALU.mult),
            wait=[("S", mks2)])
        mkv3 = p.mark("V")
        for hi in range(2):
            sl = slice(512 * hi, 512 * hi + 512)
            p.t(lambda e, sl=sl: e.matmul(c.PSA[:, sl], lhsT=c.ONES[0:1, :],
                                          rhs=ROW_S[:, sl], start=True, stop=True),
                wait=[("V", mkv3), ("S", mks2)])
            p.t(lambda e, sl=sl: e.matmul(c.PSB[:, sl], lhsT=c.ONES[0:1, :],
                                          rhs=ROW_T[:, sl], start=True, stop=True),
                inc=(hi == 1))
        mk_bc = p.mark("T")
        p.s(lambda e: e.copy(out=c.fs_rb[:, :], in_=c.PSA[:, :]), wait=[("T", mk_bc)])
        p.s(lambda e: e.copy(out=c.fs_mrb[:, :], in_=c.PSB[:, :]))
        mk_prev_evac = p.mark("S")
        for dc in range(NC):
            p.v(lambda e, dc=dc, qlo=qlo: e.tensor_tensor(
                out=c.fs_tmp[:, :], in0=c.X[dc][:, qlo:qlo + 1024],
                in1=c.fs_rb[:, :], op=ALU.mult),
                wait=[("S", mk_prev_evac), ("T", c.mk_xb_free)])
            p.v(lambda e, dc=dc, qlo=qlo: e.tensor_tensor(
                out=c.BFA[0:128, BFA_XB + 2048 * dc + qlo:BFA_XB + 2048 * dc + qlo + 1024],
                in0=c.fs_tmp[:, :], in1=c.fs_mrb[:, :], op=ALU.subtract))
    c.mk_ln_evac = mk_prev_evac
    c.mk_ln_done = p.mark("V")


def emit_ffn0(c):
    p = c.p
    mkln = c.mk_ln_done
    p.g(lambda e: e.dma_start(out=bfa(c, BFA_DISHA, 64, 1024), in_=c.din["DISHA"]),
        wait=[("T", c.mk_xb_free)], dma=True)
    p.g(lambda e: e.dma_start(out=bfa(c, BFA_SSEL, 128, 64), in_=c.din["SSEL"]),
        dma=True)
    mk_small = p.mark("DG")
    DISHAB = bfa(c, BFA_DISHA, 64, 1024)
    SSELB = bfa(c, BFA_SSEL, 128, 64)

    for qs in range(QS):
        sl = slice(512 * qs, 512 * (qs + 1))
        psl = slice(512 * (qs % 2), 512 * (qs % 2) + 512)
        for kc in range(NC):
            p.t(lambda e, kc=kc, sl=sl, psl=psl: e.matmul(
                c.PSC[0:64, psl], lhsT=SSELB, rhs=XBc(c, kc, sl),
                start=(kc == 0), stop=(kc == NC - 1)),
                wait=[("V", mkln), ("DG", max(c.mk_w0, mk_small)),
                      ("S", p.mark("S"))],
                inc=(kc == NC - 1))
        mkt = p.mark("T")
        p.s(lambda e, sl=sl, psl=psl: e.copy(
            out=c.BFA[0:64, BFA_PL + sl.start:BFA_PL + sl.stop],
            in_=c.PSC[0:64, psl]), wait=[("T", mkt)])
    mk_pool = p.mark("S")

    slot_free = [0, 0, 0, 0]
    sidx = 0
    for oc in range(NC):
        for qs in range(QS):
            sl = slice(512 * qs, 512 * (qs + 1))
            ps_t = c.PSA if (sidx % 4) < 2 else c.PSB
            psl = slice(512 * (sidx % 2), 512 * (sidx % 2) + 512)
            for kc in range(NC):
                p.t(lambda e, kc=kc, oc=oc, sl=sl, ps_t=ps_t, psl=psl: e.matmul(
                    ps_t[:, psl],
                    lhsT=bfa(c, BFA_W0T + 1024 * kc + 128 * oc, 128, 128),
                    rhs=XBc(c, kc, sl), start=(kc == 0), stop=False),
                    wait=[("V", slot_free[sidx % 4])] if kc == 0 else [])
            p.t(lambda e, oc=oc, sl=sl, psl=psl, ps_t=ps_t: e.matmul(
                ps_t[:, psl], lhsT=DISHAB[:, 128 * oc:128 * (oc + 1)],
                rhs=c.BFA[0:64, BFA_PL + sl.start:BFA_PL + sl.stop],
                start=False, stop=True),
                wait=[("S", mk_pool)], inc=True)
            mkt = p.mark("T")
            slot_free[sidx % 4] = p.v(
                lambda e, oc=oc, sl=sl, ps_t=ps_t, psl=psl: e.tensor_tensor(
                    out=c.X[oc][:, sl], in0=c.X[oc][:, sl], in1=ps_t[:, psl],
                    op=ALU.add),
                wait=[("T", mkt)])
            sidx += 1
    c.mk_xb_free = p.mark("T")


def emit_attn0(c):
    p = c.p
    mkln = c.mk_ln_done
    PQ = c.din["PQ"]
    POT = c.din["POT"]

    PKB = 36864
    PVB = 37888
    PQBUF = 38912
    POTBUF = 12288
    STG_K = 32768
    STG_V = 34816
    for h0 in range(8):
        p.g(lambda e, h0=h0: e.dma_start(
            out=c.dram["pqb"].ap()[h0].rearrange("p (k j) -> p k j", k=8),
            in_=PQ[h0].rearrange("k p j -> p k j")), dma="DG2")
    mk_pqcast = p.mark("DG2")
    p.g(lambda e: e.dma_start(out=bfa(c, PKB, 128, 1024).rearrange("p (k j) -> p k j", k=8),
                              in_=c.din["PK"].rearrange("k p j -> p k j")), dma=True)
    p.g(lambda e: e.dma_start(out=bfa(c, PVB, 128, 1024).rearrange("p (k j) -> p k j", k=8),
                              in_=c.din["PV"].rearrange("k p j -> p k j")), dma=True)
    mk_pkv = p.mark("DG")

    for qs in range(QS):
        sl = slice(512 * qs, 512 * (qs + 1))
        ps_t = c.PSA if qs % 2 == 0 else c.PSB
        psl = slice(512, 1024)
        for kc in range(NC):
            p.t(lambda e, kc=kc, sl=sl, ps_t=ps_t, psl=psl: e.matmul(
                ps_t[:, psl], lhsT=bfa(c, PKB + 128 * kc, 128, 128),
                rhs=XBc(c, kc, sl), start=(kc == 0), stop=(kc == NC - 1)),
                wait=([("DG", mk_pkv), ("S", p.mark("S")), ("V", mkln)]
                      if kc == 0 else []),
                inc=(kc == NC - 1))
        mkt = p.mark("T")
        p.s(lambda e, sl=sl, ps_t=ps_t, psl=psl: e.copy(
            out=c.BFA[0:128, STG_K + sl.start:STG_K + sl.stop], in_=ps_t[:, psl]),
            wait=[("T", mkt)])

    for tc in range(16):
        tsl = slice(128 * tc, 128 * (tc + 1))
        psl = slice(512 * (tc % 2), 512 * (tc % 2) + 128)
        for kc in range(NC):
            p.t(lambda e, kc=kc, tsl=tsl, psl=psl: e.matmul(
                c.PSC[:, psl], lhsT=XBc(c, kc, tsl),
                rhs=bfa(c, PVB + 128 * kc, 128, 128),
                start=(kc == 0), stop=(kc == NC - 1)),
                wait=[("S", p.mark("S"))] if kc == 0 else [],
                inc=(kc == NC - 1))
        mkt = p.mark("T")
        p.s(lambda e, tc=tc, psl=psl: e.copy(
            out=bfa(c, STG_V + 128 * tc, 128, 128), in_=c.PSC[:, psl]),
            wait=[("T", mkt)])
    c.mk_xb_free = p.mark("T")
    mk_stage = p.mark("S")

    kv_in = c.dram["kv_in"]
    kv_out = c.dram["kv_out"]
    p.y(lambda e: e.dma_start(out=kv_in.ap(), in_=bfa(c, STG_K, 128, 4096)),
        wait=[("S", mk_stage)])
    mk_kvout = p.mark("DY")
    p.emit("g", lambda e: e.collective_compute(
        "AllGather", ALU.bypass, replica_groups=GROUPS_RG,
        ins=[kv_in.ap().opt()], outs=[kv_out.ap().opt()]),
        wait=[("DY", mk_kvout)], inc=[("CC0", 1)])
    pq_mk = [0, 0]
    qt_mks = []
    for h in range(8):
        buf = h % 2
        p.y(lambda e, h=h, buf=buf: e.dma_start(
            out=bfa(c, PQBUF + 1024 * buf, 128, 1024),
            in_=c.dram["pqb"].ap()[h]),
            wait=[("T", pq_mk[buf]), ("DG2", mk_pqcast)])
        mk_dma = p.mark("DY")
        for qs in range(QS):
            sl = slice(512 * qs, 512 * (qs + 1))
            ps_t = c.PSA if qs % 2 == 0 else c.PSB
            psl = slice(0, 512)
            for kc in range(NC):
                p.t(lambda e, h=h, buf=buf, kc=kc, sl=sl, ps_t=ps_t, psl=psl: e.matmul(
                    ps_t[:, psl], lhsT=bfa(c, PQBUF + 1024 * buf + 128 * kc, 128, 128),
                    rhs=XBc(c, kc, sl), start=(kc == 0), stop=(kc == NC - 1)),
                    wait=([("V", mkln), ("DY", mk_dma), ("S", p.mark("S"))]
                          if kc == 0 else []),
                    inc=(kc == NC - 1))
            mkt = p.mark("T")
            if qs == QS - 1:
                pq_mk[buf] = mkt
                if DEBUG_PQDUMP:
                    p.y(lambda e, h=h, buf=buf: e.dma_start(
                        out=c.dbg_pq[h], in_=bfa(c, PQBUF + 1024 * buf, 128, 1024)),
                        wait=[("T", mkt)])
            p.s(lambda e, h=h, sl=sl, ps_t=ps_t, psl=psl: e.copy(
                out=c.BFA[0:128, BFA_QT0 + 2048 * h + sl.start:BFA_QT0 + 2048 * h + sl.stop],
                in_=ps_t[:, psl]), wait=[("T", mkt)])

    c.mk_xb_free = p.mark("T")
    for r in range(4):
        p.y(lambda e, r=r: e.dma_start(
            out=bfa(c, BFA_KTF + 2048 * r, 128, 2048),
            in_=kv_out.ap()[128 * r:128 * (r + 1), 0:2048]),
            wait=[("CC0", 1), ("DY", mk_kvout), ("T", c.mk_xb_free)])
        p.y(lambda e, r=r: e.dma_start(
            out=bfa(c, BFA_VF + 2048 * r, 128, 2048),
            in_=kv_out.ap()[128 * r:128 * (r + 1), 2048:4096]))
    mk_kv = p.mark("DY")

    mk_ot_free = [0] * 8
    pot_mk = [0, 0]
    for qt in range(2):
        qlo = 1024 * qt
        for h in range(8):
            hstart_t = p.mark("T")
            hstart_s = p.mark("S")
            hstart_v = p.mark("V")
            mk_exp = [0, 0]
            mk_add = 0
            mk_addp = [0, 0, 0, 0]
            for mc in range(65):
                if mc < 64:
                    ps_t = c.PSA if mc % 2 == 0 else c.PSB
                    for hi in range(2):
                        p.t(lambda e, mc=mc, hi=hi, ps_t=ps_t, h=h, qlo=qlo: e.matmul(
                            ps_t[:, 512 * hi:512 * hi + 512],
                            lhsT=bfa(c, BFA_KTF + 128 * mc, 128, 128),
                            rhs=c.BFA[0:128, BFA_QT0 + 2048 * h + qlo + 512 * hi:
                                      BFA_QT0 + 2048 * h + qlo + 512 * hi + 512],
                            start=True, stop=True),
                            wait=([("DY", mk_kv), ("S", max(mk_exp[mc % 2], hstart_s)),
                                   ("V", hstart_v)]
                                  if hi == 0 else []),
                            inc=(hi == 1))
                    mk_l = p.mark("T")
                    mk_exp[mc % 2] = p.s(
                        lambda e, mc=mc, ps_t=ps_t: e.activation(
                            out=bfa(c, BFA_E0 + 1024 * (mc % 4), 128, 1024),
                            in_=ps_t[:, :], func=AF.Exp, bias=c.CBIAS[:, 1:2]),
                        wait=[("T", max(mk_l, c.mk_xb_free)), ("V", mk_addp[mc % 4])])
                    if mc == 0:
                        mk_add = p.v(lambda e: e.tensor_copy(
                            out=c.fs_acc[:, :], in_=bfa(c, BFA_E0, 128, 1024)),
                            wait=[("S", mk_exp[0]), ("T", hstart_t)])
                    else:
                        mk_add = p.v(lambda e, mc=mc: e.tensor_tensor(
                            out=c.fs_acc[:, :], in0=c.fs_acc[:, :],
                            in1=bfa(c, BFA_E0 + 1024 * (mc % 4), 128, 1024), op=ALU.add),
                            wait=[("S", mk_exp[mc % 2])])
                    mk_addp[mc % 4] = mk_add
                if mc >= 1:
                    m = mc - 1
                    for hi in range(2):
                        p.t(lambda e, m=m, hi=hi: e.matmul(
                            c.PSC[:, 512 * hi:512 * hi + 512],
                            lhsT=bfa(c, BFA_VF + 128 * m, 128, 128),
                            rhs=c.BFA[0:128, BFA_E0 + 1024 * (m % 4) + 512 * hi:
                                      BFA_E0 + 1024 * (m % 4) + 512 * hi + 512],
                            start=(m == 0), stop=(m == 63)),
                            wait=([("S", mk_exp[m % 2])] if hi == 0 else []),
                            inc=(hi == 1 and m == 63))
            mk_av = p.mark("T")
            for hi in range(2):
                p.t(lambda e, hi=hi: e.matmul(
                    c.PSA[0:1, 512 * hi:512 * hi + 512], lhsT=c.ONES[:, 0:1],
                    rhs=c.fs_acc[:, 512 * hi:512 * hi + 512], start=True, stop=True),
                    wait=[("V", mk_add), ("S", mk_exp[0])], inc=(hi == 1))
            mk_sum = p.mark("T")
            ROW = c.fs_rows[0:1, 0:1024]
            p.v(lambda e: e.reciprocal(out=ROW, in_=c.PSA[0:1, 0:1024]),
                wait=[("T", mk_sum)])
            mk_rec = p.mark("V")
            for hi in range(2):
                p.t(lambda e, hi=hi: e.matmul(
                    c.PSB[:, 512 * hi:512 * hi + 512], lhsT=c.ONES[0:1, :],
                    rhs=ROW[:, 512 * hi:512 * hi + 512], start=True, stop=True),
                    wait=[("V", mk_rec), ("S", mk_exp[1])], inc=(hi == 1))
            mk_bc = p.mark("T")
            p.s(lambda e: e.copy(out=c.fs_rb[:, :], in_=c.PSB[:, :]),
                wait=[("T", mk_bc)])
            mk_rbe = p.mark("S")
            p.v(lambda e, h=h: e.tensor_tensor(
                out=bfa(c, BFA_OT + 1024 * h, 128, 1024), in0=c.PSC[:, :],
                in1=c.fs_rb[:, :], op=ALU.mult),
                wait=[("S", mk_rbe), ("T", max(mk_av, mk_ot_free[h]))])
        mk_norm = p.mark("V")

        yslot_free = [0, 0]
        for dc in range(NC):
            buf = dc % 2
            p.g(lambda e, dc=dc, buf=buf: e.dma_start(
                out=bfa(c, POTBUF + 1024 * buf, 128, 1024).rearrange("p (h j) -> p h j", h=8),
                in_=POT[:, :, 128 * dc:128 * (dc + 1)].rearrange("h p j -> p h j")),
                wait=[("T", max(pot_mk[buf], c.mk_xb_free))], dma="DG2")
            mk_dma = p.mark("DG2")
            for qs in range(2):
                si = (dc * 2 + qs) % 2
                ps_t = c.PSA if si == 0 else c.PSB
                for h in range(8):
                    p.t(lambda e, dc=dc, qs=qs, h=h, buf=buf, ps_t=ps_t: e.matmul(
                        ps_t[:, 0:512],
                        lhsT=bfa(c, POTBUF + 1024 * buf + 128 * h, 128, 128),
                        rhs=bfa(c, BFA_OT + 1024 * h + 512 * qs, 128, 512),
                        start=(h == 0), stop=(h == 7)),
                        wait=[("V", max(mk_norm, yslot_free[si])), ("DG2", mk_dma)]
                        if h == 0 else [],
                        inc=(h == 7))
                mkt = p.mark("T")
                if qs == 1:
                    pot_mk[buf] = mkt
                yslot_free[si] = p.v(
                    lambda e, dc=dc, qs=qs, qlo=qlo, ps_t=ps_t: e.tensor_tensor(
                        out=c.X[dc][:, qlo + 512 * qs:qlo + 512 * qs + 512],
                        in0=c.X[dc][:, qlo + 512 * qs:qlo + 512 * qs + 512],
                        in1=ps_t[:, 0:512], op=ALU.add),
                    wait=[("T", mkt)])
        mk_y = p.mark("T")
        for h in range(8):
            mk_ot_free[h] = mk_y


def emit_moe(c):
    p = c.p
    mkln = c.mk_ln_done
    MSD = c.din["MS"]
    GWTB = bfa(c, BFA_MTMP, 128, 64)
    GTOK = c.fs_tmp[:, 0:128]
    GROW = c.fs_rows[0:1, 0:2048]
    SC = c.fs_mrb

    p.g(lambda e: e.dma_start(out=GWTB.rearrange("p (k e) -> p k e", k=8),
                              in_=c.din["GWT"].rearrange("k p e -> p k e")),
        wait=[("S", p.mark("S"))], dma="DG3")
    mk_gw = p.mark("DG3")

    ev_mks = [0, 0]
    for tc in range(16):
        psl = slice(512 * (tc % 2), 512 * (tc % 2) + 8)
        tsl = slice(128 * tc, 128 * (tc + 1))
        for kc in range(NC):
            p.t(lambda e, kc=kc, tsl=tsl, psl=psl: e.matmul(
                c.PSA[:, psl], lhsT=XBc(c, kc, tsl),
                rhs=GWTB[:, 8 * kc:8 * kc + 8],
                start=(kc == 0), stop=(kc == NC - 1)),
                wait=([("V", mkln), ("DG3", mk_gw), ("S", ev_mks[tc % 2])]
                      if kc == 0 else []),
                inc=(kc == NC - 1))
        mkt = p.mark("T")
        ev_mks[tc % 2] = p.s(lambda e, tc=tc, psl=psl: e.copy(
            out=GTOK[:, 8 * tc:8 * tc + 8], in_=c.PSA[:, psl]), wait=[("T", mkt)])
    mk_glog = p.mark("S")

    M1N = SC[:, 0:1]
    M1P = SC[:, 1:2]
    M2N = SC[:, 2:3]
    M2P = SC[:, 3:4]
    ZS = SC[:, 4:5]
    RS = SC[:, 5:6]
    SEXP = SC[:, 8:16]
    L2 = SC[:, 16:24]
    KEEP = SC[:, 24:32]
    for tc in range(16):
        G8 = GTOK[:, 8 * tc:8 * tc + 8]
        mv = p.v(lambda e, G8=G8: e.tensor_reduce(out=M1N, in_=G8, axis=AX.X,
                                                  op=ALU.max, negate=True),
                 wait=[("S", mk_glog)])
        p.s(lambda e, G8=G8: e.activation(out=SEXP, in_=G8, func=AF.Exp,
                                          bias=M1N), wait=[("V", mv)])
        mv = p.v(lambda e: e.tensor_scalar_mul(out=M1P, in0=M1N, scalar1=-1.0),
                 wait=[("V", mv)])
        mv = p.v(lambda e, G8=G8: e.tensor_scalar(out=KEEP, in0=G8, scalar1=M1P,
                                                  scalar2=None, op0=ALU.is_lt),
                 wait=[("V", mv)])
        mv = p.v(lambda e: e.tensor_scalar(out=L2, in0=KEEP, scalar1=1e30,
                                           scalar2=-1e30, op0=ALU.mult, op1=ALU.add),
                 wait=[("V", mv)])
        mv = p.v(lambda e, G8=G8: e.tensor_tensor(out=L2, in0=L2, in1=G8, op=ALU.add),
                 wait=[("V", mv)])
        mv = p.v(lambda e: e.tensor_reduce(out=M2N, in_=L2, axis=AX.X, op=ALU.max,
                                           negate=True), wait=[("V", mv)])
        mv = p.v(lambda e: e.tensor_scalar_mul(out=M2P, in0=M2N, scalar1=-1.0),
                 wait=[("V", mv)])
        mv = p.v(lambda e, G8=G8: e.tensor_scalar(out=KEEP, in0=G8, scalar1=M2P,
                                                  scalar2=None, op0=ALU.is_ge),
                 wait=[("V", mv)])
        mks = p.mark("S")
        mv = p.v(lambda e: e.tensor_tensor(out=SEXP, in0=SEXP, in1=KEEP, op=ALU.mult),
                 wait=[("S", mks), ("V", mv)])
        mv = p.v(lambda e: e.tensor_reduce(out=ZS, in_=SEXP, axis=AX.X, op=ALU.add),
                 wait=[("V", mv)])
        mv = p.v(lambda e: e.reciprocal(out=RS, in_=ZS), wait=[("V", mv)])
        p.v(lambda e, G8=G8: e.tensor_scalar(out=G8, in0=SEXP, scalar1=RS,
                                             scalar2=0.5, op0=ALU.mult, op1=ALU.mult),
            wait=[("V", mv)])
    mk_gtok = p.mark("V")

    w13_mk = [0, 0]
    w2r_mk = 0
    mk_geb_read = 0
    mk_hb_free = 0
    for ex in range(EXPERTS):
        mk_ps_a = p.mark("S")
        mk_vv = p.mark("V")
        tr_ev = [mk_ps_a, mk_ps_a]
        for tc in range(16):
            po = 128 * (tc % 2)
            po = 512 * (tc % 2)
            p.t(lambda e, ex=ex, tc=tc, po=po: e.transpose(
                c.PSA[0:1, po:po + 128],
                in_=GTOK[:, 8 * tc + ex:8 * tc + ex + 1], identity=c.IDENT[:, :]),
                wait=[("V", max(mk_gtok, mk_vv)), ("S", tr_ev[tc % 2])], inc=True)
            mkt = p.mark("T")
            tr_ev[tc % 2] = p.s(lambda e, tc=tc, po=po: e.copy(
                out=GROW[:, 128 * tc:128 * (tc + 1)],
                in_=c.PSA[0:1, po:po + 128]), wait=[("T", mkt)])
        mk_grow = p.mark("S")
        bc_ev = [mk_grow, mk_grow]
        for hi in range(4):
            p.t(lambda e, hi=hi: e.matmul(
                c.PSA[:, 512 * (hi % 2):512 * (hi % 2) + 512], lhsT=c.ONES[0:1, :],
                rhs=GROW[:, 512 * hi:512 * hi + 512], start=True, stop=True),
                wait=[("S", max(mk_grow, bc_ev[hi % 2]))], inc=True)
            mkt = p.mark("T")
            bc_ev[hi % 2] = p.s(lambda e, hi=hi: e.copy(
                out=bfa(c, BFA_GEB + 512 * hi, 128, 512),
                in_=c.PSA[:, 512 * (hi % 2):512 * (hi % 2) + 512]),
                wait=[("T", mkt), ("V", mk_geb_read)])
        mk_geb = p.mark("S")

        for half in range(2):
            hcs = list(range(11 * half, 11 * half + 11))
            silu_mks = [0, 0]
            t3_mks = [0, 0]
            mtmp_v = [0, 0, 0, 0]
            pv0 = p.mark("V")
            sidx = 0
            for i, hc in enumerate(hcs):
                buf = hc % 2
                p.g(lambda e, ex=ex, hc=hc, buf=buf: e.dma_start(
                    out=bfa(c, BFA_W13 + 2048 * buf, 128, 2048).rearrange(
                        "p (m f) -> p m f", m=2),
                    in_=MSD[ex, hc, 0:2].rearrange("m p f -> p m f")),
                    wait=[("T", w13_mk[buf])], dma="DG3")
                mk_dma = p.mark("DG3")
                p.g(lambda e, ex=ex, hc=hc, i=i: e.dma_start(
                    out=bfa(c, BFA_W2R + 1024 * i, 128, 1024),
                    in_=MSD[ex, hc, 2]),
                    wait=[("T", w2r_mk)], dma="DG4")
                for sl4 in range(4):
                    sl = slice(512 * sl4, 512 * (sl4 + 1))
                    par = sidx % 2
                    psl = slice(512 * par, 512 * par + 512)
                    for kc in range(NC):
                        p.t(lambda e, buf=buf, kc=kc, sl=sl, psl=psl: e.matmul(
                            c.PSA[:, psl], lhsT=bfa(c, BFA_W13 + 2048 * buf + 128 * kc, 128, 128),
                            rhs=XBc(c, kc, sl), start=(kc == 0), stop=(kc == NC - 1)),
                            wait=([("DG3", mk_dma), ("S", silu_mks[par]), ("V", mkln)]
                                  if kc == 0 else []))
                    for kc in range(NC):
                        p.t(lambda e, buf=buf, kc=kc, sl=sl, psl=psl: e.matmul(
                            c.PSB[:, psl],
                            lhsT=bfa(c, BFA_W13 + 2048 * buf + 1024 + 128 * kc, 128, 128),
                            rhs=XBc(c, kc, sl), start=(kc == 0), stop=(kc == NC - 1)),
                            wait=[("V", max(t3_mks[par], pv0))] if kc == 0 else [],
                            inc=(kc == NC - 1))
                    mkt = p.mark("T")
                    ms = sidx % 4
                    silu_mks[par] = p.s(lambda e, psl=psl: e.activation(
                        out=bfa(c, BFA_MTMP, 128, 512), in_=c.PSA[:, psl],
                        func=AF.Silu), wait=[("T", mkt), ("V", mtmp_v[0])])
                    t3_mks[par] = p.v(lambda e, sl4=sl4, psl=psl: e.tensor_tensor(
                        out=bfa(c, BFA_MTMP + 512, 128, 512), in0=c.PSB[:, psl],
                        in1=bfa(c, BFA_GEB + 512 * sl4, 128, 512), op=ALU.mult),
                        wait=[("T", mkt), ("S", mk_geb)])
                    mtmp_v[0] = p.v(lambda e, i=i, sl4=sl4: e.tensor_tensor(
                        out=bfa(c, BFA_HB + 2048 * i + 512 * sl4, 128, 512),
                        in0=bfa(c, BFA_MTMP, 128, 512),
                        in1=bfa(c, BFA_MTMP + 512, 128, 512), op=ALU.mult),
                        wait=[("S", silu_mks[par]), ("T", mk_hb_free)])
                    sidx += 1
                w13_mk[buf] = p.mark("T")
            mk_w2dma = p.mark("DG4")
            mk_h = p.mark("V")

            oslot_mk = [0, 0]
            ti = 0
            for dc in range(NC):
                for qh in range(2):
                    ps_t = c.PSB if ti % 2 == 0 else c.PSC
                    for i, hc in enumerate(hcs):
                        for s2 in range(2):
                            p.t(lambda e, dc=dc, qh=qh, i=i, s2=s2, ps_t=ps_t: e.matmul(
                                ps_t[:, 512 * s2:512 * s2 + 512],
                                lhsT=bfa(c, BFA_W2R + 1024 * i + 128 * dc, 128, 128),
                                rhs=bfa(c, BFA_HB + 2048 * i + 1024 * qh + 512 * s2,
                                        128, 512),
                                start=(i == 0), stop=(i == 10)),
                                wait=([("DG4", mk_w2dma), ("V", max(mk_h, oslot_mk[ti % 2]))]
                                      if (i == 0 and s2 == 0) else []),
                                inc=(i == 10 and s2 == 1))
                    mkt = p.mark("T")
                    oslot_mk[ti % 2] = p.v(
                        lambda e, dc=dc, qh=qh, ps_t=ps_t: e.tensor_tensor(
                            out=c.X[dc][:, 1024 * qh:1024 * qh + 1024],
                            in0=c.X[dc][:, 1024 * qh:1024 * qh + 1024],
                            in1=ps_t[:, :], op=ALU.add),
                        wait=[("T", mkt)])
                    ti += 1
            w2r_mk = p.mark("T")
            mk_hb_free = w2r_mk
        mk_geb_read = max(t3_mks)
    c.mk_xb_free = p.mark("T")


ATT1_SCALE = 0.35355339059327373


def emit_attn1(c):
    from concourse.bass_core_rust import IndirectOffsetOnAxis
    p = c.p
    mkln = c.mk_ln_done
    CE = c.CEXTRA
    BQ_, BK_ = CE[:, 0:8], CE[:, 8:16]
    BQS, BKS = CE[:, 16:24], CE[:, 24:32]
    MX = CE[0:64, 32:48]
    Z3 = CE[0:64, 48:64]
    MXG = CE[0:64, 64:80]
    NEGM = CE[0:64, 96:112]
    Z3R = CE[0:64, 112:128]
    ZPART = CE[0:64, 116 + 12:116 + 14]
    ZPART = CE[0:64, 126:128]
    K1F = CE[:, 128:192]
    K1N = CE[:, 192:256]
    K1ROW = CE[:, 256:260]
    NM64 = CE[0:64, 260:262]
    SM64 = CE[0:64, 262:264]
    RC64 = CE[0:64, 264:266]
    CS64 = CE[0:64, 266:268]
    RSC = CE[0:64, 268:270]

    WQ_BUF = [49152, 50176]
    VSTG = [22528, 21504]
    CONVB = [55296, 55680]
    HALO_L, HALO_R = 49152, 50176

    p.y(lambda e: e.dma_start(out=BQ_, in_=c.din["BQ"]))
    p.y(lambda e: e.dma_start(out=BK_, in_=c.din["BK"]))
    p.y(lambda e: e.dma_start(out=c.fs_rows[0:1, 0:1024], in_=c.din["BV"]),
        wait=[("V", mkln)])
    p.y(lambda e: e.dma_start(out=c.HIDX[:, :], in_=c.din["HALOIDX"]))
    mk_bload = p.mark("DY")
    p.g(lambda e: e.dma_start(out=c.BFA[0:64, 51200:54272], in_=c.din["EYES"]),
        wait=[("T", c.mk_xb_free)], dma="DG5")
    mk_eyesb = p.mark("DG5")
    p.s(lambda e: e.activation(out=BQS, in_=BQ_, func=AF.Copy, scale=ATT1_SCALE),
        wait=[("DY", mk_bload)])
    p.s(lambda e: e.activation(out=BKS, in_=BK_, func=AF.Copy, scale=ATT1_SCALE))
    mk_bscale = p.mark("S")
    for kc in range(NC):
        p.g(lambda e, kc=kc: e.dma_start(out=bfa(c, BFA_WVR + 1024 * kc, 128, 1024),
                                         in_=c.din["WV"][kc]), dma="DG5")
    mk_wv = p.mark("DG5")
    for hi in range(2):
        p.t(lambda e, hi=hi: e.matmul(
            c.PSA[:, 512 * hi:512 * hi + 512], lhsT=c.ONES[0:1, :],
            rhs=c.fs_rows[0:1, 512 * hi:512 * hi + 512], start=True, stop=True),
            wait=[("DY", mk_bload), ("S", p.mark("S"))], inc=(hi == 1))
    mk_bvb = p.mark("T")
    p.s(lambda e: e.copy(out=bfa(c, BFA_BVB, 128, 1024), in_=c.PSA[:, :]),
        wait=[("T", mk_bvb)])
    mk_bvbe = p.mark("S")

    vstg_mk = [0, 0]
    for tc in range(16):
        tsl = slice(128 * tc, 128 * (tc + 1))
        ps_t = c.PSB if tc % 2 == 0 else c.PSC
        for jh in range(2):
            for kc in range(NC):
                p.t(lambda e, kc=kc, tsl=tsl, jh=jh, ps_t=ps_t: e.matmul(
                    ps_t[:, 512 * jh:512 * jh + 512], lhsT=XBc(c, kc, tsl),
                    rhs=bfa(c, BFA_WVR + 1024 * kc + 512 * jh, 128, 512),
                    start=(kc == 0), stop=(kc == NC - 1)),
                    wait=([("V", max(mkln, vstg_mk[tc % 2])), ("DG5", mk_wv)]
                          if kc == 0 else []),
                    inc=(kc == NC - 1 and jh == 1))
        mkt = p.mark("T")
        vstg_mk[tc % 2] = p.v(lambda e, tc=tc, ps_t=ps_t: e.tensor_tensor(
            out=bfa(c, VSTG[tc % 2], 128, 1024), in0=ps_t[:, :],
            in1=bfa(c, BFA_BVB, 128, 1024), op=ALU.add),
            wait=[("T", mkt), ("S", mk_bvbe), ("DY", p.mark("DY"))])
        mkv = p.mark("V")
        p.y(lambda e, tc=tc: e.dma_start(out=c.dram["v_dram"].ap()[tc],
                                         in_=bfa(c, VSTG[tc % 2], 128, 1024)),
            wait=[("V", mkv)])
        if tc == 0:
            p.y(lambda e: e.dma_start(out=c.dram["vh_in"].ap()[0:128, :],
                                      in_=bfa(c, VSTG[0], 128, 1024)))
        if tc == 15:
            p.y(lambda e: e.dma_start(out=c.dram["vh_in"].ap()[128:256, :],
                                      in_=bfa(c, VSTG[1], 128, 1024)))
    mk_vdma = p.mark("DY")
    p.emit("g", lambda e: e.collective_compute(
        "AllGather", ALU.bypass, replica_groups=GROUPS_RG,
        ins=[c.dram["vh_in"].ap().opt()], outs=[c.dram["vh_out"].ap().opt()]),
        wait=[("DY", mk_vdma)], inc=[("CC4", 1)])

    wbuf_mk = [0, 0]
    for z, (WJ, BS, base) in enumerate(
            [("WQJ", BQS, BFA_QT1), ("WKJ", BKS, BFA_KT1)]):
        for jc in range(NC):
            buf = jc % 2
            p.g(lambda e, WJ=WJ, jc=jc, buf=buf: e.dma_start(
                out=bfa(c, WQ_BUF[buf], 128, 1024).rearrange("p (k j) -> p k j", k=8),
                in_=c.din[WJ][jc].rearrange("k p j -> p k j")),
                wait=[("T", wbuf_mk[buf])], dma="DG5")
            mk_dma = p.mark("DG5")
            for qs in range(QS):
                sl = slice(512 * qs, 512 * (qs + 1))
                ps_t = c.PSA if qs % 2 == 0 else c.PSB
                for kc in range(NC):
                    p.t(lambda e, buf=buf, kc=kc, sl=sl, ps_t=ps_t: e.matmul(
                        ps_t[:, 0:512], lhsT=bfa(c, WQ_BUF[buf] + 128 * kc, 128, 128),
                        rhs=XBc(c, kc, sl), start=(kc == 0), stop=(kc == NC - 1)),
                        wait=([("DG5", mk_dma), ("S", p.mark("S")), ("V", mkln)]
                              if kc == 0 else []),
                        inc=(kc == NC - 1))
                mkt = p.mark("T")
                if qs == QS - 1:
                    wbuf_mk[buf] = mkt
                p.s(lambda e, base=base, jc=jc, sl=sl, ps_t=ps_t, BS=BS: e.activation(
                    out=c.BFA[0:128, base + 2048 * jc + sl.start:base + 2048 * jc + sl.stop],
                    in_=ps_t[:, 0:512], func=AF.Identity, scale=ATT1_SCALE,
                    bias=BS[:, jc:jc + 1]),
                    wait=[("T", mkt), ("DY", mk_vdma if z == 0 else 0),
                          ("S", mk_bscale)])
    c.mk_xb_free = p.mark("T")
    mk_qk = p.mark("S")

    for tc in range(16):
        p.y(lambda e, tc=tc: e.dma_start(out=bfa(c, BFA_V1 + 1024 * tc, 128, 1024),
                                         in_=c.dram["v_dram"].ap()[tc]),
            wait=[("T", c.mk_xb_free), ("S", mk_qk), ("DY", mk_vdma)])
    p.v(lambda e: e.memset(bfa(c, HALO_L, 128, 1024), 0.0),
        wait=[("T", c.mk_xb_free)])
    p.v(lambda e: e.memset(bfa(c, HALO_R, 128, 1024), 0.0))
    mk_hz = p.mark("V")
    p.g(lambda e: e.indirect_dma_start(
        out=bfa(c, HALO_L, 128, 1024), out_offset=None,
        in_=c.dram["vh_out"].ap(),
        in_offset=IndirectOffsetOnAxis(c.HIDX[:, 0:1], 0),
        bounds_check=1023, oob_is_err=False),
        wait=[("CC4", 1), ("V", mk_hz), ("DY", mk_bload)], dma="DG5")
    p.g(lambda e: e.indirect_dma_start(
        out=bfa(c, HALO_R, 128, 1024), out_offset=None,
        in_=c.dram["vh_out"].ap(),
        in_offset=IndirectOffsetOnAxis(c.HIDX[:, 1:2], 0),
        bounds_check=1023, oob_is_err=False), dma="DG5")
    mk_v1 = p.mark("DY")
    mk_halo = p.mark("DG5")

    mv = 0
    for z, base in enumerate([BFA_QT1, BFA_KT1]):
        for dc in range(NC):
            mv = p.v(lambda e, z=z, dc=dc, base=base: e.tensor_reduce(
                out=c.QKL[:, 128 * z + 16 * dc:128 * z + 16 * dc + 16],
                in_=c.BFA[0:128, base + 2048 * dc:
                          base + 2048 * dc + 2048].rearrange(
                    "p (s j) -> p s j", s=16),
                axis=AX.X, op=ALU.add),
                wait=[("S", mk_qk)])
    p.s(lambda e: e.activation(out=c.QKL2[:, 0:256], in_=c.QKL[:, 0:256], func=AF.Copy,
                               scale=1.0 / 128.0), wait=[("V", mv)])
    mk_l = p.mark("S")
    p.y(lambda e: e.dma_start(
        out=c.dram["lm_in"].ap().rearrange("(p z c) l -> p z c l", p=128, z=2, c=8),
        in_=c.QKL2[:, 0:256].rearrange("p (z c l) -> p z c l", z=2, c=8)),
        wait=[("S", mk_l)])
    mk_lmin = p.mark("DY")
    p.emit("g", lambda e: e.collective_compute(
        "AllGather", ALU.bypass, replica_groups=GROUPS_RG,
        ins=[c.dram["lm_in"].ap().opt()], outs=[c.dram["lm_out"].ap().opt()]),
        wait=[("DY", mk_lmin)], inc=[("CC1", 1)])
    for r in range(4):
        p.y(lambda e, r=r: e.dma_start(
            out=c.QLF[:, :].rearrange("p (z c l) -> p z c l", z=2, c=8)[
                :, :, :, 16 * r:16 * (r + 1)],
            in_=c.dram["lm_out"].ap().rearrange(
                "(r p z c) l -> r p z c l", r=4, p=128, z=2, c=8)[r]),
            wait=[("CC1", 1)])
    mk_qlf = p.mark("DY")
    p.s(lambda e: e.copy(out=bfa(c, BFA_QLB, 128, 1024), in_=c.QLF[:, :]),
        wait=[("DY", mk_qlf)])
    mk_qlb = p.mark("S")

    l3_done_s = 0
    for h in range(ATT1_H):
        dc, pb = h // 2, 64 * (h % 2)
        for qs in range(QS):
            ps_t = c.PSB if qs < 2 else c.PSC
            p.t(lambda e, h=h, dc=dc, pb=pb, qs=qs, ps_t=ps_t: e.matmul(
                ps_t[0:64, 512 * (qs % 2):512 * (qs % 2) + 512],
                lhsT=c.BFA[pb:pb + 64, BFA_QLB + 64 * dc:BFA_QLB + 64 * dc + 64],
                rhs=c.BFA[pb:pb + 64, BFA_KT1 + 2048 * dc + 512 * qs:
                          BFA_KT1 + 2048 * dc + 512 * qs + 512],
                start=True, stop=True),
                wait=([("S", max(mk_qlb, l3_done_s)), ("V", p.mark("V"))]
                      if qs == 0 else []),
                inc=(qs == 3))
        mkt = p.mark("T")
        mv = p.v(lambda e: e.tensor_reduce(out=ZPART[:, 0:1], in_=c.PSB[0:64, :],
                                           axis=AX.X, op=ALU.max), wait=[("T", mkt)])
        mv = p.v(lambda e: e.tensor_reduce(out=ZPART[:, 1:2], in_=c.PSC[0:64, :],
                                           axis=AX.X, op=ALU.max), wait=[("V", mv)])
        mv = p.v(lambda e, h=h: e.tensor_tensor(out=MX[:, h:h + 1], in0=ZPART[:, 0:1],
                                                in1=ZPART[:, 1:2], op=ALU.max),
                 wait=[("V", mv)])
        l3_done_s = p.s(lambda e: e.copy(out=ZPART[:, 0:1], in_=ZPART[:, 0:1]),
                        wait=[("V", mv)])
    p.y(lambda e: e.dma_start(out=c.dram["mx_in"].ap(), in_=MX), wait=[("S", l3_done_s)])
    p.emit("g", lambda e: e.collective_compute(
        "AllReduce", ALU.max, replica_groups=GROUPS_RG,
        ins=[c.dram["mx_in"].ap().opt()], outs=[c.dram["mx_out"].ap().opt()]),
        wait=[("DY", p.mark("DY"))], inc=[("CC2", 1)])
    p.y(lambda e: e.dma_start(out=MXG, in_=c.dram["mx_out"].ap()), wait=[("CC2", 1)])
    mk_mxg = p.mark("DY")
    mv = p.v(lambda e: e.tensor_scalar_mul(out=NEGM, in0=MXG, scalar1=-1.0),
             wait=[("DY", mk_mxg)])
    mk_negm_v = mv

    TB_K2E = c.UBUF[0:64, 0:1024]
    TB_K2N = c.fs_tmp[0:64, 0:1024]
    TB_K2NT = c.fs_rb[0:64, 0:1024]
    TB_V = c.fs_mrb[0:64, 0:1024]
    TB_KVS = c.fs_sq[0:64, 0:1024]
    TB_KVT = c.UBUF[0:64, 0:1024]
    TB_A = c.INV2[0:64, 0:1024]
    NMB = CE[0:64, 320:336]
    SMB = CE[0:64, 336:352]
    RCB = CE[0:64, 352:368]
    RSCB = CE[0:64, 368:384]
    CSB = CE[0:1, 384:400]
    EYE_B = [c.BFA[0:64, 51200 + 1024 * k:51200 + 1024 * (k + 1)] for k in range(3)]

    def seg3(ap):
        return ap.rearrange("p (s j) -> p s j", s=16)

    mkt = 0
    HORD = [h for h in range(ATT1_H) if h % 2 == 0] + \
        [h for h in range(ATT1_H) if h % 2 == 1]
    for hi, h in enumerate(HORD):
        dc, pb = h // 2, 64 * (h % 2)
        mkt = p.t(lambda e, h=h, dc=dc, pb=pb: e.matmul(
            c.PSA[0:64, 64 * h:64 * h + 64],
            lhsT=c.QLF[pb:pb + 64, 64 * dc:64 * dc + 64],
            rhs=c.QLF[pb:pb + 64, 512 + 64 * dc:512 + 64 * dc + 64],
            start=True, stop=True),
            wait=([("DY", mk_qlf), ("S", p.mark("S"))] if hi == 0 else []),
            inc=(hi == ATT1_H - 1))
    v_nmb = p.v(lambda e: e.tensor_reduce(
        out=NMB, in_=seg3(c.PSA[0:64, 0:1024]), axis=AX.X, op=ALU.max, negate=True),
        wait=[("T", mkt)])
    mks = 0
    for h in range(ATT1_H):
        mks = p.s(lambda e, h=h: e.activation(
            out=TB_K2E[:, 64 * h:64 * h + 64], in_=c.PSA[0:64, 64 * h:64 * h + 64],
            func=AF.Exp, bias=NMB[:, h:h + 1]),
            wait=[("V", v_nmb)] if h == 0 else [])
    mv = p.v(lambda e: e.tensor_reduce(
        out=SMB, in_=seg3(TB_K2E), axis=AX.X, op=ALU.add), wait=[("S", mks)])
    mv = p.v(lambda e: e.reciprocal(out=RCB, in_=SMB), wait=[("V", mv)])
    for h in range(ATT1_H):
        mv = p.v(lambda e, h=h: e.tensor_scalar(
            out=TB_K2N[:, 64 * h:64 * h + 64], in0=TB_K2E[:, 64 * h:64 * h + 64],
            scalar1=RCB[:, h:h + 1], scalar2=None, op0=ALU.mult),
            wait=[("V", mv)])
    for hi in range(2):
        mkt = p.t(lambda e, hi=hi: e.matmul(
            c.PSB[0:1, 512 * hi:512 * hi + 512], lhsT=c.ONES[0:64, 0:1],
            rhs=TB_K2N[:, 512 * hi:512 * hi + 512], start=True, stop=True),
            wait=[("V", mv)] if hi == 0 else [], inc=(hi == 1))
    mv2 = p.v(lambda e: e.tensor_reduce(
        out=CSB, in_=seg3(c.PSB[0:1, 0:1024]), axis=AX.X, op=ALU.max),
        wait=[("T", mkt)])
    mv2 = p.v(lambda e: e.reciprocal(out=CSB, in_=CSB), wait=[("V", mv2)])
    mkt = p.t(lambda e: e.matmul(c.PSA[0:64, 0:16], lhsT=c.ONES[0:1, 0:64], rhs=CSB,
                                 start=True, stop=True),
              wait=[("V", mv2), ("S", mks)], inc=True)
    ev = p.s(lambda e: e.copy(out=RSCB, in_=c.PSA[0:64, 0:16]), wait=[("T", mkt)])
    for h in range(ATT1_H):
        mkt = p.t(lambda e, h=h: e.transpose(
            c.PSA[0:64, 64 * h:64 * h + 64], in_=TB_K2N[:, 64 * h:64 * h + 64],
            identity=c.IDENT[0:64, 0:64]),
            wait=[("S", ev)] if h == 0 else [], inc=(h == ATT1_H - 1))
    ev = p.s(lambda e: e.copy(out=TB_K2NT, in_=c.PSA[0:64, 0:1024]), wait=[("T", mkt)])
    mv = 0
    for h in range(ATT1_H):
        mv = p.v(lambda e, h=h: e.tensor_scalar(
            out=TB_V[:, 64 * h:64 * h + 64], in0=TB_K2NT[:, 64 * h:64 * h + 64],
            scalar1=RSCB[:, h:h + 1], scalar2=None, op0=ALU.mult),
            wait=[("S", ev)] if h == 0 else [])
    v_in = mv
    ev_i = ev

    def batch_pe(dst_ps, lhs_t, rhs_t, first_wait, trans=False):
        mkt = 0
        for h in range(ATT1_H):
            sl = slice(64 * h, 64 * h + 64)
            if trans:
                mkt = p.t(lambda e, sl=sl, dst_ps=dst_ps, lhs_t=lhs_t: e.transpose(
                    dst_ps[0:64, sl], in_=lhs_t[:, sl],
                    identity=c.IDENT[0:64, 0:64]),
                    wait=first_wait if h == 0 else [], inc=(h == ATT1_H - 1))
            else:
                mkt = p.t(lambda e, sl=sl, dst_ps=dst_ps, lhs_t=lhs_t, rhs_t=rhs_t:
                          e.matmul(dst_ps[0:64, sl], lhsT=lhs_t[:, sl],
                                   rhs=rhs_t[:, sl], start=True, stop=True),
                          wait=first_wait if h == 0 else [],
                          inc=(h == ATT1_H - 1))
        return mkt

    for it in range(6):
        mkt = batch_pe(c.PSA, TB_K2NT, TB_V, [("S", ev_i), ("V", v_in)])
        ev_a = p.s(lambda e: e.copy(out=TB_KVS, in_=c.PSA[0:64, 0:1024]),
                   wait=[("T", mkt)])
        v_c = p.v(lambda e: e.tensor_tensor(out=TB_A, in0=EYE_B[0], in1=TB_KVS,
                                            op=ALU.subtract),
                  wait=[("S", ev_a), ("DG5", mk_eyesb)])
        mkt = batch_pe(c.PSB, TB_KVS, None, [("S", ev_a)], trans=True)
        ev_b = p.s(lambda e: e.copy(out=TB_KVT, in_=c.PSB[0:64, 0:1024]),
                   wait=[("T", mkt)])
        mkt = batch_pe(c.PSA, TB_KVT, TB_A, [("S", ev_b), ("V", v_c)])
        v_e = p.v(lambda e: e.tensor_tensor(out=TB_A, in0=EYE_B[1],
                                            in1=c.PSA[0:64, 0:1024],
                                            op=ALU.subtract), wait=[("T", mkt)])
        mkt = batch_pe(c.PSB, TB_V, None, [("S", ev_b)], trans=True)
        ev_vt = p.s(lambda e: e.copy(out=TB_KVS, in_=c.PSB[0:64, 0:1024]),
                    wait=[("T", mkt), ("V", v_c)])
        mkt = batch_pe(c.PSA, TB_KVT, TB_A, [("V", v_e)])
        v_g = p.v(lambda e: e.tensor_tensor(out=TB_A, in0=EYE_B[2],
                                            in1=c.PSA[0:64, 0:1024],
                                            op=ALU.subtract), wait=[("T", mkt)])
        mkt = batch_pe(c.PSB, TB_KVS, TB_A, [("S", ev_vt), ("V", v_g)])
        dst = c.INV2[0:64, 0:1024] if it == 5 else TB_V
        ev_i = p.s(lambda e, dst=dst: e.activation(
            out=dst, in_=c.PSB[0:64, 0:1024], func=AF.Copy, scale=0.25),
            wait=[("T", mkt)])
        v_in = v_g
    mk_inv = p.mark("S")

    ub_s = 0
    for h in range(ATT1_H):
        dc, pb = h // 2, 64 * (h % 2)
        for qs in range(QS):
            ps_t = c.PSB if qs < 2 else c.PSC
            p.t(lambda e, dc=dc, pb=pb, qs=qs, ps_t=ps_t: e.matmul(
                ps_t[0:64, 512 * (qs % 2):512 * (qs % 2) + 512],
                lhsT=c.BFA[pb:pb + 64, BFA_QLB + 64 * dc:BFA_QLB + 64 * dc + 64],
                rhs=c.BFA[pb:pb + 64, BFA_KT1 + 2048 * dc + 512 * qs:
                          BFA_KT1 + 2048 * dc + 512 * qs + 512],
                start=True, stop=True),
                wait=([("S", p.mark("S")), ("V", mk_negm_v)] if qs == 0 else []),
                inc=(qs == 3))
        mkt = p.mark("T")
        mks = p.s(lambda e, h=h: e.activation(
            out=bfa(c, BFA_E3, 64, 1024), in_=c.PSB[0:64, :], func=AF.Exp,
            bias=NEGM[:, h:h + 1], accum_out=ZPART[:, 0:1]), wait=[("T", mkt)])
        mks = p.s(lambda e, h=h: e.activation(
            out=bfa(c, BFA_E3 + 1024, 64, 1024), in_=c.PSC[0:64, :], func=AF.Exp,
            bias=NEGM[:, h:h + 1], accum_out=ZPART[:, 1:2]))
        mv = p.v(lambda e, h=h: e.tensor_tensor(out=Z3[:, h:h + 1], in0=ZPART[:, 0:1],
                                                in1=ZPART[:, 1:2], op=ALU.add),
                 wait=[("S", mks)])
        tr_s = [mks, mks]
        for grp in range(2):
            po = 1024 * grp
            mkt = 0
            for j in range(8):
                s16 = 8 * grp + j
                mkt = p.t(lambda e, s16=s16, po=po, j=j: e.transpose(
                    c.PSE[:, po + 64 * j:po + 64 * j + 64],
                    in_=c.BFA[0:64, BFA_E3 + 128 * s16:BFA_E3 + 128 * s16 + 128],
                    identity=c.IDENTB[0:64, 0:64]),
                    wait=[("S", tr_s[grp])] if j == 0 else [], inc=(j == 7))
            tr_s[grp] = p.s(lambda e, grp=grp, po=po: e.copy(
                out=bfa(c, BFA_E3T + 512 * grp, 128, 512),
                in_=c.PSE[:, po:po + 512]),
                wait=[("T", mkt)])
        mk_e3t = p.mark("S")
        for s16 in range(16):
            p.t(lambda e, s16=s16, h=h: e.matmul(
                c.PSA[0:64, 0:64], lhsT=bfa(c, BFA_E3T + 64 * s16, 128, 64),
                rhs=bfa(c, BFA_V1 + 1024 * s16 + 64 * h, 128, 64),
                start=(s16 == 0), stop=(s16 == 15)),
                wait=([("S", max(mk_e3t, ub_s)), ("DY", mk_v1)] if s16 == 0 else []),
                inc=(s16 == 15))
        mkt = p.mark("T")
        ub_s = p.s(lambda e, h=h: e.copy(out=c.UBUF[0:64, 64 * h:64 * h + 64],
                                         in_=c.PSA[0:64, 0:64]), wait=[("T", mkt)])
    mk_uz_v = p.mark("V")
    p.y(lambda e: e.dma_start(out=c.dram["uz_in"].ap()[:, 0:1024], in_=c.UBUF[:, :]),
        wait=[("S", ub_s), ("V", mk_uz_v)])
    p.y(lambda e: e.dma_start(out=c.dram["uz_in"].ap()[:, 1024:1040], in_=Z3))
    mk_uzin = p.mark("DY")
    p.emit("g", lambda e: e.collective_compute(
        "AllReduce", ALU.add, replica_groups=GROUPS_RG,
        ins=[c.dram["uz_in"].ap().opt()], outs=[c.dram["uz_out"].ap().opt()]),
        wait=[("DY", mk_uzin)], inc=[("CC3", 1)])
    p.y(lambda e: e.dma_start(out=c.UBUF[:, :], in_=c.dram["uz_out"].ap()[:, 0:1024]),
        wait=[("CC3", 1)])
    p.y(lambda e: e.dma_start(out=Z3, in_=c.dram["uz_out"].ap()[:, 1024:1040]))
    mk_uzout = p.mark("DY")
    mv = p.v(lambda e: e.reciprocal(out=Z3R, in_=Z3), wait=[("DY", mk_uzout)])
    for h in range(ATT1_H):
        mv = p.v(lambda e, h=h: e.tensor_scalar(
            out=c.UBUF[0:64, 64 * h:64 * h + 64],
            in0=c.UBUF[0:64, 64 * h:64 * h + 64],
            scalar1=Z3R[:, h:h + 1], scalar2=None, op0=ALU.mult), wait=[("V", mv)])
    mk_k3v = mv

    convb_mk = [0, 0]
    pc_s = p.mark("S")
    pc_v = p.mark("V")
    for h in range(ATT1_H):
        dc, pb = h // 2, 64 * (h % 2)
        cbuf = h % 2
        p.g(lambda e, h=h, cbuf=cbuf: e.dma_start(
            out=bfa(c, CONVB[cbuf], 128, 384).rearrange("p (b j) -> p b j", b=3),
            in_=c.din["CONVT"][h].rearrange("b p j -> p b j")),
            wait=[("T", convb_mk[cbuf])], dma="DG5")
        mk_cb = p.mark("DG5")
        NMX16 = CE[:, 256:272]
        RCP16 = CE[:, 288:304]
        K1FB = 51200
        K1NB = 52224
        for tc in range(16):
            p.t(lambda e, dc=dc, pb=pb, tc=tc: e.matmul(
                c.PSA[:, 64 * tc:64 * tc + 64],
                lhsT=c.BFA[pb:pb + 64, BFA_QT1 + 2048 * dc + 128 * tc:
                           BFA_QT1 + 2048 * dc + 128 * tc + 128],
                rhs=c.BFA[pb:pb + 64, BFA_QLB + 512 + 64 * dc:BFA_QLB + 512 + 64 * dc + 64],
                start=True, stop=True),
                wait=([("S", pc_s), ("V", pc_v)] if tc == 0 else []),
                inc=(tc == 15))
        mk_k1mm = p.mark("T")
        mk_exp1 = p.s(lambda e: e.activation(
            out=bfa(c, K1FB, 128, 1024), in_=c.PSA[:, :], func=AF.Exp),
            wait=[("T", mk_k1mm)])
        sum_v = p.v(lambda e: e.tensor_reduce(
            out=RCP16, in_=bfa(c, K1FB, 128, 1024).rearrange(
                "p (s j) -> p s j", s=16),
            axis=AX.X, op=ALU.add),
            wait=[("S", mk_exp1)])
        sum_v = p.v(lambda e: e.reciprocal(out=RCP16, in_=RCP16), wait=[("V", sum_v)])
        mult_vs = []
        for tc in range(16):
            sum_v = p.v(lambda e, tc=tc: e.tensor_scalar(
                out=bfa(c, K1NB + 64 * tc, 128, 64),
                in0=bfa(c, K1FB + 64 * tc, 128, 64),
                scalar1=RCP16[:, tc:tc + 1], scalar2=None, op0=ALU.mult),
                wait=[("V", sum_v)])
            mult_vs.append(sum_v)
        k1t_s = [pc_s, pc_s]
        for grp in range(2):
            pt = 1024 * grp
            mkt = 0
            for j in range(8):
                tc = 8 * grp + j
                mkt = p.t(lambda e, tc=tc, pt=pt, j=j: e.transpose(
                    c.PSE[0:64, pt + 128 * j:pt + 128 * j + 128],
                    in_=bfa(c, K1NB + 64 * tc, 128, 64),
                    identity=c.IDENTB[:, :]),
                    wait=[("V", mult_vs[tc])] + ([("S", k1t_s[grp])] if j == 0 else []),
                    inc=(j == 7))
            k1t_s[grp] = p.s(lambda e, grp=grp, pt=pt: e.copy(
                out=c.K1T[0:64, 1024 * grp:1024 * grp + 1024],
                in_=c.PSE[0:64, pt:pt + 1024]),
                wait=[("T", mkt)])
        mk_k1t = p.mark("S")
        for qs in range(QS):
            ps_t = c.PSB if qs < 2 else c.PSC
            p.t(lambda e, h=h, qs=qs, ps_t=ps_t: e.matmul(
                ps_t[0:64, 512 * (qs % 2):512 * (qs % 2) + 512],
                lhsT=c.INV2[0:64, 64 * h:64 * h + 64],
                rhs=c.K1T[0:64, 512 * qs:512 * qs + 512], start=True, stop=True),
                wait=([("S", max(mk_k1t, mk_inv)), ("V", pc_v)] if qs == 0 else []),
                inc=(qs == 3))
        mkt = p.mark("T")
        p.s(lambda e: e.copy(out=c.ATTNT[0:64, 0:1024], in_=c.PSB[0:64, :]),
            wait=[("T", mkt)])
        mks = p.s(lambda e: e.copy(out=c.ATTNT[0:64, 1024:2048], in_=c.PSC[0:64, :]))
        mk_at = p.mark("S")
        for qs in range(QS):
            ps_t = c.PSB if qs < 2 else c.PSC
            p.t(lambda e, h=h, pb=pb, qs=qs, ps_t=ps_t: e.matmul(
                ps_t[pb:pb + 64, 512 * (qs % 2):512 * (qs % 2) + 512],
                lhsT=c.UBUF[0:64, 64 * h:64 * h + 64],
                rhs=c.ATTNT[0:64, 512 * qs:512 * qs + 512], start=True, stop=False),
                wait=([("S", mk_at), ("V", max(mk_k3v, pc_v))] if qs == 0 else []))
        for tc in range(16):
            ps_t = c.PSB if tc < 8 else c.PSC
            pco = 512 * ((tc // 4) % 2) + 128 * (tc % 4)
            for b, delta in enumerate((-1, 0, 1)):
                src = tc + delta
                if src < 0:
                    lhs = bfa(c, HALO_L + 64 * h, 128, 64)
                elif src > 15:
                    lhs = bfa(c, HALO_R + 64 * h, 128, 64)
                else:
                    lhs = bfa(c, BFA_V1 + 1024 * src + 64 * h, 128, 64)
                p.t(lambda e, lhs=lhs, b=b, cbuf=cbuf, pb=pb, ps_t=ps_t, pco=pco: e.matmul(
                    ps_t[pb:pb + 64, pco:pco + 128], lhsT=lhs,
                    rhs=bfa(c, CONVB[cbuf] + 128 * b, 128, 128),
                    start=False, stop=(b == 2)),
                    wait=([("DG5", max(mk_cb, mk_halo)), ("DY", mk_v1)]
                          if (tc == 0 and b == 0) else []),
                    inc=(tc == 15 and b == 2))
        mkt = p.mark("T")
        if h == ATT1_H - 1:
            convb_mk[cbuf] = mkt
        else:
            convb_mk[cbuf] = mkt
        for qs in range(QS):
            ps_t = c.PSB if qs < 2 else c.PSC
            mv = p.v(lambda e, dc=dc, pb=pb, qs=qs, ps_t=ps_t: e.tensor_tensor(
                out=c.X[dc][pb:pb + 64, 512 * qs:512 * qs + 512],
                in0=c.X[dc][pb:pb + 64, 512 * qs:512 * qs + 512],
                in1=ps_t[pb:pb + 64, 512 * (qs % 2):512 * (qs % 2) + 512],
                op=ALU.add), wait=[("T", mkt)])
        pc_v = p.mark("V")
        pc_s = p.mark("S")
    c.mk_xb_free = max(c.mk_xb_free, p.mark("T"))


def host_prep(inputs, sublayers=(0, 1, 2, 3, 4, 5)):
    rep = {}
    consts = np.zeros((128, 384), dtype=np.float32)
    consts[:, 0:128] = 1.0
    consts[:, 128:256] = np.eye(128, dtype=np.float32)
    consts[:, 256:264] = -1e30
    consts[:, 264] = 1e-5
    consts[:, 265] = -ATT0_SHIFT
    rep["CONSTS"] = consts
    need_ffn0 = 0 in sublayers or 2 in sublayers
    need_attn0 = 1 in sublayers
    need_moe = 3 in sublayers or 5 in sublayers
    need_attn1 = 4 in sublayers
    if need_ffn0:
        w = np.asarray(inputs["L0_ffn_weight"], dtype=np.float32)
        rep["W0T"] = np.ascontiguousarray(w.T).reshape(NC, 128, D)
        rep["DISHA"] = np.asarray(inputs["L0_ffn_disha"], dtype=np.float32)
        ssel = np.zeros((128, 64), dtype=np.float32)
        for pp in range(128):
            ssel[pp, pp % 64] = 1.0
        rep["SSEL"] = ssel
    if need_attn0:
        pq = np.asarray(inputs["L0_attn_proj_q"], dtype=np.float32)
        rep["PQ"] = np.ascontiguousarray(pq.reshape(8, NC, 128, 128))
        rep["PK"] = np.ascontiguousarray(
            np.asarray(inputs["L0_attn_proj_k"], dtype=np.float32).reshape(NC, 128, 128))
        rep["PV"] = np.ascontiguousarray(
            np.asarray(inputs["L0_attn_proj_v"], dtype=np.float32).reshape(NC, 128, 128))
        po = np.asarray(inputs["L0_attn_proj_o"], dtype=np.float32)
        rep["POT"] = np.ascontiguousarray(po.transpose(0, 2, 1))
    if need_moe:
        w1 = np.asarray(inputs["L1_ffn_w1"], dtype=np.float32)
        w3 = np.asarray(inputs["L1_ffn_w3"], dtype=np.float32)
        w2 = np.asarray(inputs["L1_ffn_w2"], dtype=np.float32)
        p1 = w1.reshape(EXPERTS, HC, 128, NC, 128).transpose(0, 1, 4, 3, 2).reshape(
            EXPERTS, HC, 128, D)
        p3 = w3.reshape(EXPERTS, HC, 128, NC, 128).transpose(0, 1, 4, 3, 2).reshape(
            EXPERTS, HC, 128, D)
        p2 = w2.reshape(EXPERTS, NC, 128, HC, 128).transpose(0, 3, 4, 1, 2).reshape(
            EXPERTS, HC, 128, D)
        rep["MS"] = np.ascontiguousarray(np.stack([p1, p3, p2], axis=2))
        rep["GWT"] = np.ascontiguousarray(
            np.asarray(inputs["L1_ffn_gate_w"], dtype=np.float32).T.reshape(NC, 128, 8))
    if need_attn1:
        wq = np.asarray(inputs["L1_attn_Wq"], dtype=np.float32)
        wk = np.asarray(inputs["L1_attn_Wk"], dtype=np.float32)
        wv = np.asarray(inputs["L1_attn_Wv"], dtype=np.float32)
        rep["WQJ"] = np.ascontiguousarray(wq.reshape(NC, 128, NC, 128).transpose(2, 0, 1, 3))
        rep["WKJ"] = np.ascontiguousarray(wk.reshape(NC, 128, NC, 128).transpose(2, 0, 1, 3))
        rep["WV"] = np.ascontiguousarray(wv.reshape(NC, 128, D))
        rep["BQ"] = np.ascontiguousarray(
            np.asarray(inputs["L1_attn_bq"], dtype=np.float32).reshape(8, 128).T)
        rep["BK"] = np.ascontiguousarray(
            np.asarray(inputs["L1_attn_bk"], dtype=np.float32).reshape(8, 128).T)
        rep["BV"] = np.asarray(inputs["L1_attn_bv"], dtype=np.float32).reshape(1, D)
        cw = np.asarray(inputs["L1_attn_conv_w"], dtype=np.float32).reshape(ATT1_H, 65)
        convt = np.zeros((ATT1_H, 3, 128, 128), dtype=np.float32)
        jj = np.arange(128)
        for h in range(ATT1_H):
            for b, delta in enumerate((-1, 0, 1)):
                for pp in range(128):
                    t = delta * 128 + pp - jj + 32
                    m = (t >= 0) & (t < 65)
                    convt[h, b, pp, m] = cw[h, t[m]]
        rep["CONVT"] = convt
        eyes = np.zeros((64, 3072), dtype=np.float32)
        for k, v in enumerate((7.0, 15.0, 13.0)):
            for h in range(ATT1_H):
                eyes[:, 1024 * k + 64 * h:1024 * k + 64 * h + 64] = v * np.eye(64)
        rep["EYES"] = eyes

    x = np.asarray(inputs["L0_attn_x"], dtype=np.float32)
    per_core = []
    for core in range(8):
        g, r = core // 4, core % 4
        d = dict(rep)
        d["xin"] = np.ascontiguousarray(x[g, Q * r:Q * (r + 1), :].T)
        if need_attn1:
            hi = np.full((128, 2), 99999, dtype=np.int32)
            if r > 0:
                hi[:, 0] = 256 * (r - 1) + 128 + np.arange(128)
            if r < 3:
                hi[:, 1] = 256 * (r + 1) + np.arange(128)
            d["HALOIDX"] = hi
        per_core.append(d)
    return per_core


def run_launch(sublayers, per_core_inputs):
    nc = build_program(tuple(sublayers))
    res = run_bass_kernel_spmd(nc, per_core_inputs, core_ids=list(range(8)))
    return [res.results[cc]["xout"] for cc in range(8)]


LAUNCH_GROUPS = [(0, 1, 2, 3, 4, 5)]


def kernel(**inputs):
    x = np.asarray(inputs["L0_attn_x"], dtype=np.float32)
    cur = [None] * 8
    for gi, grp in enumerate(LAUNCH_GROUPS):
        per_core = host_prep(inputs, grp)
        if gi > 0:
            for core in range(8):
                per_core[core]["xin"] = cur[core]
        outs = run_launch(grp, per_core)
        cur = [np.ascontiguousarray(o) for o in outs]
    y = np.empty((2, S_FULL, D), dtype=np.float32)
    for core in range(8):
        g, r = core // 4, core % 4
        y[g, Q * r:Q * (r + 1), :] = cur[core].T
    return y


# version 15
# speedup vs baseline: 1.0570x; 1.0091x over previous
import numpy as np
import ml_dtypes
from contextlib import ExitStack
import concourse.bass as bass
import concourse.mybir as mybir
from concourse.bass_utils import run_bass_kernel_spmd

F32 = mybir.dt.float32
BF16 = mybir.dt.bfloat16
AF = mybir.ActivationFunctionType
ALU = mybir.AluOpType
AX = mybir.AxisListType

Q = 2048
D = 1024
NC = 8
QS = 4
EXPERTS = 8
HID = 2816
HC = 22
ATT0_SHIFT = 40.0
ATT1_H = 16
S_FULL = 8192
GROUPS_RG = [[0, 1, 2, 3], [4, 5, 6, 7]]

ENGS = ("t", "v", "s", "g", "y")
DEBUG_TAPS = []
DEBUG_PQDUMP = False
SERIAL_DMA = True


class Plan:

    def __init__(self):
        self.ops = {e: [] for e in ENGS}
        self.cnt = {}
        self.waited = {e: {} for e in ENGS}

    def mark(self, sem):
        return self.cnt.get(sem, 0)

    def emit(self, eng, fn=None, wait=(), inc=()):
        w2 = []
        for s, v in wait:
            if v <= 0:
                continue
            assert v <= self.cnt.get(s, 0), \
                f"wait {s}>={v} exceeds planned {self.cnt.get(s, 0)}"
            if v > self.waited[eng].get(s, 0):
                self.waited[eng][s] = v
                w2.append((s, v))
        incs = tuple(inc)
        assert len(incs) <= 1
        for s, a in incs:
            self.cnt[s] = self.cnt.get(s, 0) + a
        self.ops[eng].append((fn, tuple(w2), incs))
        return self.cnt.get(incs[0][0], None) if incs else None

    def t(self, fn, wait=(), inc=False):
        return self.emit("t", fn, wait, [("T", 1)] if inc else [])

    def v(self, fn, wait=(), inc=True):
        return self.emit("v", fn, wait, [("V", 1)] if inc else [])

    def s(self, fn, wait=(), inc=True):
        return self.emit("s", fn, wait, [("S", 1)] if inc else [])

    def g(self, fn, wait=(), inc=False, dma=False):
        if dma:
            sem = "DG" if dma is True else dma
            w = list(wait) + ([(sem, self.cnt.get(sem, 0))] if SERIAL_DMA else [])
            return self.emit("g", fn, w, [(sem, 16)])
        return self.emit("g", fn, wait, [("G", 1)] if inc else [])

    def y(self, fn, wait=(), dma=True):
        if dma:
            w = list(wait) + ([("DY", self.cnt.get("DY", 0))] if SERIAL_DMA else [])
            return self.emit("y", fn, w, [("DY", 16)])
        return self.emit("y", fn, wait, [])


class Ctx:
    pass


MAP_BASE = 18944
OFF_X = MAP_BASE
OFF_FSCR = MAP_BASE + 65536
FS_SQ = OFF_FSCR
FS_TMP = OFF_FSCR + 4096
FS_RB = OFF_FSCR + 8192
FS_MRB = OFF_FSCR + 12288
FS_ACC = FS_SQ
FS_ROWS = OFF_FSCR + 16384
OFF_BFA = OFF_FSCR + 24576
OFF_CONST = OFF_BFA + 114688
TOTAL_LIMIT = 229376

BFA_XB = 0
BFA_PL = 16384
BFA_DISHA = 18432
BFA_SSEL = 19456
BFA_QT0 = 16384
BFA_KTF = 32768
BFA_VF = 40960
BFA_OT = 0
BFA_E0 = 8192
BFA_WS2 = 10240
BFA_KVSTAGE = 32768
BFA_PKPV = 36864
BFA_W0T = 49152
BFA_HB = 16384
BFA_W13 = 38912
BFA_W2R = 43008
BFA_GEB = 54272
BFA_MTMP = 56320
BFA_WVR = 24576
BFA_VSTG = 22528
BFA_QT1 = 16384
BFA_KT1 = 32768
BFA_V1 = 0
BFA_HALO = 49152
BFA_E3 = 51200
BFA_E3T = 53248
BFA_QLB = 54272
BFA_CONV = 54784
BFA_WS5 = 55552
BFA_BVB = 23552


def bfa(c, elem_off, parts, width):
    return c.BFA[0:parts, elem_off:elem_off + width]


def XBc(c, dc, sl=None):
    if sl is None:
        return bfa(c, BFA_XB + 2048 * dc, 128, Q)
    return c.BFA[0:128, BFA_XB + 2048 * dc + sl.start:BFA_XB + 2048 * dc + sl.stop]


def _sem_names():
    return ["T", "V", "S", "G", "DY", "DG", "DG2", "DG3", "DG4", "DG5"] + [f"CC{i}" for i in range(8)]


def build_program(sublayers=(0, 1, 2, 3, 4, 5)):
    nc = bass.Bass("TRN2", target_bir_lowering=False, debug=False)
    p = Plan()
    c = Ctx()
    c.nc = nc
    c.p = p
    c.mk_xb_free = 0

    need_ffn0 = 0 in sublayers or 2 in sublayers
    need_attn0 = 1 in sublayers
    need_moe = 3 in sublayers or 5 in sublayers
    need_attn1 = 4 in sublayers

    din = {}

    def DIN(name, shape):
        din[name] = nc.dram_tensor(name, list(shape), F32, kind="ExternalInput").ap()
        return din[name]

    DIN("xin", (D, Q))
    c.xout_ap = nc.dram_tensor("xout", [D, Q], F32, kind="ExternalOutput").ap()
    DIN("CONSTS", (128, 384))
    if need_ffn0:
        DIN("W0T", (NC, 128, D))
        DIN("DISHA", (64, D))
        DIN("SSEL", (128, 64))
    if need_attn0:
        if DEBUG_PQDUMP:
            t = nc.dram_tensor("dbg_pq", [8, 128, 1024], BF16, kind="ExternalOutput")
            c.dbg_pq = t.ap()
        DIN("PQ", (8, NC, 128, 128))
        DIN("PK", (NC, 128, 128))
        DIN("PV", (NC, 128, 128))
        DIN("POT", (8, 128, D))
    if need_moe:
        DIN("MS", (EXPERTS, HC, 3, 128, D))
        DIN("GWT", (NC, 128, 8))
    if need_attn1:
        DIN("WQJ", (NC, NC, 128, 128))
        DIN("WKJ", (NC, NC, 128, 128))
        DIN("WV", (NC, 128, D))
        DIN("BQ", (128, 8))
        DIN("BK", (128, 8))
        DIN("BV", (1, D))
        DIN("CONVT", (ATT1_H, 3, 128, 128))
        DIN("EYES", (64, 3072))
        din["HALOIDX"] = nc.dram_tensor("HALOIDX", [128, 2], mybir.dt.int32,
                                        kind="ExternalInput").ap()
    c.din = din

    c.dram = {}
    if need_attn0:
        c.dram["pqb"] = nc.dram_tensor("pqb", [8, 128, 1024], BF16, kind="Internal")
        c.dram["kv_in"] = nc.dram_tensor("kv_in", [128, 4096], BF16, kind="Internal")
        c.dram["kv_out"] = nc.dram_tensor("kv_out", [512, 4096], BF16, kind="Internal")
    if need_attn1:
        c.dram["v_dram"] = nc.dram_tensor("v_dram", [16, 128, D], BF16, kind="Internal")
        c.dram["lm_in"] = nc.dram_tensor("lm_in", [2048, 16], F32, kind="Internal")
        c.dram["lm_out"] = nc.dram_tensor("lm_out", [8192, 16], F32, kind="Internal")
        c.dram["mx_in"] = nc.dram_tensor("mx_in", [64, 16], F32, kind="Internal")
        c.dram["mx_out"] = nc.dram_tensor("mx_out", [64, 16], F32, kind="Internal")
        c.dram["uz_in"] = nc.dram_tensor("uz_in", [64, 1040], F32, kind="Internal")
        c.dram["uz_out"] = nc.dram_tensor("uz_out", [64, 1040], F32, kind="Internal")
        c.dram["vh_in"] = nc.dram_tensor("vh_in", [256, D], BF16, kind="Internal")
        c.dram["vh_out"] = nc.dram_tensor("vh_out", [1024, D], BF16, kind="Internal")

    with ExitStack() as stack:
        def sb_at(name, shape, dt, off):
            return nc.alloc_sbuf_tensor_at(name, list(shape), dt, offset=off)

        c.X = [sb_at(f"X{dc}", [128, Q], F32, OFF_X + 8192 * dc) for dc in range(NC)]
        c.fs_sq = sb_at("fs_sq", [128, 1024], F32, FS_SQ)
        c.fs_tmp = sb_at("fs_tmp", [128, 1024], F32, FS_TMP)
        c.fs_rb = sb_at("fs_rb", [128, 1024], F32, FS_RB)
        c.fs_mrb = sb_at("fs_mrb", [128, 1024], F32, FS_MRB)
        c.fs_acc = sb_at("fs_acc", [128, 1024], F32, FS_ACC)
        c.fs_rows = sb_at("fs_rows", [1, 2048], F32, FS_ROWS)
        c.BFA = sb_at("BFA", [128, 57344], BF16, OFF_BFA)
        c.ONES = sb_at("ONES", [128, 128], F32, OFF_CONST)
        c.IDENT = sb_at("IDENT", [128, 128], F32, OFF_CONST + 512)
        c.IDENTB = sb_at("IDENTB", [128, 128], BF16, OFF_CONST + 1024)
        c.NEG8 = sb_at("NEG8", [128, 8], F32, OFF_CONST + 1280)
        c.CBIAS = sb_at("CBIAS", [128, 8], F32, OFF_CONST + 1312)
        c.CEXTRA = sb_at("CEXTRA", [128, 624], F32, OFF_CONST + 1344)
        assert OFF_CONST + 1344 + 2496 <= TOTAL_LIMIT

        c.QLF = sb_at("QLF", [128, 1024], F32, FS_SQ)
        c.EYES_T = sb_at("EYES_T", [64, 192], F32, OFF_CONST + 1344 + 2496 + 32)
        c.SC64 = sb_at("SC64", [64, 1024], F32, FS_TMP)
        c.K1T = sb_at("K1T", [64, 2048], F32, FS_RB)
        c.QKL = sb_at("QKL", [128, 512], F32, FS_RB)
        c.QKL2 = sb_at("QKL2", [128, 512], F32, FS_RB + 2048)
        c.INV2 = sb_at("INV2", [64, 1024], F32, FS_ROWS)
        c.UBUF = sb_at("UBUF", [64, 1024], F32, FS_ROWS + 4096)
        c.ATTNT = sb_at("ATTNT", [64, 2048], F32, FS_SQ)
        c.HIDX = nc.alloc_sbuf_tensor_at("HIDX", [128, 2], mybir.dt.int32,
                                         offset=OFF_CONST + 1344 + 2496)
        c.PSA = stack.enter_context(nc.psum_tensor("PSA", [128, 1024], F32))
        c.PSB = stack.enter_context(nc.psum_tensor("PSB", [128, 1024], F32))
        c.PSC = stack.enter_context(nc.psum_tensor("PSC", [128, 1024], F32))
        c.PSE = stack.enter_context(nc.psum_tensor("PSE", [128, 2048], BF16))

        sems = {}
        for sn in _sem_names():
            sems[sn] = stack.enter_context(nc.semaphore(sn))
        c.sems = sems
        block = stack.enter_context(nc.Block())

        plan_all(c, sublayers)

        def runner(eng_key):
            def run(eng):
                for fn, waits, incs in p.ops[eng_key]:
                    for s, v in waits:
                        eng.wait_ge(sems[s], v)
                    inst = fn(eng) if fn is not None else None
                    for s, a in incs:
                        if inst is None:
                            inst = eng.nop()
                        inst.then_inc(sems[s], a)
            return run

        block.tensor(runner("t"))
        block.vector(runner("v"))
        block.scalar(runner("s"))
        block.gpsimd(runner("g"))
        block.sync(runner("y"))

    return nc


def plan_all(c, sublayers):
    p = c.p
    CON = c.din["CONSTS"]
    p.y(lambda e: e.dma_start(out=c.ONES[:, :], in_=CON[:, 0:128]))
    p.y(lambda e: e.dma_start(out=c.IDENT[:, :], in_=CON[:, 128:256]))
    p.g(lambda e: e.dma_start(out=c.IDENTB[:, :], in_=CON[:, 128:256]), dma=True)
    p.y(lambda e: e.dma_start(out=c.NEG8[:, :], in_=CON[:, 256:264]))
    p.y(lambda e: e.dma_start(out=c.CBIAS[:, 0:4], in_=CON[:, 264:268]))
    xin = c.din["xin"]
    for dc in range(NC):
        p.y(lambda e, dc=dc: e.dma_start(out=c.X[dc][:, :],
                                         in_=xin[128 * dc:128 * (dc + 1), :]))
    c.mk_ld = p.mark("DY")
    c.mk_ldg = p.mark("DG")

    if 0 in sublayers or 2 in sublayers:
        W0T = c.din["W0T"]
        for kc in range(NC):
            p.g(lambda e, kc=kc: e.dma_start(out=bfa(c, BFA_W0T + 1024 * kc, 128, 1024),
                                             in_=W0T[kc]), dma=True)
    c.mk_w0 = p.mark("DG")

    first = True
    for sl in sublayers:
        if sl in (0, 2):
            emit_ln(c, half=True, first=first)
            emit_ffn0(c)
        elif sl == 1:
            emit_ln(c, half=False, first=first)
            emit_attn0(c)
        elif sl in (3, 5):
            emit_ln(c, half=False, first=first)
            emit_moe(c)
        elif sl == 4:
            emit_ln(c, half=False, first=first)
            emit_attn1(c)
        first = False

    mkv = p.mark("V")
    for dc in range(NC):
        p.y(lambda e, dc=dc: e.dma_start(
            out=c.xout_ap[128 * dc:128 * (dc + 1), :], in_=c.X[dc][:, :]),
            wait=[("V", mkv)])
    for name, ap_fn, shape in DEBUG_TAPS:
        dout = c.nc.dram_tensor("dbg_" + name, list(shape), ap_fn(c).dtype,
                                kind="ExternalOutput").ap()
        p.y(lambda e, dout=dout, ap_fn=ap_fn: e.dma_start(out=dout, in_=ap_fn(c)),
            wait=[("V", mkv), ("T", p.mark("T")), ("S", p.mark("S"))])
    p.emit("y", None, wait=[("DY", p.mark("DY"))])


def emit_ln(c, half, first=False):
    p = c.p
    mkv_entry = p.mark("V")
    ROW_M = c.fs_rows[0:1, 0:1024]
    ROW_S = c.fs_rows[0:1, 1024:2048]
    ROW_T = c.fs_tmp[0:1, 0:1024]
    wait_ld = [("DY", c.mk_ld)] if first else []
    mk_prev_evac = getattr(c, "mk_ln_evac", 0)

    for qt in range(2):
        qlo = 1024 * qt
        mk_sumsq = 0
        for dc in range(NC):
            p.s(lambda e, dc=dc, qlo=qlo: e.activation(
                out=c.fs_sq[:, :], in_=c.X[dc][:, qlo:qlo + 1024], func=AF.Square),
                wait=wait_ld + [("V", mkv_entry), ("T", mk_sumsq)])
            mksq = p.mark("S")
            for hi in range(2):
                sl = slice(512 * hi, 512 * hi + 512)
                p.t(lambda e, dc=dc, qlo=qlo, sl=sl: e.matmul(
                    c.PSC[0:1, sl], lhsT=c.ONES[:, 0:1],
                    rhs=c.X[dc][:, qlo + sl.start:qlo + sl.stop],
                    start=(dc == 0), stop=(dc == NC - 1)),
                    wait=wait_ld + [("V", mkv_entry), ("S", mk_prev_evac)])
                mk_sumsq = p.t(lambda e, dc=dc, sl=sl: e.matmul(
                    c.PSA[0:1, sl], lhsT=c.ONES[:, 0:1], rhs=c.fs_sq[:, sl],
                    start=(dc == 0), stop=(dc == NC - 1)),
                    wait=[("S", mksq)], inc=True)
        mk_stats = p.mark("T")
        p.s(lambda e: e.activation(out=ROW_M, in_=c.PSC[0:1, 0:1024], func=AF.Copy,
                                   scale=1.0 / D), wait=[("T", mk_stats)])
        p.s(lambda e: e.activation(out=ROW_S, in_=c.PSA[0:1, 0:1024], func=AF.Copy,
                                   scale=1.0 / D))
        mks = p.mark("S")
        p.v(lambda e: e.tensor_tensor(out=ROW_T, in0=ROW_M, in1=ROW_M, op=ALU.mult),
            wait=[("S", mks)])
        p.v(lambda e: e.tensor_tensor(out=ROW_S, in0=ROW_S, in1=ROW_T, op=ALU.subtract))
        mkv2 = p.mark("V")
        p.s(lambda e: e.activation(out=ROW_S, in_=ROW_S, func=AF.Ln,
                                   bias=c.CBIAS[0:1, 0:1]),
            wait=[("V", mkv2)])
        p.s(lambda e: e.activation(out=ROW_S, in_=ROW_S, func=AF.Exp, scale=-0.5))
        if half:
            p.s(lambda e: e.activation(out=ROW_S, in_=ROW_S, func=AF.Copy, scale=0.5))
        mks2 = p.mark("S")
        p.v(lambda e: e.tensor_tensor(out=ROW_T, in0=ROW_M, in1=ROW_S, op=ALU.mult),
            wait=[("S", mks2)])
        mkv3 = p.mark("V")
        for hi in range(2):
            sl = slice(512 * hi, 512 * hi + 512)
            p.t(lambda e, sl=sl: e.matmul(c.PSA[:, sl], lhsT=c.ONES[0:1, :],
                                          rhs=ROW_S[:, sl], start=True, stop=True),
                wait=[("V", mkv3), ("S", mks2)])
            p.t(lambda e, sl=sl: e.matmul(c.PSB[:, sl], lhsT=c.ONES[0:1, :],
                                          rhs=ROW_T[:, sl], start=True, stop=True),
                inc=(hi == 1))
        mk_bc = p.mark("T")
        p.s(lambda e: e.copy(out=c.fs_rb[:, :], in_=c.PSA[:, :]), wait=[("T", mk_bc)])
        p.s(lambda e: e.copy(out=c.fs_mrb[:, :], in_=c.PSB[:, :]))
        mk_prev_evac = p.mark("S")
        for dc in range(NC):
            p.v(lambda e, dc=dc, qlo=qlo: e.tensor_tensor(
                out=c.fs_tmp[:, :], in0=c.X[dc][:, qlo:qlo + 1024],
                in1=c.fs_rb[:, :], op=ALU.mult),
                wait=[("S", mk_prev_evac), ("T", c.mk_xb_free)])
            p.v(lambda e, dc=dc, qlo=qlo: e.tensor_tensor(
                out=c.BFA[0:128, BFA_XB + 2048 * dc + qlo:BFA_XB + 2048 * dc + qlo + 1024],
                in0=c.fs_tmp[:, :], in1=c.fs_mrb[:, :], op=ALU.subtract))
    c.mk_ln_evac = mk_prev_evac
    c.mk_ln_done = p.mark("V")


def emit_ffn0(c):
    p = c.p
    mkln = c.mk_ln_done
    p.g(lambda e: e.dma_start(out=bfa(c, BFA_DISHA, 64, 1024), in_=c.din["DISHA"]),
        wait=[("T", c.mk_xb_free)], dma=True)
    p.g(lambda e: e.dma_start(out=bfa(c, BFA_SSEL, 128, 64), in_=c.din["SSEL"]),
        dma=True)
    mk_small = p.mark("DG")
    DISHAB = bfa(c, BFA_DISHA, 64, 1024)
    SSELB = bfa(c, BFA_SSEL, 128, 64)

    for qs in range(QS):
        sl = slice(512 * qs, 512 * (qs + 1))
        psl = slice(512 * (qs % 2), 512 * (qs % 2) + 512)
        for kc in range(NC):
            p.t(lambda e, kc=kc, sl=sl, psl=psl: e.matmul(
                c.PSC[0:64, psl], lhsT=SSELB, rhs=XBc(c, kc, sl),
                start=(kc == 0), stop=(kc == NC - 1)),
                wait=[("V", mkln), ("DG", max(c.mk_w0, mk_small)),
                      ("S", p.mark("S"))],
                inc=(kc == NC - 1))
        mkt = p.mark("T")
        p.s(lambda e, sl=sl, psl=psl: e.copy(
            out=c.BFA[0:64, BFA_PL + sl.start:BFA_PL + sl.stop],
            in_=c.PSC[0:64, psl]), wait=[("T", mkt)])
    mk_pool = p.mark("S")

    slot_free = [0, 0, 0, 0]
    sidx = 0
    for oc in range(NC):
        for qs in range(QS):
            sl = slice(512 * qs, 512 * (qs + 1))
            ps_t = c.PSA if (sidx % 4) < 2 else c.PSB
            psl = slice(512 * (sidx % 2), 512 * (sidx % 2) + 512)
            for kc in range(NC):
                p.t(lambda e, kc=kc, oc=oc, sl=sl, ps_t=ps_t, psl=psl: e.matmul(
                    ps_t[:, psl],
                    lhsT=bfa(c, BFA_W0T + 1024 * kc + 128 * oc, 128, 128),
                    rhs=XBc(c, kc, sl), start=(kc == 0), stop=False),
                    wait=[("V", slot_free[sidx % 4])] if kc == 0 else [])
            p.t(lambda e, oc=oc, sl=sl, psl=psl, ps_t=ps_t: e.matmul(
                ps_t[:, psl], lhsT=DISHAB[:, 128 * oc:128 * (oc + 1)],
                rhs=c.BFA[0:64, BFA_PL + sl.start:BFA_PL + sl.stop],
                start=False, stop=True),
                wait=[("S", mk_pool)], inc=True)
            mkt = p.mark("T")
            slot_free[sidx % 4] = p.v(
                lambda e, oc=oc, sl=sl, ps_t=ps_t, psl=psl: e.tensor_tensor(
                    out=c.X[oc][:, sl], in0=c.X[oc][:, sl], in1=ps_t[:, psl],
                    op=ALU.add),
                wait=[("T", mkt)])
            sidx += 1
    c.mk_xb_free = p.mark("T")


def emit_attn0(c):
    p = c.p
    mkln = c.mk_ln_done
    PQ = c.din["PQ"]
    POT = c.din["POT"]

    PKB = 36864
    PVB = 37888
    PQBUF = 38912
    POTBUF = 12288
    STG_K = 32768
    STG_V = 34816
    for h0 in range(8):
        p.g(lambda e, h0=h0: e.dma_start(
            out=c.dram["pqb"].ap()[h0].rearrange("p (k j) -> p k j", k=8),
            in_=PQ[h0].rearrange("k p j -> p k j")), dma="DG2")
    mk_pqcast = p.mark("DG2")
    p.g(lambda e: e.dma_start(out=bfa(c, PKB, 128, 1024).rearrange("p (k j) -> p k j", k=8),
                              in_=c.din["PK"].rearrange("k p j -> p k j")), dma=True)
    p.g(lambda e: e.dma_start(out=bfa(c, PVB, 128, 1024).rearrange("p (k j) -> p k j", k=8),
                              in_=c.din["PV"].rearrange("k p j -> p k j")), dma=True)
    mk_pkv = p.mark("DG")

    for qs in range(QS):
        sl = slice(512 * qs, 512 * (qs + 1))
        ps_t = c.PSA if qs % 2 == 0 else c.PSB
        psl = slice(512, 1024)
        for kc in range(NC):
            p.t(lambda e, kc=kc, sl=sl, ps_t=ps_t, psl=psl: e.matmul(
                ps_t[:, psl], lhsT=bfa(c, PKB + 128 * kc, 128, 128),
                rhs=XBc(c, kc, sl), start=(kc == 0), stop=(kc == NC - 1)),
                wait=([("DG", mk_pkv), ("S", p.mark("S")), ("V", mkln)]
                      if kc == 0 else []),
                inc=(kc == NC - 1))
        mkt = p.mark("T")
        p.s(lambda e, sl=sl, ps_t=ps_t, psl=psl: e.copy(
            out=c.BFA[0:128, STG_K + sl.start:STG_K + sl.stop], in_=ps_t[:, psl]),
            wait=[("T", mkt)])

    for tc in range(16):
        tsl = slice(128 * tc, 128 * (tc + 1))
        psl = slice(512 * (tc % 2), 512 * (tc % 2) + 128)
        for kc in range(NC):
            p.t(lambda e, kc=kc, tsl=tsl, psl=psl: e.matmul(
                c.PSC[:, psl], lhsT=XBc(c, kc, tsl),
                rhs=bfa(c, PVB + 128 * kc, 128, 128),
                start=(kc == 0), stop=(kc == NC - 1)),
                wait=[("S", p.mark("S"))] if kc == 0 else [],
                inc=(kc == NC - 1))
        mkt = p.mark("T")
        p.s(lambda e, tc=tc, psl=psl: e.copy(
            out=bfa(c, STG_V + 128 * tc, 128, 128), in_=c.PSC[:, psl]),
            wait=[("T", mkt)])
    c.mk_xb_free = p.mark("T")
    mk_stage = p.mark("S")

    kv_in = c.dram["kv_in"]
    kv_out = c.dram["kv_out"]
    p.y(lambda e: e.dma_start(out=kv_in.ap(), in_=bfa(c, STG_K, 128, 4096)),
        wait=[("S", mk_stage)])
    mk_kvout = p.mark("DY")
    p.emit("g", lambda e: e.collective_compute(
        "AllGather", ALU.bypass, replica_groups=GROUPS_RG,
        ins=[kv_in.ap().opt()], outs=[kv_out.ap().opt()]),
        wait=[("DY", mk_kvout)], inc=[("CC0", 1)])
    pq_mk = [0, 0]
    qt_mks = []
    for h in range(8):
        buf = h % 2
        p.y(lambda e, h=h, buf=buf: e.dma_start(
            out=bfa(c, PQBUF + 1024 * buf, 128, 1024),
            in_=c.dram["pqb"].ap()[h]),
            wait=[("T", pq_mk[buf]), ("DG2", mk_pqcast)])
        mk_dma = p.mark("DY")
        for qs in range(QS):
            sl = slice(512 * qs, 512 * (qs + 1))
            ps_t = c.PSA if qs % 2 == 0 else c.PSB
            psl = slice(0, 512)
            for kc in range(NC):
                p.t(lambda e, h=h, buf=buf, kc=kc, sl=sl, ps_t=ps_t, psl=psl: e.matmul(
                    ps_t[:, psl], lhsT=bfa(c, PQBUF + 1024 * buf + 128 * kc, 128, 128),
                    rhs=XBc(c, kc, sl), start=(kc == 0), stop=(kc == NC - 1)),
                    wait=([("V", mkln), ("DY", mk_dma), ("S", p.mark("S"))]
                          if kc == 0 else []),
                    inc=(kc == NC - 1))
            mkt = p.mark("T")
            if qs == QS - 1:
                pq_mk[buf] = mkt
                if DEBUG_PQDUMP:
                    p.y(lambda e, h=h, buf=buf: e.dma_start(
                        out=c.dbg_pq[h], in_=bfa(c, PQBUF + 1024 * buf, 128, 1024)),
                        wait=[("T", mkt)])
            p.s(lambda e, h=h, sl=sl, ps_t=ps_t, psl=psl: e.copy(
                out=c.BFA[0:128, BFA_QT0 + 2048 * h + sl.start:BFA_QT0 + 2048 * h + sl.stop],
                in_=ps_t[:, psl]), wait=[("T", mkt)])

    c.mk_xb_free = p.mark("T")
    for r in range(4):
        p.y(lambda e, r=r: e.dma_start(
            out=bfa(c, BFA_KTF + 2048 * r, 128, 2048),
            in_=kv_out.ap()[128 * r:128 * (r + 1), 0:2048]),
            wait=[("CC0", 1), ("DY", mk_kvout), ("T", c.mk_xb_free)])
        p.y(lambda e, r=r: e.dma_start(
            out=bfa(c, BFA_VF + 2048 * r, 128, 2048),
            in_=kv_out.ap()[128 * r:128 * (r + 1), 2048:4096]))
    mk_kv = p.mark("DY")

    mk_ot_free = [0] * 8
    pot_mk = [0, 0]
    for qt in range(2):
        qlo = 1024 * qt
        for h in range(8):
            hstart_t = p.mark("T")
            hstart_s = p.mark("S")
            hstart_v = p.mark("V")
            mk_exp = [0, 0]
            mk_add = 0
            mk_addp = [0, 0, 0, 0]
            exp_marks = [0] * 64
            for mc in range(66):
                if mc < 64:
                    ps_t = c.PSA if mc % 2 == 0 else c.PSB
                    for hi in range(2):
                        p.t(lambda e, mc=mc, hi=hi, ps_t=ps_t, h=h, qlo=qlo: e.matmul(
                            ps_t[:, 512 * hi:512 * hi + 512],
                            lhsT=bfa(c, BFA_KTF + 128 * mc, 128, 128),
                            rhs=c.BFA[0:128, BFA_QT0 + 2048 * h + qlo + 512 * hi:
                                      BFA_QT0 + 2048 * h + qlo + 512 * hi + 512],
                            start=True, stop=True),
                            wait=([("DY", mk_kv), ("S", max(mk_exp[mc % 2], hstart_s)),
                                   ("V", hstart_v)]
                                  if hi == 0 else []),
                            inc=(hi == 1))
                    mk_l = p.mark("T")
                    mk_exp[mc % 2] = p.s(
                        lambda e, mc=mc, ps_t=ps_t: e.activation(
                            out=bfa(c, BFA_E0 + 1024 * (mc % 4), 128, 1024),
                            in_=ps_t[:, :], func=AF.Exp, bias=c.CBIAS[:, 1:2]),
                        wait=[("T", max(mk_l, c.mk_xb_free)), ("V", mk_addp[mc % 4])])
                    exp_marks[mc] = mk_exp[mc % 2]
                    if mc == 0:
                        mk_add = p.v(lambda e: e.tensor_copy(
                            out=c.fs_acc[:, :], in_=bfa(c, BFA_E0, 128, 1024)),
                            wait=[("S", mk_exp[0]), ("T", hstart_t)])
                    else:
                        mk_add = p.v(lambda e, mc=mc: e.tensor_tensor(
                            out=c.fs_acc[:, :], in0=c.fs_acc[:, :],
                            in1=bfa(c, BFA_E0 + 1024 * (mc % 4), 128, 1024), op=ALU.add),
                            wait=[("S", mk_exp[mc % 2])])
                    mk_addp[mc % 4] = mk_add
                if mc >= 2:
                    m = mc - 2
                    for hi in range(2):
                        p.t(lambda e, m=m, hi=hi: e.matmul(
                            c.PSC[:, 512 * hi:512 * hi + 512],
                            lhsT=bfa(c, BFA_VF + 128 * m, 128, 128),
                            rhs=c.BFA[0:128, BFA_E0 + 1024 * (m % 4) + 512 * hi:
                                      BFA_E0 + 1024 * (m % 4) + 512 * hi + 512],
                            start=(m == 0), stop=(m == 63)),
                            wait=([("S", exp_marks[m])] if hi == 0 else []),
                            inc=(hi == 1 and m == 63))
            mk_av = p.mark("T")
            for hi in range(2):
                p.t(lambda e, hi=hi: e.matmul(
                    c.PSA[0:1, 512 * hi:512 * hi + 512], lhsT=c.ONES[:, 0:1],
                    rhs=c.fs_acc[:, 512 * hi:512 * hi + 512], start=True, stop=True),
                    wait=[("V", mk_add), ("S", mk_exp[0])], inc=(hi == 1))
            mk_sum = p.mark("T")
            ROW = c.fs_rows[0:1, 0:1024]
            p.v(lambda e: e.reciprocal(out=ROW, in_=c.PSA[0:1, 0:1024]),
                wait=[("T", mk_sum)])
            mk_rec = p.mark("V")
            for hi in range(2):
                p.t(lambda e, hi=hi: e.matmul(
                    c.PSB[:, 512 * hi:512 * hi + 512], lhsT=c.ONES[0:1, :],
                    rhs=ROW[:, 512 * hi:512 * hi + 512], start=True, stop=True),
                    wait=[("V", mk_rec), ("S", mk_exp[1])], inc=(hi == 1))
            mk_bc = p.mark("T")
            p.s(lambda e: e.copy(out=c.fs_rb[:, :], in_=c.PSB[:, :]),
                wait=[("T", mk_bc)])
            mk_rbe = p.mark("S")
            p.v(lambda e, h=h: e.tensor_tensor(
                out=bfa(c, BFA_OT + 1024 * h, 128, 1024), in0=c.PSC[:, :],
                in1=c.fs_rb[:, :], op=ALU.mult),
                wait=[("S", mk_rbe), ("T", max(mk_av, mk_ot_free[h]))])
        mk_norm = p.mark("V")

        yslot_free = [0, 0]
        for dc in range(NC):
            buf = dc % 2
            p.g(lambda e, dc=dc, buf=buf: e.dma_start(
                out=bfa(c, POTBUF + 1024 * buf, 128, 1024).rearrange("p (h j) -> p h j", h=8),
                in_=POT[:, :, 128 * dc:128 * (dc + 1)].rearrange("h p j -> p h j")),
                wait=[("T", max(pot_mk[buf], c.mk_xb_free))], dma="DG2")
            mk_dma = p.mark("DG2")
            for qs in range(2):
                si = (dc * 2 + qs) % 2
                ps_t = c.PSA if si == 0 else c.PSB
                for h in range(8):
                    p.t(lambda e, dc=dc, qs=qs, h=h, buf=buf, ps_t=ps_t: e.matmul(
                        ps_t[:, 0:512],
                        lhsT=bfa(c, POTBUF + 1024 * buf + 128 * h, 128, 128),
                        rhs=bfa(c, BFA_OT + 1024 * h + 512 * qs, 128, 512),
                        start=(h == 0), stop=(h == 7)),
                        wait=[("V", max(mk_norm, yslot_free[si])), ("DG2", mk_dma)]
                        if h == 0 else [],
                        inc=(h == 7))
                mkt = p.mark("T")
                if qs == 1:
                    pot_mk[buf] = mkt
                yslot_free[si] = p.v(
                    lambda e, dc=dc, qs=qs, qlo=qlo, ps_t=ps_t: e.tensor_tensor(
                        out=c.X[dc][:, qlo + 512 * qs:qlo + 512 * qs + 512],
                        in0=c.X[dc][:, qlo + 512 * qs:qlo + 512 * qs + 512],
                        in1=ps_t[:, 0:512], op=ALU.add),
                    wait=[("T", mkt)])
        mk_y = p.mark("T")
        for h in range(8):
            mk_ot_free[h] = mk_y


def emit_moe(c):
    p = c.p
    mkln = c.mk_ln_done
    MSD = c.din["MS"]
    GWTB = bfa(c, BFA_MTMP, 128, 64)
    GTOK = c.fs_tmp[:, 0:128]
    GROW = c.fs_rows[0:1, 0:2048]
    SC = c.fs_mrb

    p.g(lambda e: e.dma_start(out=GWTB.rearrange("p (k e) -> p k e", k=8),
                              in_=c.din["GWT"].rearrange("k p e -> p k e")),
        wait=[("S", p.mark("S"))], dma="DG3")
    mk_gw = p.mark("DG3")

    ev_mks = [0, 0]
    for tc in range(16):
        psl = slice(512 * (tc % 2), 512 * (tc % 2) + 8)
        tsl = slice(128 * tc, 128 * (tc + 1))
        for kc in range(NC):
            p.t(lambda e, kc=kc, tsl=tsl, psl=psl: e.matmul(
                c.PSA[:, psl], lhsT=XBc(c, kc, tsl),
                rhs=GWTB[:, 8 * kc:8 * kc + 8],
                start=(kc == 0), stop=(kc == NC - 1)),
                wait=([("V", mkln), ("DG3", mk_gw), ("S", ev_mks[tc % 2])]
                      if kc == 0 else []),
                inc=(kc == NC - 1))
        mkt = p.mark("T")
        ev_mks[tc % 2] = p.s(lambda e, tc=tc, psl=psl: e.copy(
            out=GTOK[:, 8 * tc:8 * tc + 8], in_=c.PSA[:, psl]), wait=[("T", mkt)])
    mk_glog = p.mark("S")

    M1N = SC[:, 0:1]
    M1P = SC[:, 1:2]
    M2N = SC[:, 2:3]
    M2P = SC[:, 3:4]
    ZS = SC[:, 4:5]
    RS = SC[:, 5:6]
    SEXP = SC[:, 8:16]
    L2 = SC[:, 16:24]
    KEEP = SC[:, 24:32]
    for tc in range(16):
        G8 = GTOK[:, 8 * tc:8 * tc + 8]
        mv = p.v(lambda e, G8=G8: e.tensor_reduce(out=M1N, in_=G8, axis=AX.X,
                                                  op=ALU.max, negate=True),
                 wait=[("S", mk_glog)])
        p.s(lambda e, G8=G8: e.activation(out=SEXP, in_=G8, func=AF.Exp,
                                          bias=M1N), wait=[("V", mv)])
        mv = p.v(lambda e: e.tensor_scalar_mul(out=M1P, in0=M1N, scalar1=-1.0),
                 wait=[("V", mv)])
        mv = p.v(lambda e, G8=G8: e.tensor_scalar(out=KEEP, in0=G8, scalar1=M1P,
                                                  scalar2=None, op0=ALU.is_lt),
                 wait=[("V", mv)])
        mv = p.v(lambda e: e.tensor_scalar(out=L2, in0=KEEP, scalar1=1e30,
                                           scalar2=-1e30, op0=ALU.mult, op1=ALU.add),
                 wait=[("V", mv)])
        mv = p.v(lambda e, G8=G8: e.tensor_tensor(out=L2, in0=L2, in1=G8, op=ALU.add),
                 wait=[("V", mv)])
        mv = p.v(lambda e: e.tensor_reduce(out=M2N, in_=L2, axis=AX.X, op=ALU.max,
                                           negate=True), wait=[("V", mv)])
        mv = p.v(lambda e: e.tensor_scalar_mul(out=M2P, in0=M2N, scalar1=-1.0),
                 wait=[("V", mv)])
        mv = p.v(lambda e, G8=G8: e.tensor_scalar(out=KEEP, in0=G8, scalar1=M2P,
                                                  scalar2=None, op0=ALU.is_ge),
                 wait=[("V", mv)])
        mks = p.mark("S")
        mv = p.v(lambda e: e.tensor_tensor(out=SEXP, in0=SEXP, in1=KEEP, op=ALU.mult),
                 wait=[("S", mks), ("V", mv)])
        mv = p.v(lambda e: e.tensor_reduce(out=ZS, in_=SEXP, axis=AX.X, op=ALU.add),
                 wait=[("V", mv)])
        mv = p.v(lambda e: e.reciprocal(out=RS, in_=ZS), wait=[("V", mv)])
        p.v(lambda e, G8=G8: e.tensor_scalar(out=G8, in0=SEXP, scalar1=RS,
                                             scalar2=0.5, op0=ALU.mult, op1=ALU.mult),
            wait=[("V", mv)])
    mk_gtok = p.mark("V")

    w13_mk = [0, 0]
    w2r_mk = 0
    mk_geb_read = 0
    mk_hb_free = 0
    for ex in range(EXPERTS):
        mk_ps_a = p.mark("S")
        mk_vv = p.mark("V")
        tr_ev = [mk_ps_a, mk_ps_a]
        for tc in range(16):
            po = 128 * (tc % 2)
            po = 512 * (tc % 2)
            p.t(lambda e, ex=ex, tc=tc, po=po: e.transpose(
                c.PSA[0:1, po:po + 128],
                in_=GTOK[:, 8 * tc + ex:8 * tc + ex + 1], identity=c.IDENT[:, :]),
                wait=[("V", max(mk_gtok, mk_vv)), ("S", tr_ev[tc % 2])], inc=True)
            mkt = p.mark("T")
            tr_ev[tc % 2] = p.s(lambda e, tc=tc, po=po: e.copy(
                out=GROW[:, 128 * tc:128 * (tc + 1)],
                in_=c.PSA[0:1, po:po + 128]), wait=[("T", mkt)])
        mk_grow = p.mark("S")
        bc_ev = [mk_grow, mk_grow]
        for hi in range(4):
            p.t(lambda e, hi=hi: e.matmul(
                c.PSA[:, 512 * (hi % 2):512 * (hi % 2) + 512], lhsT=c.ONES[0:1, :],
                rhs=GROW[:, 512 * hi:512 * hi + 512], start=True, stop=True),
                wait=[("S", max(mk_grow, bc_ev[hi % 2]))], inc=True)
            mkt = p.mark("T")
            bc_ev[hi % 2] = p.s(lambda e, hi=hi: e.copy(
                out=bfa(c, BFA_GEB + 512 * hi, 128, 512),
                in_=c.PSA[:, 512 * (hi % 2):512 * (hi % 2) + 512]),
                wait=[("T", mkt), ("V", mk_geb_read)])
        mk_geb = p.mark("S")

        for half in range(2):
            hcs = list(range(11 * half, 11 * half + 11))
            silu_mks = [0, 0]
            t3_mks = [0, 0]
            mtmp_v = [0, 0, 0, 0]
            pv0 = p.mark("V")
            sidx = 0
            for i, hc in enumerate(hcs):
                buf = hc % 2
                p.g(lambda e, ex=ex, hc=hc, buf=buf: e.dma_start(
                    out=bfa(c, BFA_W13 + 2048 * buf, 128, 2048).rearrange(
                        "p (m f) -> p m f", m=2),
                    in_=MSD[ex, hc, 0:2].rearrange("m p f -> p m f")),
                    wait=[("T", w13_mk[buf])], dma="DG3")
                mk_dma = p.mark("DG3")
                p.g(lambda e, ex=ex, hc=hc, i=i: e.dma_start(
                    out=bfa(c, BFA_W2R + 1024 * i, 128, 1024),
                    in_=MSD[ex, hc, 2]),
                    wait=[("T", w2r_mk)], dma="DG4")
                for sl4 in range(4):
                    sl = slice(512 * sl4, 512 * (sl4 + 1))
                    par = sidx % 2
                    psl = slice(512 * par, 512 * par + 512)
                    for kc in range(NC):
                        p.t(lambda e, buf=buf, kc=kc, sl=sl, psl=psl: e.matmul(
                            c.PSA[:, psl], lhsT=bfa(c, BFA_W13 + 2048 * buf + 128 * kc, 128, 128),
                            rhs=XBc(c, kc, sl), start=(kc == 0), stop=(kc == NC - 1)),
                            wait=([("DG3", mk_dma), ("S", silu_mks[par]), ("V", mkln)]
                                  if kc == 0 else []))
                    for kc in range(NC):
                        p.t(lambda e, buf=buf, kc=kc, sl=sl, psl=psl: e.matmul(
                            c.PSB[:, psl],
                            lhsT=bfa(c, BFA_W13 + 2048 * buf + 1024 + 128 * kc, 128, 128),
                            rhs=XBc(c, kc, sl), start=(kc == 0), stop=(kc == NC - 1)),
                            wait=[("V", max(t3_mks[par], pv0))] if kc == 0 else [],
                            inc=(kc == NC - 1))
                    mkt = p.mark("T")
                    ms = sidx % 4
                    silu_mks[par] = p.s(lambda e, psl=psl: e.activation(
                        out=bfa(c, BFA_MTMP, 128, 512), in_=c.PSA[:, psl],
                        func=AF.Silu), wait=[("T", mkt), ("V", mtmp_v[0])])
                    t3_mks[par] = p.v(lambda e, sl4=sl4, psl=psl: e.tensor_tensor(
                        out=bfa(c, BFA_MTMP + 512, 128, 512), in0=c.PSB[:, psl],
                        in1=bfa(c, BFA_GEB + 512 * sl4, 128, 512), op=ALU.mult),
                        wait=[("T", mkt), ("S", mk_geb)])
                    mtmp_v[0] = p.v(lambda e, i=i, sl4=sl4: e.tensor_tensor(
                        out=bfa(c, BFA_HB + 2048 * i + 512 * sl4, 128, 512),
                        in0=bfa(c, BFA_MTMP, 128, 512),
                        in1=bfa(c, BFA_MTMP + 512, 128, 512), op=ALU.mult),
                        wait=[("S", silu_mks[par]), ("T", mk_hb_free)])
                    sidx += 1
                w13_mk[buf] = p.mark("T")
            mk_w2dma = p.mark("DG4")
            mk_h = p.mark("V")

            oslot_mk = [0, 0]
            ti = 0
            for dc in range(NC):
                for qh in range(2):
                    ps_t = c.PSB if ti % 2 == 0 else c.PSC
                    for i, hc in enumerate(hcs):
                        for s2 in range(2):
                            p.t(lambda e, dc=dc, qh=qh, i=i, s2=s2, ps_t=ps_t: e.matmul(
                                ps_t[:, 512 * s2:512 * s2 + 512],
                                lhsT=bfa(c, BFA_W2R + 1024 * i + 128 * dc, 128, 128),
                                rhs=bfa(c, BFA_HB + 2048 * i + 1024 * qh + 512 * s2,
                                        128, 512),
                                start=(i == 0), stop=(i == 10)),
                                wait=([("DG4", mk_w2dma), ("V", max(mk_h, oslot_mk[ti % 2]))]
                                      if (i == 0 and s2 == 0) else []),
                                inc=(i == 10 and s2 == 1))
                    mkt = p.mark("T")
                    oslot_mk[ti % 2] = p.v(
                        lambda e, dc=dc, qh=qh, ps_t=ps_t: e.tensor_tensor(
                            out=c.X[dc][:, 1024 * qh:1024 * qh + 1024],
                            in0=c.X[dc][:, 1024 * qh:1024 * qh + 1024],
                            in1=ps_t[:, :], op=ALU.add),
                        wait=[("T", mkt)])
                    ti += 1
            w2r_mk = p.mark("T")
            mk_hb_free = w2r_mk
        mk_geb_read = max(t3_mks)
    c.mk_xb_free = p.mark("T")


ATT1_SCALE = 0.35355339059327373


def emit_attn1(c):
    from concourse.bass_core_rust import IndirectOffsetOnAxis
    p = c.p
    mkln = c.mk_ln_done
    CE = c.CEXTRA
    BQ_, BK_ = CE[:, 0:8], CE[:, 8:16]
    BQS, BKS = CE[:, 16:24], CE[:, 24:32]
    MX = CE[0:64, 32:48]
    Z3 = CE[0:64, 48:64]
    MXG = CE[0:64, 64:80]
    NEGM = CE[0:64, 96:112]
    Z3R = CE[0:64, 112:128]
    ZPART = CE[0:64, 116 + 12:116 + 14]
    ZPART = CE[0:64, 126:128]
    K1F = CE[:, 128:192]
    K1N = CE[:, 192:256]
    K1ROW = CE[:, 256:260]
    NM64 = CE[0:64, 260:262]
    SM64 = CE[0:64, 262:264]
    RC64 = CE[0:64, 264:266]
    CS64 = CE[0:64, 266:268]
    RSC = CE[0:64, 268:270]

    WQ_BUF = [49152, 50176]
    VSTG = [22528, 21504]
    CONVB = [55296, 55680]
    HALO_L, HALO_R = 49152, 50176

    p.y(lambda e: e.dma_start(out=BQ_, in_=c.din["BQ"]))
    p.y(lambda e: e.dma_start(out=BK_, in_=c.din["BK"]))
    p.y(lambda e: e.dma_start(out=c.fs_rows[0:1, 0:1024], in_=c.din["BV"]),
        wait=[("V", mkln)])
    p.y(lambda e: e.dma_start(out=c.HIDX[:, :], in_=c.din["HALOIDX"]))
    mk_bload = p.mark("DY")
    p.g(lambda e: e.dma_start(out=c.BFA[0:64, 51200:54272], in_=c.din["EYES"]),
        wait=[("T", c.mk_xb_free)], dma="DG5")
    mk_eyesb = p.mark("DG5")
    p.s(lambda e: e.activation(out=BQS, in_=BQ_, func=AF.Copy, scale=ATT1_SCALE),
        wait=[("DY", mk_bload)])
    p.s(lambda e: e.activation(out=BKS, in_=BK_, func=AF.Copy, scale=ATT1_SCALE))
    mk_bscale = p.mark("S")
    for kc in range(NC):
        p.g(lambda e, kc=kc: e.dma_start(out=bfa(c, BFA_WVR + 1024 * kc, 128, 1024),
                                         in_=c.din["WV"][kc]), dma="DG5")
    mk_wv = p.mark("DG5")
    for hi in range(2):
        p.t(lambda e, hi=hi: e.matmul(
            c.PSA[:, 512 * hi:512 * hi + 512], lhsT=c.ONES[0:1, :],
            rhs=c.fs_rows[0:1, 512 * hi:512 * hi + 512], start=True, stop=True),
            wait=[("DY", mk_bload), ("S", p.mark("S"))], inc=(hi == 1))
    mk_bvb = p.mark("T")
    p.s(lambda e: e.copy(out=bfa(c, BFA_BVB, 128, 1024), in_=c.PSA[:, :]),
        wait=[("T", mk_bvb)])
    mk_bvbe = p.mark("S")

    vstg_mk = [0, 0]
    for tc in range(16):
        tsl = slice(128 * tc, 128 * (tc + 1))
        ps_t = c.PSB if tc % 2 == 0 else c.PSC
        for jh in range(2):
            for kc in range(NC):
                p.t(lambda e, kc=kc, tsl=tsl, jh=jh, ps_t=ps_t: e.matmul(
                    ps_t[:, 512 * jh:512 * jh + 512], lhsT=XBc(c, kc, tsl),
                    rhs=bfa(c, BFA_WVR + 1024 * kc + 512 * jh, 128, 512),
                    start=(kc == 0), stop=(kc == NC - 1)),
                    wait=([("V", max(mkln, vstg_mk[tc % 2])), ("DG5", mk_wv)]
                          if kc == 0 else []),
                    inc=(kc == NC - 1 and jh == 1))
        mkt = p.mark("T")
        vstg_mk[tc % 2] = p.v(lambda e, tc=tc, ps_t=ps_t: e.tensor_tensor(
            out=bfa(c, VSTG[tc % 2], 128, 1024), in0=ps_t[:, :],
            in1=bfa(c, BFA_BVB, 128, 1024), op=ALU.add),
            wait=[("T", mkt), ("S", mk_bvbe), ("DY", p.mark("DY"))])
        mkv = p.mark("V")
        p.y(lambda e, tc=tc: e.dma_start(out=c.dram["v_dram"].ap()[tc],
                                         in_=bfa(c, VSTG[tc % 2], 128, 1024)),
            wait=[("V", mkv)])
        if tc == 0:
            p.y(lambda e: e.dma_start(out=c.dram["vh_in"].ap()[0:128, :],
                                      in_=bfa(c, VSTG[0], 128, 1024)))
        if tc == 15:
            p.y(lambda e: e.dma_start(out=c.dram["vh_in"].ap()[128:256, :],
                                      in_=bfa(c, VSTG[1], 128, 1024)))
    mk_vdma = p.mark("DY")
    p.emit("g", lambda e: e.collective_compute(
        "AllGather", ALU.bypass, replica_groups=GROUPS_RG,
        ins=[c.dram["vh_in"].ap().opt()], outs=[c.dram["vh_out"].ap().opt()]),
        wait=[("DY", mk_vdma)], inc=[("CC4", 1)])

    wbuf_mk = [0, 0]
    for z, (WJ, BS, base) in enumerate(
            [("WQJ", BQS, BFA_QT1), ("WKJ", BKS, BFA_KT1)]):
        for jc in range(NC):
            buf = jc % 2
            p.g(lambda e, WJ=WJ, jc=jc, buf=buf: e.dma_start(
                out=bfa(c, WQ_BUF[buf], 128, 1024).rearrange("p (k j) -> p k j", k=8),
                in_=c.din[WJ][jc].rearrange("k p j -> p k j")),
                wait=[("T", wbuf_mk[buf])], dma="DG5")
            mk_dma = p.mark("DG5")
            for qs in range(QS):
                sl = slice(512 * qs, 512 * (qs + 1))
                ps_t = c.PSA if qs % 2 == 0 else c.PSB
                for kc in range(NC):
                    p.t(lambda e, buf=buf, kc=kc, sl=sl, ps_t=ps_t: e.matmul(
                        ps_t[:, 0:512], lhsT=bfa(c, WQ_BUF[buf] + 128 * kc, 128, 128),
                        rhs=XBc(c, kc, sl), start=(kc == 0), stop=(kc == NC - 1)),
                        wait=([("DG5", mk_dma), ("S", p.mark("S")), ("V", mkln)]
                              if kc == 0 else []),
                        inc=(kc == NC - 1))
                mkt = p.mark("T")
                if qs == QS - 1:
                    wbuf_mk[buf] = mkt
                p.s(lambda e, base=base, jc=jc, sl=sl, ps_t=ps_t, BS=BS: e.activation(
                    out=c.BFA[0:128, base + 2048 * jc + sl.start:base + 2048 * jc + sl.stop],
                    in_=ps_t[:, 0:512], func=AF.Identity, scale=ATT1_SCALE,
                    bias=BS[:, jc:jc + 1]),
                    wait=[("T", mkt), ("DY", mk_vdma if z == 0 else 0),
                          ("S", mk_bscale)])
    c.mk_xb_free = p.mark("T")
    mk_qk = p.mark("S")

    for tc in range(16):
        p.y(lambda e, tc=tc: e.dma_start(out=bfa(c, BFA_V1 + 1024 * tc, 128, 1024),
                                         in_=c.dram["v_dram"].ap()[tc]),
            wait=[("T", c.mk_xb_free), ("S", mk_qk), ("DY", mk_vdma)])
    p.v(lambda e: e.memset(bfa(c, HALO_L, 128, 1024), 0.0),
        wait=[("T", c.mk_xb_free)])
    p.v(lambda e: e.memset(bfa(c, HALO_R, 128, 1024), 0.0))
    mk_hz = p.mark("V")
    p.g(lambda e: e.indirect_dma_start(
        out=bfa(c, HALO_L, 128, 1024), out_offset=None,
        in_=c.dram["vh_out"].ap(),
        in_offset=IndirectOffsetOnAxis(c.HIDX[:, 0:1], 0),
        bounds_check=1023, oob_is_err=False),
        wait=[("CC4", 1), ("V", mk_hz), ("DY", mk_bload)], dma="DG5")
    p.g(lambda e: e.indirect_dma_start(
        out=bfa(c, HALO_R, 128, 1024), out_offset=None,
        in_=c.dram["vh_out"].ap(),
        in_offset=IndirectOffsetOnAxis(c.HIDX[:, 1:2], 0),
        bounds_check=1023, oob_is_err=False), dma="DG5")
    mk_v1 = p.mark("DY")
    mk_halo = p.mark("DG5")

    mv = 0
    for z, base in enumerate([BFA_QT1, BFA_KT1]):
        for dc in range(NC):
            mv = p.v(lambda e, z=z, dc=dc, base=base: e.tensor_reduce(
                out=c.QKL[:, 128 * z + 16 * dc:128 * z + 16 * dc + 16],
                in_=c.BFA[0:128, base + 2048 * dc:
                          base + 2048 * dc + 2048].rearrange(
                    "p (s j) -> p s j", s=16),
                axis=AX.X, op=ALU.add),
                wait=[("S", mk_qk)])
    p.s(lambda e: e.activation(out=c.QKL2[:, 0:256], in_=c.QKL[:, 0:256], func=AF.Copy,
                               scale=1.0 / 128.0), wait=[("V", mv)])
    mk_l = p.mark("S")
    p.y(lambda e: e.dma_start(
        out=c.dram["lm_in"].ap().rearrange("(p z c) l -> p z c l", p=128, z=2, c=8),
        in_=c.QKL2[:, 0:256].rearrange("p (z c l) -> p z c l", z=2, c=8)),
        wait=[("S", mk_l)])
    mk_lmin = p.mark("DY")
    p.emit("g", lambda e: e.collective_compute(
        "AllGather", ALU.bypass, replica_groups=GROUPS_RG,
        ins=[c.dram["lm_in"].ap().opt()], outs=[c.dram["lm_out"].ap().opt()]),
        wait=[("DY", mk_lmin)], inc=[("CC1", 1)])
    for r in range(4):
        p.y(lambda e, r=r: e.dma_start(
            out=c.QLF[:, :].rearrange("p (z c l) -> p z c l", z=2, c=8)[
                :, :, :, 16 * r:16 * (r + 1)],
            in_=c.dram["lm_out"].ap().rearrange(
                "(r p z c) l -> r p z c l", r=4, p=128, z=2, c=8)[r]),
            wait=[("CC1", 1)])
    mk_qlf = p.mark("DY")
    p.s(lambda e: e.copy(out=bfa(c, BFA_QLB, 128, 1024), in_=c.QLF[:, :]),
        wait=[("DY", mk_qlf)])
    mk_qlb = p.mark("S")

    l3_done_s = 0
    for h in range(ATT1_H):
        dc, pb = h // 2, 64 * (h % 2)
        for qs in range(QS):
            ps_t = c.PSB if qs < 2 else c.PSC
            p.t(lambda e, h=h, dc=dc, pb=pb, qs=qs, ps_t=ps_t: e.matmul(
                ps_t[0:64, 512 * (qs % 2):512 * (qs % 2) + 512],
                lhsT=c.BFA[pb:pb + 64, BFA_QLB + 64 * dc:BFA_QLB + 64 * dc + 64],
                rhs=c.BFA[pb:pb + 64, BFA_KT1 + 2048 * dc + 512 * qs:
                          BFA_KT1 + 2048 * dc + 512 * qs + 512],
                start=True, stop=True),
                wait=([("S", max(mk_qlb, l3_done_s)), ("V", p.mark("V"))]
                      if qs == 0 else []),
                inc=(qs == 3))
        mkt = p.mark("T")
        mv = p.v(lambda e: e.tensor_reduce(out=ZPART[:, 0:1], in_=c.PSB[0:64, :],
                                           axis=AX.X, op=ALU.max), wait=[("T", mkt)])
        mv = p.v(lambda e: e.tensor_reduce(out=ZPART[:, 1:2], in_=c.PSC[0:64, :],
                                           axis=AX.X, op=ALU.max), wait=[("V", mv)])
        mv = p.v(lambda e, h=h: e.tensor_tensor(out=MX[:, h:h + 1], in0=ZPART[:, 0:1],
                                                in1=ZPART[:, 1:2], op=ALU.max),
                 wait=[("V", mv)])
        l3_done_s = p.s(lambda e: e.copy(out=ZPART[:, 0:1], in_=ZPART[:, 0:1]),
                        wait=[("V", mv)])
    p.y(lambda e: e.dma_start(out=c.dram["mx_in"].ap(), in_=MX), wait=[("S", l3_done_s)])
    p.emit("g", lambda e: e.collective_compute(
        "AllReduce", ALU.max, replica_groups=GROUPS_RG,
        ins=[c.dram["mx_in"].ap().opt()], outs=[c.dram["mx_out"].ap().opt()]),
        wait=[("DY", p.mark("DY"))], inc=[("CC2", 1)])
    p.y(lambda e: e.dma_start(out=MXG, in_=c.dram["mx_out"].ap()), wait=[("CC2", 1)])
    mk_mxg = p.mark("DY")
    mv = p.v(lambda e: e.tensor_scalar_mul(out=NEGM, in0=MXG, scalar1=-1.0),
             wait=[("DY", mk_mxg)])
    mk_negm_v = mv

    TB_K2E = c.UBUF[0:64, 0:1024]
    TB_K2N = c.fs_tmp[0:64, 0:1024]
    TB_K2NT = c.fs_rb[0:64, 0:1024]
    TB_V = c.fs_mrb[0:64, 0:1024]
    TB_KVS = c.fs_sq[0:64, 0:1024]
    TB_KVT = c.UBUF[0:64, 0:1024]
    TB_A = c.INV2[0:64, 0:1024]
    NMB = CE[0:64, 320:336]
    SMB = CE[0:64, 336:352]
    RCB = CE[0:64, 352:368]
    RSCB = CE[0:64, 368:384]
    CSB = CE[0:1, 384:400]
    EYE_B = [c.BFA[0:64, 51200 + 1024 * k:51200 + 1024 * (k + 1)] for k in range(3)]

    def seg3(ap):
        return ap.rearrange("p (s j) -> p s j", s=16)

    mkt = 0
    HORD = [h for h in range(ATT1_H) if h % 2 == 0] + \
        [h for h in range(ATT1_H) if h % 2 == 1]
    for hi, h in enumerate(HORD):
        dc, pb = h // 2, 64 * (h % 2)
        mkt = p.t(lambda e, h=h, dc=dc, pb=pb: e.matmul(
            c.PSA[0:64, 64 * h:64 * h + 64],
            lhsT=c.QLF[pb:pb + 64, 64 * dc:64 * dc + 64],
            rhs=c.QLF[pb:pb + 64, 512 + 64 * dc:512 + 64 * dc + 64],
            start=True, stop=True),
            wait=([("DY", mk_qlf), ("S", p.mark("S"))] if hi == 0 else []),
            inc=(hi == ATT1_H - 1))
    v_nmb = p.v(lambda e: e.tensor_reduce(
        out=NMB, in_=seg3(c.PSA[0:64, 0:1024]), axis=AX.X, op=ALU.max, negate=True),
        wait=[("T", mkt)])
    mks = 0
    for h in range(ATT1_H):
        mks = p.s(lambda e, h=h: e.activation(
            out=TB_K2E[:, 64 * h:64 * h + 64], in_=c.PSA[0:64, 64 * h:64 * h + 64],
            func=AF.Exp, bias=NMB[:, h:h + 1]),
            wait=[("V", v_nmb)] if h == 0 else [])
    mv = p.v(lambda e: e.tensor_reduce(
        out=SMB, in_=seg3(TB_K2E), axis=AX.X, op=ALU.add), wait=[("S", mks)])
    mv = p.v(lambda e: e.reciprocal(out=RCB, in_=SMB), wait=[("V", mv)])
    for h in range(ATT1_H):
        mv = p.v(lambda e, h=h: e.tensor_scalar(
            out=TB_K2N[:, 64 * h:64 * h + 64], in0=TB_K2E[:, 64 * h:64 * h + 64],
            scalar1=RCB[:, h:h + 1], scalar2=None, op0=ALU.mult),
            wait=[("V", mv)])
    for hi in range(2):
        mkt = p.t(lambda e, hi=hi: e.matmul(
            c.PSB[0:1, 512 * hi:512 * hi + 512], lhsT=c.ONES[0:64, 0:1],
            rhs=TB_K2N[:, 512 * hi:512 * hi + 512], start=True, stop=True),
            wait=[("V", mv)] if hi == 0 else [], inc=(hi == 1))
    mv2 = p.v(lambda e: e.tensor_reduce(
        out=CSB, in_=seg3(c.PSB[0:1, 0:1024]), axis=AX.X, op=ALU.max),
        wait=[("T", mkt)])
    mv2 = p.v(lambda e: e.reciprocal(out=CSB, in_=CSB), wait=[("V", mv2)])
    mkt = p.t(lambda e: e.matmul(c.PSA[0:64, 0:16], lhsT=c.ONES[0:1, 0:64], rhs=CSB,
                                 start=True, stop=True),
              wait=[("V", mv2), ("S", mks)], inc=True)
    ev = p.s(lambda e: e.copy(out=RSCB, in_=c.PSA[0:64, 0:16]), wait=[("T", mkt)])
    for h in range(ATT1_H):
        mkt = p.t(lambda e, h=h: e.transpose(
            c.PSA[0:64, 64 * h:64 * h + 64], in_=TB_K2N[:, 64 * h:64 * h + 64],
            identity=c.IDENT[0:64, 0:64]),
            wait=[("S", ev)] if h == 0 else [], inc=(h == ATT1_H - 1))
    ev = p.s(lambda e: e.copy(out=TB_K2NT, in_=c.PSA[0:64, 0:1024]), wait=[("T", mkt)])
    mv = 0
    for h in range(ATT1_H):
        mv = p.v(lambda e, h=h: e.tensor_scalar(
            out=TB_V[:, 64 * h:64 * h + 64], in0=TB_K2NT[:, 64 * h:64 * h + 64],
            scalar1=RSCB[:, h:h + 1], scalar2=None, op0=ALU.mult),
            wait=[("S", ev)] if h == 0 else [])
    v_in = mv
    ev_i = ev

    def batch_pe(dst_ps, lhs_t, rhs_t, first_wait, trans=False):
        mkt = 0
        for h in range(ATT1_H):
            sl = slice(64 * h, 64 * h + 64)
            if trans:
                mkt = p.t(lambda e, sl=sl, dst_ps=dst_ps, lhs_t=lhs_t: e.transpose(
                    dst_ps[0:64, sl], in_=lhs_t[:, sl],
                    identity=c.IDENT[0:64, 0:64]),
                    wait=first_wait if h == 0 else [], inc=(h == ATT1_H - 1))
            else:
                mkt = p.t(lambda e, sl=sl, dst_ps=dst_ps, lhs_t=lhs_t, rhs_t=rhs_t:
                          e.matmul(dst_ps[0:64, sl], lhsT=lhs_t[:, sl],
                                   rhs=rhs_t[:, sl], start=True, stop=True),
                          wait=first_wait if h == 0 else [],
                          inc=(h == ATT1_H - 1))
        return mkt

    for it in range(6):
        mkt = batch_pe(c.PSA, TB_K2NT, TB_V, [("S", ev_i), ("V", v_in)])
        ev_a = p.s(lambda e: e.copy(out=TB_KVS, in_=c.PSA[0:64, 0:1024]),
                   wait=[("T", mkt)])
        v_c = p.v(lambda e: e.tensor_tensor(out=TB_A, in0=EYE_B[0], in1=TB_KVS,
                                            op=ALU.subtract),
                  wait=[("S", ev_a), ("DG5", mk_eyesb)])
        mkt = batch_pe(c.PSB, TB_KVS, None, [("S", ev_a)], trans=True)
        ev_b = p.s(lambda e: e.copy(out=TB_KVT, in_=c.PSB[0:64, 0:1024]),
                   wait=[("T", mkt)])
        mkt = batch_pe(c.PSA, TB_KVT, TB_A, [("S", ev_b), ("V", v_c)])
        v_e = p.v(lambda e: e.tensor_tensor(out=TB_A, in0=EYE_B[1],
                                            in1=c.PSA[0:64, 0:1024],
                                            op=ALU.subtract), wait=[("T", mkt)])
        mkt = batch_pe(c.PSB, TB_V, None, [("S", ev_b)], trans=True)
        ev_vt = p.s(lambda e: e.copy(out=TB_KVS, in_=c.PSB[0:64, 0:1024]),
                    wait=[("T", mkt), ("V", v_c)])
        mkt = batch_pe(c.PSA, TB_KVT, TB_A, [("V", v_e)])
        v_g = p.v(lambda e: e.tensor_tensor(out=TB_A, in0=EYE_B[2],
                                            in1=c.PSA[0:64, 0:1024],
                                            op=ALU.subtract), wait=[("T", mkt)])
        mkt = batch_pe(c.PSB, TB_KVS, TB_A, [("S", ev_vt), ("V", v_g)])
        dst = c.INV2[0:64, 0:1024] if it == 5 else TB_V
        ev_i = p.s(lambda e, dst=dst: e.activation(
            out=dst, in_=c.PSB[0:64, 0:1024], func=AF.Copy, scale=0.25),
            wait=[("T", mkt)])
        v_in = v_g
    mk_inv = p.mark("S")

    ub_s = 0
    for h in range(ATT1_H):
        dc, pb = h // 2, 64 * (h % 2)
        for qs in range(QS):
            ps_t = c.PSB if qs < 2 else c.PSC
            p.t(lambda e, dc=dc, pb=pb, qs=qs, ps_t=ps_t: e.matmul(
                ps_t[0:64, 512 * (qs % 2):512 * (qs % 2) + 512],
                lhsT=c.BFA[pb:pb + 64, BFA_QLB + 64 * dc:BFA_QLB + 64 * dc + 64],
                rhs=c.BFA[pb:pb + 64, BFA_KT1 + 2048 * dc + 512 * qs:
                          BFA_KT1 + 2048 * dc + 512 * qs + 512],
                start=True, stop=True),
                wait=([("S", p.mark("S")), ("V", mk_negm_v)] if qs == 0 else []),
                inc=(qs == 3))
        mkt = p.mark("T")
        mks = p.s(lambda e, h=h: e.activation(
            out=bfa(c, BFA_E3, 64, 1024), in_=c.PSB[0:64, :], func=AF.Exp,
            bias=NEGM[:, h:h + 1], accum_out=ZPART[:, 0:1]), wait=[("T", mkt)])
        mks = p.s(lambda e, h=h: e.activation(
            out=bfa(c, BFA_E3 + 1024, 64, 1024), in_=c.PSC[0:64, :], func=AF.Exp,
            bias=NEGM[:, h:h + 1], accum_out=ZPART[:, 1:2]))
        mv = p.v(lambda e, h=h: e.tensor_tensor(out=Z3[:, h:h + 1], in0=ZPART[:, 0:1],
                                                in1=ZPART[:, 1:2], op=ALU.add),
                 wait=[("S", mks)])
        tr_s = [mks, mks]
        for grp in range(2):
            po = 1024 * grp
            mkt = 0
            for j in range(8):
                s16 = 8 * grp + j
                mkt = p.t(lambda e, s16=s16, po=po, j=j: e.transpose(
                    c.PSE[:, po + 64 * j:po + 64 * j + 64],
                    in_=c.BFA[0:64, BFA_E3 + 128 * s16:BFA_E3 + 128 * s16 + 128],
                    identity=c.IDENTB[0:64, 0:64]),
                    wait=[("S", tr_s[grp])] if j == 0 else [], inc=(j == 7))
            tr_s[grp] = p.s(lambda e, grp=grp, po=po: e.copy(
                out=bfa(c, BFA_E3T + 512 * grp, 128, 512),
                in_=c.PSE[:, po:po + 512]),
                wait=[("T", mkt)])
        mk_e3t = p.mark("S")
        for s16 in range(16):
            p.t(lambda e, s16=s16, h=h: e.matmul(
                c.PSA[0:64, 0:64], lhsT=bfa(c, BFA_E3T + 64 * s16, 128, 64),
                rhs=bfa(c, BFA_V1 + 1024 * s16 + 64 * h, 128, 64),
                start=(s16 == 0), stop=(s16 == 15)),
                wait=([("S", max(mk_e3t, ub_s)), ("DY", mk_v1)] if s16 == 0 else []),
                inc=(s16 == 15))
        mkt = p.mark("T")
        ub_s = p.s(lambda e, h=h: e.copy(out=c.UBUF[0:64, 64 * h:64 * h + 64],
                                         in_=c.PSA[0:64, 0:64]), wait=[("T", mkt)])
    mk_uz_v = p.mark("V")
    p.y(lambda e: e.dma_start(out=c.dram["uz_in"].ap()[:, 0:1024], in_=c.UBUF[:, :]),
        wait=[("S", ub_s), ("V", mk_uz_v)])
    p.y(lambda e: e.dma_start(out=c.dram["uz_in"].ap()[:, 1024:1040], in_=Z3))
    mk_uzin = p.mark("DY")
    p.emit("g", lambda e: e.collective_compute(
        "AllReduce", ALU.add, replica_groups=GROUPS_RG,
        ins=[c.dram["uz_in"].ap().opt()], outs=[c.dram["uz_out"].ap().opt()]),
        wait=[("DY", mk_uzin)], inc=[("CC3", 1)])
    p.y(lambda e: e.dma_start(out=c.UBUF[:, :], in_=c.dram["uz_out"].ap()[:, 0:1024]),
        wait=[("CC3", 1)])
    p.y(lambda e: e.dma_start(out=Z3, in_=c.dram["uz_out"].ap()[:, 1024:1040]))
    mk_uzout = p.mark("DY")
    mv = p.v(lambda e: e.reciprocal(out=Z3R, in_=Z3), wait=[("DY", mk_uzout)])
    for h in range(ATT1_H):
        mv = p.v(lambda e, h=h: e.tensor_scalar(
            out=c.UBUF[0:64, 64 * h:64 * h + 64],
            in0=c.UBUF[0:64, 64 * h:64 * h + 64],
            scalar1=Z3R[:, h:h + 1], scalar2=None, op0=ALU.mult), wait=[("V", mv)])
    mk_k3v = mv

    convb_mk = [0, 0]
    pc_s = p.mark("S")
    pc_v = p.mark("V")
    for h in range(ATT1_H):
        dc, pb = h // 2, 64 * (h % 2)
        cbuf = h % 2
        p.g(lambda e, h=h, cbuf=cbuf: e.dma_start(
            out=bfa(c, CONVB[cbuf], 128, 384).rearrange("p (b j) -> p b j", b=3),
            in_=c.din["CONVT"][h].rearrange("b p j -> p b j")),
            wait=[("T", convb_mk[cbuf])], dma="DG5")
        mk_cb = p.mark("DG5")
        NMX16 = CE[:, 256:272]
        RCP16 = CE[:, 288:304]
        K1FB = 51200
        K1NB = 52224
        for tc in range(16):
            p.t(lambda e, dc=dc, pb=pb, tc=tc: e.matmul(
                c.PSA[:, 64 * tc:64 * tc + 64],
                lhsT=c.BFA[pb:pb + 64, BFA_QT1 + 2048 * dc + 128 * tc:
                           BFA_QT1 + 2048 * dc + 128 * tc + 128],
                rhs=c.BFA[pb:pb + 64, BFA_QLB + 512 + 64 * dc:BFA_QLB + 512 + 64 * dc + 64],
                start=True, stop=True),
                wait=([("S", pc_s), ("V", pc_v)] if tc == 0 else []),
                inc=(tc == 15))
        mk_k1mm = p.mark("T")
        mk_exp1 = p.s(lambda e: e.activation(
            out=bfa(c, K1FB, 128, 1024), in_=c.PSA[:, :], func=AF.Exp),
            wait=[("T", mk_k1mm)])
        sum_v = p.v(lambda e: e.tensor_reduce(
            out=RCP16, in_=bfa(c, K1FB, 128, 1024).rearrange(
                "p (s j) -> p s j", s=16),
            axis=AX.X, op=ALU.add),
            wait=[("S", mk_exp1)])
        sum_v = p.v(lambda e: e.reciprocal(out=RCP16, in_=RCP16), wait=[("V", sum_v)])
        mult_vs = []
        for tc in range(16):
            sum_v = p.v(lambda e, tc=tc: e.tensor_scalar(
                out=bfa(c, K1NB + 64 * tc, 128, 64),
                in0=bfa(c, K1FB + 64 * tc, 128, 64),
                scalar1=RCP16[:, tc:tc + 1], scalar2=None, op0=ALU.mult),
                wait=[("V", sum_v)])
            mult_vs.append(sum_v)
        k1t_s = [pc_s, pc_s]
        for grp in range(2):
            pt = 1024 * grp
            mkt = 0
            for j in range(8):
                tc = 8 * grp + j
                mkt = p.t(lambda e, tc=tc, pt=pt, j=j: e.transpose(
                    c.PSE[0:64, pt + 128 * j:pt + 128 * j + 128],
                    in_=bfa(c, K1NB + 64 * tc, 128, 64),
                    identity=c.IDENTB[:, :]),
                    wait=[("V", mult_vs[tc])] + ([("S", k1t_s[grp])] if j == 0 else []),
                    inc=(j == 7))
            k1t_s[grp] = p.s(lambda e, grp=grp, pt=pt: e.copy(
                out=c.K1T[0:64, 1024 * grp:1024 * grp + 1024],
                in_=c.PSE[0:64, pt:pt + 1024]),
                wait=[("T", mkt)])
        mk_k1t = p.mark("S")
        for qs in range(QS):
            ps_t = c.PSB if qs < 2 else c.PSC
            p.t(lambda e, h=h, qs=qs, ps_t=ps_t: e.matmul(
                ps_t[0:64, 512 * (qs % 2):512 * (qs % 2) + 512],
                lhsT=c.INV2[0:64, 64 * h:64 * h + 64],
                rhs=c.K1T[0:64, 512 * qs:512 * qs + 512], start=True, stop=True),
                wait=([("S", max(mk_k1t, mk_inv)), ("V", pc_v)] if qs == 0 else []),
                inc=(qs == 3))
        mkt = p.mark("T")
        p.s(lambda e: e.copy(out=c.ATTNT[0:64, 0:1024], in_=c.PSB[0:64, :]),
            wait=[("T", mkt)])
        mks = p.s(lambda e: e.copy(out=c.ATTNT[0:64, 1024:2048], in_=c.PSC[0:64, :]))
        mk_at = p.mark("S")
        for qs in range(QS):
            ps_t = c.PSB if qs < 2 else c.PSC
            p.t(lambda e, h=h, pb=pb, qs=qs, ps_t=ps_t: e.matmul(
                ps_t[pb:pb + 64, 512 * (qs % 2):512 * (qs % 2) + 512],
                lhsT=c.UBUF[0:64, 64 * h:64 * h + 64],
                rhs=c.ATTNT[0:64, 512 * qs:512 * qs + 512], start=True, stop=False),
                wait=([("S", mk_at), ("V", max(mk_k3v, pc_v))] if qs == 0 else []))
        for tc in range(16):
            ps_t = c.PSB if tc < 8 else c.PSC
            pco = 512 * ((tc // 4) % 2) + 128 * (tc % 4)
            for b, delta in enumerate((-1, 0, 1)):
                src = tc + delta
                if src < 0:
                    lhs = bfa(c, HALO_L + 64 * h, 128, 64)
                elif src > 15:
                    lhs = bfa(c, HALO_R + 64 * h, 128, 64)
                else:
                    lhs = bfa(c, BFA_V1 + 1024 * src + 64 * h, 128, 64)
                p.t(lambda e, lhs=lhs, b=b, cbuf=cbuf, pb=pb, ps_t=ps_t, pco=pco: e.matmul(
                    ps_t[pb:pb + 64, pco:pco + 128], lhsT=lhs,
                    rhs=bfa(c, CONVB[cbuf] + 128 * b, 128, 128),
                    start=False, stop=(b == 2)),
                    wait=([("DG5", max(mk_cb, mk_halo)), ("DY", mk_v1)]
                          if (tc == 0 and b == 0) else []),
                    inc=(tc == 15 and b == 2))
        mkt = p.mark("T")
        if h == ATT1_H - 1:
            convb_mk[cbuf] = mkt
        else:
            convb_mk[cbuf] = mkt
        for qs in range(QS):
            ps_t = c.PSB if qs < 2 else c.PSC
            mv = p.v(lambda e, dc=dc, pb=pb, qs=qs, ps_t=ps_t: e.tensor_tensor(
                out=c.X[dc][pb:pb + 64, 512 * qs:512 * qs + 512],
                in0=c.X[dc][pb:pb + 64, 512 * qs:512 * qs + 512],
                in1=ps_t[pb:pb + 64, 512 * (qs % 2):512 * (qs % 2) + 512],
                op=ALU.add), wait=[("T", mkt)])
        pc_v = p.mark("V")
        pc_s = p.mark("S")
    c.mk_xb_free = max(c.mk_xb_free, p.mark("T"))


def host_prep(inputs, sublayers=(0, 1, 2, 3, 4, 5)):
    rep = {}
    consts = np.zeros((128, 384), dtype=np.float32)
    consts[:, 0:128] = 1.0
    consts[:, 128:256] = np.eye(128, dtype=np.float32)
    consts[:, 256:264] = -1e30
    consts[:, 264] = 1e-5
    consts[:, 265] = -ATT0_SHIFT
    rep["CONSTS"] = consts
    need_ffn0 = 0 in sublayers or 2 in sublayers
    need_attn0 = 1 in sublayers
    need_moe = 3 in sublayers or 5 in sublayers
    need_attn1 = 4 in sublayers
    if need_ffn0:
        w = np.asarray(inputs["L0_ffn_weight"], dtype=np.float32)
        rep["W0T"] = np.ascontiguousarray(w.T).reshape(NC, 128, D)
        rep["DISHA"] = np.asarray(inputs["L0_ffn_disha"], dtype=np.float32)
        ssel = np.zeros((128, 64), dtype=np.float32)
        for pp in range(128):
            ssel[pp, pp % 64] = 1.0
        rep["SSEL"] = ssel
    if need_attn0:
        pq = np.asarray(inputs["L0_attn_proj_q"], dtype=np.float32)
        rep["PQ"] = np.ascontiguousarray(pq.reshape(8, NC, 128, 128))
        rep["PK"] = np.ascontiguousarray(
            np.asarray(inputs["L0_attn_proj_k"], dtype=np.float32).reshape(NC, 128, 128))
        rep["PV"] = np.ascontiguousarray(
            np.asarray(inputs["L0_attn_proj_v"], dtype=np.float32).reshape(NC, 128, 128))
        po = np.asarray(inputs["L0_attn_proj_o"], dtype=np.float32)
        rep["POT"] = np.ascontiguousarray(po.transpose(0, 2, 1))
    if need_moe:
        w1 = np.asarray(inputs["L1_ffn_w1"], dtype=np.float32)
        w3 = np.asarray(inputs["L1_ffn_w3"], dtype=np.float32)
        w2 = np.asarray(inputs["L1_ffn_w2"], dtype=np.float32)
        p1 = w1.reshape(EXPERTS, HC, 128, NC, 128).transpose(0, 1, 4, 3, 2).reshape(
            EXPERTS, HC, 128, D)
        p3 = w3.reshape(EXPERTS, HC, 128, NC, 128).transpose(0, 1, 4, 3, 2).reshape(
            EXPERTS, HC, 128, D)
        p2 = w2.reshape(EXPERTS, NC, 128, HC, 128).transpose(0, 3, 4, 1, 2).reshape(
            EXPERTS, HC, 128, D)
        rep["MS"] = np.ascontiguousarray(np.stack([p1, p3, p2], axis=2))
        rep["GWT"] = np.ascontiguousarray(
            np.asarray(inputs["L1_ffn_gate_w"], dtype=np.float32).T.reshape(NC, 128, 8))
    if need_attn1:
        wq = np.asarray(inputs["L1_attn_Wq"], dtype=np.float32)
        wk = np.asarray(inputs["L1_attn_Wk"], dtype=np.float32)
        wv = np.asarray(inputs["L1_attn_Wv"], dtype=np.float32)
        rep["WQJ"] = np.ascontiguousarray(wq.reshape(NC, 128, NC, 128).transpose(2, 0, 1, 3))
        rep["WKJ"] = np.ascontiguousarray(wk.reshape(NC, 128, NC, 128).transpose(2, 0, 1, 3))
        rep["WV"] = np.ascontiguousarray(wv.reshape(NC, 128, D))
        rep["BQ"] = np.ascontiguousarray(
            np.asarray(inputs["L1_attn_bq"], dtype=np.float32).reshape(8, 128).T)
        rep["BK"] = np.ascontiguousarray(
            np.asarray(inputs["L1_attn_bk"], dtype=np.float32).reshape(8, 128).T)
        rep["BV"] = np.asarray(inputs["L1_attn_bv"], dtype=np.float32).reshape(1, D)
        cw = np.asarray(inputs["L1_attn_conv_w"], dtype=np.float32).reshape(ATT1_H, 65)
        convt = np.zeros((ATT1_H, 3, 128, 128), dtype=np.float32)
        jj = np.arange(128)
        for h in range(ATT1_H):
            for b, delta in enumerate((-1, 0, 1)):
                for pp in range(128):
                    t = delta * 128 + pp - jj + 32
                    m = (t >= 0) & (t < 65)
                    convt[h, b, pp, m] = cw[h, t[m]]
        rep["CONVT"] = convt
        eyes = np.zeros((64, 3072), dtype=np.float32)
        for k, v in enumerate((7.0, 15.0, 13.0)):
            for h in range(ATT1_H):
                eyes[:, 1024 * k + 64 * h:1024 * k + 64 * h + 64] = v * np.eye(64)
        rep["EYES"] = eyes

    x = np.asarray(inputs["L0_attn_x"], dtype=np.float32)
    per_core = []
    for core in range(8):
        g, r = core // 4, core % 4
        d = dict(rep)
        d["xin"] = np.ascontiguousarray(x[g, Q * r:Q * (r + 1), :].T)
        if need_attn1:
            hi = np.full((128, 2), 99999, dtype=np.int32)
            if r > 0:
                hi[:, 0] = 256 * (r - 1) + 128 + np.arange(128)
            if r < 3:
                hi[:, 1] = 256 * (r + 1) + np.arange(128)
            d["HALOIDX"] = hi
        per_core.append(d)
    return per_core


def run_launch(sublayers, per_core_inputs):
    nc = build_program(tuple(sublayers))
    res = run_bass_kernel_spmd(nc, per_core_inputs, core_ids=list(range(8)))
    return [res.results[cc]["xout"] for cc in range(8)]


LAUNCH_GROUPS = [(0, 1, 2, 3, 4, 5)]


def kernel(**inputs):
    x = np.asarray(inputs["L0_attn_x"], dtype=np.float32)
    cur = [None] * 8
    for gi, grp in enumerate(LAUNCH_GROUPS):
        per_core = host_prep(inputs, grp)
        if gi > 0:
            for core in range(8):
                per_core[core]["xin"] = cur[core]
        outs = run_launch(grp, per_core)
        cur = [np.ascontiguousarray(o) for o in outs]
    y = np.empty((2, S_FULL, D), dtype=np.float32)
    for core in range(8):
        g, r = core // 4, core % 4
        y[g, Q * r:Q * (r + 1), :] = cur[core].T
    return y


# version 16 (final)
# speedup vs baseline: 1.0575x; 1.0004x over previous
import numpy as np
import ml_dtypes
from contextlib import ExitStack
import concourse.bass as bass
import concourse.mybir as mybir
from concourse.bass_utils import run_bass_kernel_spmd

F32 = mybir.dt.float32
BF16 = mybir.dt.bfloat16
AF = mybir.ActivationFunctionType
ALU = mybir.AluOpType
AX = mybir.AxisListType

Q = 2048
D = 1024
NC = 8
QS = 4
EXPERTS = 8
HID = 2816
HC = 22
ATT0_SHIFT = 40.0
ATT1_H = 16
S_FULL = 8192
GROUPS_RG = [[0, 1, 2, 3], [4, 5, 6, 7]]

ENGS = ("t", "v", "s", "g", "y")
DEBUG_TAPS = []
DEBUG_PQDUMP = False
SERIAL_DMA = True


class Plan:

    def __init__(self):
        self.ops = {e: [] for e in ENGS}
        self.cnt = {}
        self.waited = {e: {} for e in ENGS}

    def mark(self, sem):
        return self.cnt.get(sem, 0)

    def emit(self, eng, fn=None, wait=(), inc=()):
        w2 = []
        for s, v in wait:
            if v <= 0:
                continue
            assert v <= self.cnt.get(s, 0), \
                f"wait {s}>={v} exceeds planned {self.cnt.get(s, 0)}"
            if v > self.waited[eng].get(s, 0):
                self.waited[eng][s] = v
                w2.append((s, v))
        incs = tuple(inc)
        assert len(incs) <= 1
        for s, a in incs:
            self.cnt[s] = self.cnt.get(s, 0) + a
        self.ops[eng].append((fn, tuple(w2), incs))
        return self.cnt.get(incs[0][0], None) if incs else None

    def t(self, fn, wait=(), inc=False):
        return self.emit("t", fn, wait, [("T", 1)] if inc else [])

    def v(self, fn, wait=(), inc=True):
        return self.emit("v", fn, wait, [("V", 1)] if inc else [])

    def s(self, fn, wait=(), inc=True):
        return self.emit("s", fn, wait, [("S", 1)] if inc else [])

    def g(self, fn, wait=(), inc=False, dma=False):
        if dma:
            sem = "DG" if dma is True else dma
            w = list(wait) + ([(sem, self.cnt.get(sem, 0))] if SERIAL_DMA else [])
            return self.emit("g", fn, w, [(sem, 16)])
        return self.emit("g", fn, wait, [("G", 1)] if inc else [])

    def y(self, fn, wait=(), dma=True):
        if dma:
            w = list(wait) + ([("DY", self.cnt.get("DY", 0))] if SERIAL_DMA else [])
            return self.emit("y", fn, w, [("DY", 16)])
        return self.emit("y", fn, wait, [])


class Ctx:
    pass


MAP_BASE = 18944
OFF_X = MAP_BASE
OFF_FSCR = MAP_BASE + 65536
FS_SQ = OFF_FSCR
FS_TMP = OFF_FSCR + 4096
FS_RB = OFF_FSCR + 8192
FS_MRB = OFF_FSCR + 12288
FS_ACC = FS_SQ
FS_ROWS = OFF_FSCR + 16384
OFF_BFA = OFF_FSCR + 24576
OFF_CONST = OFF_BFA + 114688
TOTAL_LIMIT = 229376

BFA_XB = 0
BFA_PL = 16384
BFA_DISHA = 18432
BFA_SSEL = 19456
BFA_QT0 = 16384
BFA_KTF = 32768
BFA_VF = 40960
BFA_OT = 0
BFA_E0 = 8192
BFA_WS2 = 10240
BFA_KVSTAGE = 32768
BFA_PKPV = 36864
BFA_W0T = 49152
BFA_HB = 16384
BFA_W13 = 38912
BFA_W2R = 43008
BFA_GEB = 54272
BFA_MTMP = 56320
BFA_WVR = 24576
BFA_VSTG = 22528
BFA_QT1 = 16384
BFA_KT1 = 32768
BFA_V1 = 0
BFA_HALO = 49152
BFA_E3 = 51200
BFA_E3T = 53248
BFA_QLB = 54272
BFA_CONV = 54784
BFA_WS5 = 55552
BFA_BVB = 23552


def bfa(c, elem_off, parts, width):
    return c.BFA[0:parts, elem_off:elem_off + width]


def XBc(c, dc, sl=None):
    if sl is None:
        return bfa(c, BFA_XB + 2048 * dc, 128, Q)
    return c.BFA[0:128, BFA_XB + 2048 * dc + sl.start:BFA_XB + 2048 * dc + sl.stop]


def _sem_names():
    return ["T", "V", "S", "G", "DY", "DG", "DG2", "DG3", "DG4", "DG5"] + [f"CC{i}" for i in range(8)]


def build_program(sublayers=(0, 1, 2, 3, 4, 5)):
    nc = bass.Bass("TRN2", target_bir_lowering=False, debug=False)
    p = Plan()
    c = Ctx()
    c.nc = nc
    c.p = p
    c.mk_xb_free = 0

    need_ffn0 = 0 in sublayers or 2 in sublayers
    need_attn0 = 1 in sublayers
    need_moe = 3 in sublayers or 5 in sublayers
    need_attn1 = 4 in sublayers

    din = {}

    def DIN(name, shape):
        din[name] = nc.dram_tensor(name, list(shape), F32, kind="ExternalInput").ap()
        return din[name]

    DIN("xin", (D, Q))
    c.xout_ap = nc.dram_tensor("xout", [D, Q], F32, kind="ExternalOutput").ap()
    DIN("CONSTS", (128, 384))
    if need_ffn0:
        DIN("W0T", (NC, 128, D))
        DIN("DISHA", (64, D))
        DIN("SSEL", (128, 64))
    if need_attn0:
        if DEBUG_PQDUMP:
            t = nc.dram_tensor("dbg_pq", [8, 128, 1024], BF16, kind="ExternalOutput")
            c.dbg_pq = t.ap()
        DIN("PQ", (8, NC, 128, 128))
        DIN("PK", (NC, 128, 128))
        DIN("PV", (NC, 128, 128))
        DIN("POT", (8, 128, D))
    if need_moe:
        DIN("MS", (EXPERTS, HC, 3, 128, D))
        DIN("GWT", (NC, 128, 8))
    if need_attn1:
        DIN("WQJ", (NC, NC, 128, 128))
        DIN("WKJ", (NC, NC, 128, 128))
        DIN("WV", (NC, 128, D))
        DIN("BQ", (128, 8))
        DIN("BK", (128, 8))
        DIN("BV", (1, D))
        DIN("CONVT", (ATT1_H, 3, 128, 128))
        DIN("EYES", (64, 3072))
        din["HALOIDX"] = nc.dram_tensor("HALOIDX", [128, 2], mybir.dt.int32,
                                        kind="ExternalInput").ap()
    c.din = din

    c.dram = {}
    if need_attn0:
        c.dram["pqb"] = nc.dram_tensor("pqb", [8, 128, 1024], BF16, kind="Internal")
        c.dram["kv_in"] = nc.dram_tensor("kv_in", [128, 4096], BF16, kind="Internal")
        c.dram["kv_out"] = nc.dram_tensor("kv_out", [512, 4096], BF16, kind="Internal")
    if need_attn1:
        c.dram["v_dram"] = nc.dram_tensor("v_dram", [16, 128, D], BF16, kind="Internal")
        c.dram["lm_in"] = nc.dram_tensor("lm_in", [2048, 16], F32, kind="Internal")
        c.dram["lm_out"] = nc.dram_tensor("lm_out", [8192, 16], F32, kind="Internal")
        c.dram["mx_in"] = nc.dram_tensor("mx_in", [64, 16], F32, kind="Internal")
        c.dram["mx_out"] = nc.dram_tensor("mx_out", [64, 16], F32, kind="Internal")
        c.dram["uz_in"] = nc.dram_tensor("uz_in", [64, 1040], F32, kind="Internal")
        c.dram["uz_out"] = nc.dram_tensor("uz_out", [64, 1040], F32, kind="Internal")
        c.dram["vh_in"] = nc.dram_tensor("vh_in", [256, D], BF16, kind="Internal")
        c.dram["vh_out"] = nc.dram_tensor("vh_out", [1024, D], BF16, kind="Internal")

    with ExitStack() as stack:
        def sb_at(name, shape, dt, off):
            return nc.alloc_sbuf_tensor_at(name, list(shape), dt, offset=off)

        c.X = [sb_at(f"X{dc}", [128, Q], F32, OFF_X + 8192 * dc) for dc in range(NC)]
        c.fs_sq = sb_at("fs_sq", [128, 1024], F32, FS_SQ)
        c.fs_tmp = sb_at("fs_tmp", [128, 1024], F32, FS_TMP)
        c.fs_rb = sb_at("fs_rb", [128, 1024], F32, FS_RB)
        c.fs_mrb = sb_at("fs_mrb", [128, 1024], F32, FS_MRB)
        c.fs_acc = sb_at("fs_acc", [128, 1024], F32, FS_ACC)
        c.fs_rows = sb_at("fs_rows", [1, 2048], F32, FS_ROWS)
        c.BFA = sb_at("BFA", [128, 57344], BF16, OFF_BFA)
        c.ONES = sb_at("ONES", [128, 128], F32, OFF_CONST)
        c.IDENT = sb_at("IDENT", [128, 128], F32, OFF_CONST + 512)
        c.IDENTB = sb_at("IDENTB", [128, 128], BF16, OFF_CONST + 1024)
        c.NEG8 = sb_at("NEG8", [128, 8], F32, OFF_CONST + 1280)
        c.CBIAS = sb_at("CBIAS", [128, 8], F32, OFF_CONST + 1312)
        c.CEXTRA = sb_at("CEXTRA", [128, 624], F32, OFF_CONST + 1344)
        assert OFF_CONST + 1344 + 2496 <= TOTAL_LIMIT

        c.QLF = sb_at("QLF", [128, 1024], F32, FS_SQ)
        c.EYES_T = sb_at("EYES_T", [64, 192], F32, OFF_CONST + 1344 + 2496 + 32)
        c.SC64 = sb_at("SC64", [64, 1024], F32, FS_TMP)
        c.K1T = sb_at("K1T", [64, 2048], F32, FS_RB)
        c.QKL = sb_at("QKL", [128, 512], F32, FS_RB)
        c.QKL2 = sb_at("QKL2", [128, 512], F32, FS_RB + 2048)
        c.INV2 = sb_at("INV2", [64, 1024], F32, FS_ROWS)
        c.UBUF = sb_at("UBUF", [64, 1024], F32, FS_ROWS + 4096)
        c.ATTNT = sb_at("ATTNT", [64, 2048], F32, FS_SQ)
        c.HIDX = nc.alloc_sbuf_tensor_at("HIDX", [128, 2], mybir.dt.int32,
                                         offset=OFF_CONST + 1344 + 2496)
        c.PSA = stack.enter_context(nc.psum_tensor("PSA", [128, 1024], F32))
        c.PSB = stack.enter_context(nc.psum_tensor("PSB", [128, 1024], F32))
        c.PSC = stack.enter_context(nc.psum_tensor("PSC", [128, 1024], F32))
        c.PSE = stack.enter_context(nc.psum_tensor("PSE", [128, 2048], BF16))

        sems = {}
        for sn in _sem_names():
            sems[sn] = stack.enter_context(nc.semaphore(sn))
        c.sems = sems
        block = stack.enter_context(nc.Block())

        plan_all(c, sublayers)

        def runner(eng_key):
            def run(eng):
                for fn, waits, incs in p.ops[eng_key]:
                    for s, v in waits:
                        eng.wait_ge(sems[s], v)
                    inst = fn(eng) if fn is not None else None
                    for s, a in incs:
                        if inst is None:
                            inst = eng.nop()
                        inst.then_inc(sems[s], a)
            return run

        block.tensor(runner("t"))
        block.vector(runner("v"))
        block.scalar(runner("s"))
        block.gpsimd(runner("g"))
        block.sync(runner("y"))

    return nc


def plan_all(c, sublayers):
    p = c.p
    CON = c.din["CONSTS"]
    p.y(lambda e: e.dma_start(out=c.ONES[:, :], in_=CON[:, 0:128]))
    p.y(lambda e: e.dma_start(out=c.IDENT[:, :], in_=CON[:, 128:256]))
    p.g(lambda e: e.dma_start(out=c.IDENTB[:, :], in_=CON[:, 128:256]), dma=True)
    p.y(lambda e: e.dma_start(out=c.NEG8[:, :], in_=CON[:, 256:264]))
    p.y(lambda e: e.dma_start(out=c.CBIAS[:, 0:4], in_=CON[:, 264:268]))
    xin = c.din["xin"]
    for dc in range(NC):
        p.y(lambda e, dc=dc: e.dma_start(out=c.X[dc][:, :],
                                         in_=xin[128 * dc:128 * (dc + 1), :]))
    c.mk_ld = p.mark("DY")
    c.mk_ldg = p.mark("DG")

    if 0 in sublayers or 2 in sublayers:
        W0T = c.din["W0T"]
        for kc in range(NC):
            p.g(lambda e, kc=kc: e.dma_start(out=bfa(c, BFA_W0T + 1024 * kc, 128, 1024),
                                             in_=W0T[kc]), dma=True)
    c.mk_w0 = p.mark("DG")

    first = True
    for sl in sublayers:
        if sl in (0, 2):
            emit_ln(c, half=True, first=first)
            emit_ffn0(c)
        elif sl == 1:
            emit_ln(c, half=False, first=first)
            emit_attn0(c)
        elif sl in (3, 5):
            emit_ln(c, half=False, first=first)
            emit_moe(c)
        elif sl == 4:
            emit_ln(c, half=False, first=first)
            emit_attn1(c)
        first = False

    mkv = p.mark("V")
    for dc in range(NC):
        p.y(lambda e, dc=dc: e.dma_start(
            out=c.xout_ap[128 * dc:128 * (dc + 1), :], in_=c.X[dc][:, :]),
            wait=[("V", mkv)])
    for name, ap_fn, shape in DEBUG_TAPS:
        dout = c.nc.dram_tensor("dbg_" + name, list(shape), ap_fn(c).dtype,
                                kind="ExternalOutput").ap()
        p.y(lambda e, dout=dout, ap_fn=ap_fn: e.dma_start(out=dout, in_=ap_fn(c)),
            wait=[("V", mkv), ("T", p.mark("T")), ("S", p.mark("S"))])
    p.emit("y", None, wait=[("DY", p.mark("DY"))])


def emit_ln(c, half, first=False):
    p = c.p
    mkv_entry = p.mark("V")
    ROW_M = c.fs_rows[0:1, 0:1024]
    ROW_S = c.fs_rows[0:1, 1024:2048]
    ROW_T = c.fs_tmp[0:1, 0:1024]
    wait_ld = [("DY", c.mk_ld)] if first else []
    mk_prev_evac = getattr(c, "mk_ln_evac", 0)

    for qt in range(2):
        qlo = 1024 * qt
        mk_sumsq = 0
        for dc in range(NC):
            p.s(lambda e, dc=dc, qlo=qlo: e.activation(
                out=c.fs_sq[:, :], in_=c.X[dc][:, qlo:qlo + 1024], func=AF.Square),
                wait=wait_ld + [("V", mkv_entry), ("T", mk_sumsq)])
            mksq = p.mark("S")
            for hi in range(2):
                sl = slice(512 * hi, 512 * hi + 512)
                p.t(lambda e, dc=dc, qlo=qlo, sl=sl: e.matmul(
                    c.PSC[0:1, sl], lhsT=c.ONES[:, 0:1],
                    rhs=c.X[dc][:, qlo + sl.start:qlo + sl.stop],
                    start=(dc == 0), stop=(dc == NC - 1)),
                    wait=wait_ld + [("V", mkv_entry), ("S", mk_prev_evac)])
                mk_sumsq = p.t(lambda e, dc=dc, sl=sl: e.matmul(
                    c.PSA[0:1, sl], lhsT=c.ONES[:, 0:1], rhs=c.fs_sq[:, sl],
                    start=(dc == 0), stop=(dc == NC - 1)),
                    wait=[("S", mksq)], inc=True)
        mk_stats = p.mark("T")
        p.s(lambda e: e.activation(out=ROW_M, in_=c.PSC[0:1, 0:1024], func=AF.Copy,
                                   scale=1.0 / D), wait=[("T", mk_stats)])
        p.s(lambda e: e.activation(out=ROW_S, in_=c.PSA[0:1, 0:1024], func=AF.Copy,
                                   scale=1.0 / D))
        mks = p.mark("S")
        p.v(lambda e: e.tensor_tensor(out=ROW_T, in0=ROW_M, in1=ROW_M, op=ALU.mult),
            wait=[("S", mks)])
        p.v(lambda e: e.tensor_tensor(out=ROW_S, in0=ROW_S, in1=ROW_T, op=ALU.subtract))
        mkv2 = p.mark("V")
        p.s(lambda e: e.activation(out=ROW_S, in_=ROW_S, func=AF.Ln,
                                   bias=c.CBIAS[0:1, 0:1]),
            wait=[("V", mkv2)])
        p.s(lambda e: e.activation(out=ROW_S, in_=ROW_S, func=AF.Exp, scale=-0.5))
        if half:
            p.s(lambda e: e.activation(out=ROW_S, in_=ROW_S, func=AF.Copy, scale=0.5))
        mks2 = p.mark("S")
        p.v(lambda e: e.tensor_tensor(out=ROW_T, in0=ROW_M, in1=ROW_S, op=ALU.mult),
            wait=[("S", mks2)])
        mkv3 = p.mark("V")
        for hi in range(2):
            sl = slice(512 * hi, 512 * hi + 512)
            p.t(lambda e, sl=sl: e.matmul(c.PSA[:, sl], lhsT=c.ONES[0:1, :],
                                          rhs=ROW_S[:, sl], start=True, stop=True),
                wait=[("V", mkv3), ("S", mks2)])
            p.t(lambda e, sl=sl: e.matmul(c.PSB[:, sl], lhsT=c.ONES[0:1, :],
                                          rhs=ROW_T[:, sl], start=True, stop=True),
                inc=(hi == 1))
        mk_bc = p.mark("T")
        p.s(lambda e: e.copy(out=c.fs_rb[:, :], in_=c.PSA[:, :]), wait=[("T", mk_bc)])
        p.s(lambda e: e.copy(out=c.fs_mrb[:, :], in_=c.PSB[:, :]))
        mk_prev_evac = p.mark("S")
        for dc in range(NC):
            p.v(lambda e, dc=dc, qlo=qlo: e.tensor_tensor(
                out=c.fs_tmp[:, :], in0=c.X[dc][:, qlo:qlo + 1024],
                in1=c.fs_rb[:, :], op=ALU.mult),
                wait=[("S", mk_prev_evac), ("T", c.mk_xb_free)])
            p.v(lambda e, dc=dc, qlo=qlo: e.tensor_tensor(
                out=c.BFA[0:128, BFA_XB + 2048 * dc + qlo:BFA_XB + 2048 * dc + qlo + 1024],
                in0=c.fs_tmp[:, :], in1=c.fs_mrb[:, :], op=ALU.subtract))
    c.mk_ln_evac = mk_prev_evac
    c.mk_ln_done = p.mark("V")


def emit_ffn0(c):
    p = c.p
    mkln = c.mk_ln_done
    p.g(lambda e: e.dma_start(out=bfa(c, BFA_DISHA, 64, 1024), in_=c.din["DISHA"]),
        wait=[("T", c.mk_xb_free)], dma=True)
    p.g(lambda e: e.dma_start(out=bfa(c, BFA_SSEL, 128, 64), in_=c.din["SSEL"]),
        dma=True)
    mk_small = p.mark("DG")
    DISHAB = bfa(c, BFA_DISHA, 64, 1024)
    SSELB = bfa(c, BFA_SSEL, 128, 64)

    for qs in range(QS):
        sl = slice(512 * qs, 512 * (qs + 1))
        psl = slice(512 * (qs % 2), 512 * (qs % 2) + 512)
        for kc in range(NC):
            p.t(lambda e, kc=kc, sl=sl, psl=psl: e.matmul(
                c.PSC[0:64, psl], lhsT=SSELB, rhs=XBc(c, kc, sl),
                start=(kc == 0), stop=(kc == NC - 1)),
                wait=[("V", mkln), ("DG", max(c.mk_w0, mk_small)),
                      ("S", p.mark("S"))],
                inc=(kc == NC - 1))
        mkt = p.mark("T")
        p.s(lambda e, sl=sl, psl=psl: e.copy(
            out=c.BFA[0:64, BFA_PL + sl.start:BFA_PL + sl.stop],
            in_=c.PSC[0:64, psl]), wait=[("T", mkt)])
    mk_pool = p.mark("S")

    slot_free = [0, 0, 0, 0]
    sidx = 0
    for oc in range(NC):
        for qs in range(QS):
            sl = slice(512 * qs, 512 * (qs + 1))
            ps_t = c.PSA if (sidx % 4) < 2 else c.PSB
            psl = slice(512 * (sidx % 2), 512 * (sidx % 2) + 512)
            for kc in range(NC):
                p.t(lambda e, kc=kc, oc=oc, sl=sl, ps_t=ps_t, psl=psl: e.matmul(
                    ps_t[:, psl],
                    lhsT=bfa(c, BFA_W0T + 1024 * kc + 128 * oc, 128, 128),
                    rhs=XBc(c, kc, sl), start=(kc == 0), stop=False),
                    wait=[("V", slot_free[sidx % 4])] if kc == 0 else [])
            p.t(lambda e, oc=oc, sl=sl, psl=psl, ps_t=ps_t: e.matmul(
                ps_t[:, psl], lhsT=DISHAB[:, 128 * oc:128 * (oc + 1)],
                rhs=c.BFA[0:64, BFA_PL + sl.start:BFA_PL + sl.stop],
                start=False, stop=True),
                wait=[("S", mk_pool)], inc=True)
            mkt = p.mark("T")
            slot_free[sidx % 4] = p.v(
                lambda e, oc=oc, sl=sl, ps_t=ps_t, psl=psl: e.tensor_tensor(
                    out=c.X[oc][:, sl], in0=c.X[oc][:, sl], in1=ps_t[:, psl],
                    op=ALU.add),
                wait=[("T", mkt)])
            sidx += 1
    c.mk_xb_free = p.mark("T")


def emit_attn0(c):
    p = c.p
    mkln = c.mk_ln_done
    PQ = c.din["PQ"]
    POT = c.din["POT"]

    PKB = 36864
    PVB = 37888
    PQBUF = 38912
    POTBUF = 12288
    STG_K = 32768
    STG_V = 34816
    for h0 in range(8):
        p.g(lambda e, h0=h0: e.dma_start(
            out=c.dram["pqb"].ap()[h0].rearrange("p (k j) -> p k j", k=8),
            in_=PQ[h0].rearrange("k p j -> p k j")), dma="DG2")
    mk_pqcast = p.mark("DG2")
    p.g(lambda e: e.dma_start(out=bfa(c, PKB, 128, 1024).rearrange("p (k j) -> p k j", k=8),
                              in_=c.din["PK"].rearrange("k p j -> p k j")), dma=True)
    p.g(lambda e: e.dma_start(out=bfa(c, PVB, 128, 1024).rearrange("p (k j) -> p k j", k=8),
                              in_=c.din["PV"].rearrange("k p j -> p k j")), dma=True)
    mk_pkv = p.mark("DG")

    for qs in range(QS):
        sl = slice(512 * qs, 512 * (qs + 1))
        ps_t = c.PSA if qs % 2 == 0 else c.PSB
        psl = slice(512, 1024)
        for kc in range(NC):
            p.t(lambda e, kc=kc, sl=sl, ps_t=ps_t, psl=psl: e.matmul(
                ps_t[:, psl], lhsT=bfa(c, PKB + 128 * kc, 128, 128),
                rhs=XBc(c, kc, sl), start=(kc == 0), stop=(kc == NC - 1)),
                wait=([("DG", mk_pkv), ("S", p.mark("S")), ("V", mkln)]
                      if kc == 0 else []),
                inc=(kc == NC - 1))
        mkt = p.mark("T")
        p.s(lambda e, sl=sl, ps_t=ps_t, psl=psl: e.copy(
            out=c.BFA[0:128, STG_K + sl.start:STG_K + sl.stop], in_=ps_t[:, psl]),
            wait=[("T", mkt)])

    for tc in range(16):
        tsl = slice(128 * tc, 128 * (tc + 1))
        psl = slice(512 * (tc % 2), 512 * (tc % 2) + 128)
        for kc in range(NC):
            p.t(lambda e, kc=kc, tsl=tsl, psl=psl: e.matmul(
                c.PSC[:, psl], lhsT=XBc(c, kc, tsl),
                rhs=bfa(c, PVB + 128 * kc, 128, 128),
                start=(kc == 0), stop=(kc == NC - 1)),
                wait=[("S", p.mark("S"))] if kc == 0 else [],
                inc=(kc == NC - 1))
        mkt = p.mark("T")
        p.s(lambda e, tc=tc, psl=psl: e.copy(
            out=bfa(c, STG_V + 128 * tc, 128, 128), in_=c.PSC[:, psl]),
            wait=[("T", mkt)])
    c.mk_xb_free = p.mark("T")
    mk_stage = p.mark("S")

    kv_in = c.dram["kv_in"]
    kv_out = c.dram["kv_out"]
    p.y(lambda e: e.dma_start(out=kv_in.ap(), in_=bfa(c, STG_K, 128, 4096)),
        wait=[("S", mk_stage)])
    mk_kvout = p.mark("DY")
    p.emit("g", lambda e: e.collective_compute(
        "AllGather", ALU.bypass, replica_groups=GROUPS_RG,
        ins=[kv_in.ap().opt()], outs=[kv_out.ap().opt()]),
        wait=[("DY", mk_kvout)], inc=[("CC0", 1)])
    pq_mk = [0, 0]
    qt_mks = []
    for h in range(8):
        buf = h % 2
        p.y(lambda e, h=h, buf=buf: e.dma_start(
            out=bfa(c, PQBUF + 1024 * buf, 128, 1024),
            in_=c.dram["pqb"].ap()[h]),
            wait=[("T", pq_mk[buf]), ("DG2", mk_pqcast)])
        mk_dma = p.mark("DY")
        for qs in range(QS):
            sl = slice(512 * qs, 512 * (qs + 1))
            ps_t = c.PSA if qs % 2 == 0 else c.PSB
            psl = slice(0, 512)
            for kc in range(NC):
                p.t(lambda e, h=h, buf=buf, kc=kc, sl=sl, ps_t=ps_t, psl=psl: e.matmul(
                    ps_t[:, psl], lhsT=bfa(c, PQBUF + 1024 * buf + 128 * kc, 128, 128),
                    rhs=XBc(c, kc, sl), start=(kc == 0), stop=(kc == NC - 1)),
                    wait=([("V", mkln), ("DY", mk_dma), ("S", p.mark("S"))]
                          if kc == 0 else []),
                    inc=(kc == NC - 1))
            mkt = p.mark("T")
            if qs == QS - 1:
                pq_mk[buf] = mkt
                if DEBUG_PQDUMP:
                    p.y(lambda e, h=h, buf=buf: e.dma_start(
                        out=c.dbg_pq[h], in_=bfa(c, PQBUF + 1024 * buf, 128, 1024)),
                        wait=[("T", mkt)])
            p.s(lambda e, h=h, sl=sl, ps_t=ps_t, psl=psl: e.copy(
                out=c.BFA[0:128, BFA_QT0 + 2048 * h + sl.start:BFA_QT0 + 2048 * h + sl.stop],
                in_=ps_t[:, psl]), wait=[("T", mkt)])

    c.mk_xb_free = p.mark("T")
    for r in range(4):
        p.y(lambda e, r=r: e.dma_start(
            out=bfa(c, BFA_KTF + 2048 * r, 128, 2048),
            in_=kv_out.ap()[128 * r:128 * (r + 1), 0:2048]),
            wait=[("CC0", 1), ("DY", mk_kvout), ("T", c.mk_xb_free)])
        p.y(lambda e, r=r: e.dma_start(
            out=bfa(c, BFA_VF + 2048 * r, 128, 2048),
            in_=kv_out.ap()[128 * r:128 * (r + 1), 2048:4096]))
    mk_kv = p.mark("DY")

    mk_ot_free = [0] * 8
    pot_mk = [0, 0]
    for qt in range(2):
        qlo = 1024 * qt
        prev_rec = None
        prev_norm = 0
        for h in range(8):
            hstart_t = p.mark("T")
            hstart_s = p.mark("S")
            hstart_v = p.mark("V") if prev_rec is None else prev_rec
            mk_exp = [0, 0]
            mk_add = 0
            mk_addp = [0, 0, 0, 0]
            exp_marks = [0] * 64
            for mc in range(66):
                if mc < 64:
                    ps_t = c.PSA if mc % 2 == 0 else c.PSB
                    for hi in range(2):
                        p.t(lambda e, mc=mc, hi=hi, ps_t=ps_t, h=h, qlo=qlo: e.matmul(
                            ps_t[:, 512 * hi:512 * hi + 512],
                            lhsT=bfa(c, BFA_KTF + 128 * mc, 128, 128),
                            rhs=c.BFA[0:128, BFA_QT0 + 2048 * h + qlo + 512 * hi:
                                      BFA_QT0 + 2048 * h + qlo + 512 * hi + 512],
                            start=True, stop=True),
                            wait=([("DY", mk_kv), ("S", max(mk_exp[mc % 2], hstart_s)),
                                   ("V", hstart_v)]
                                  if hi == 0 else []),
                            inc=(hi == 1))
                    mk_l = p.mark("T")
                    mk_exp[mc % 2] = p.s(
                        lambda e, mc=mc, ps_t=ps_t: e.activation(
                            out=bfa(c, BFA_E0 + 1024 * (mc % 4), 128, 1024),
                            in_=ps_t[:, :], func=AF.Exp, bias=c.CBIAS[:, 1:2]),
                        wait=[("T", max(mk_l, c.mk_xb_free)), ("V", mk_addp[mc % 4])])
                    exp_marks[mc] = mk_exp[mc % 2]
                    if mc == 0:
                        mk_add = p.v(lambda e: e.tensor_copy(
                            out=c.fs_acc[:, :], in_=bfa(c, BFA_E0, 128, 1024)),
                            wait=[("S", mk_exp[0]), ("T", hstart_t)])
                    else:
                        mk_add = p.v(lambda e, mc=mc: e.tensor_tensor(
                            out=c.fs_acc[:, :], in0=c.fs_acc[:, :],
                            in1=bfa(c, BFA_E0 + 1024 * (mc % 4), 128, 1024), op=ALU.add),
                            wait=[("S", mk_exp[mc % 2])])
                    mk_addp[mc % 4] = mk_add
                if mc >= 2:
                    m = mc - 2
                    for hi in range(2):
                        p.t(lambda e, m=m, hi=hi: e.matmul(
                            c.PSC[:, 512 * hi:512 * hi + 512],
                            lhsT=bfa(c, BFA_VF + 128 * m, 128, 128),
                            rhs=c.BFA[0:128, BFA_E0 + 1024 * (m % 4) + 512 * hi:
                                      BFA_E0 + 1024 * (m % 4) + 512 * hi + 512],
                            start=(m == 0), stop=(m == 63)),
                            wait=(([("S", exp_marks[m])] +
                                   ([("V", prev_norm)] if m == 0 else []))
                                  if hi == 0 else []),
                            inc=(hi == 1 and m == 63))
            mk_av = p.mark("T")
            for hi in range(2):
                p.t(lambda e, hi=hi: e.matmul(
                    c.PSA[0:1, 512 * hi:512 * hi + 512], lhsT=c.ONES[:, 0:1],
                    rhs=c.fs_acc[:, 512 * hi:512 * hi + 512], start=True, stop=True),
                    wait=[("V", mk_add), ("S", mk_exp[0])], inc=(hi == 1))
            mk_sum = p.mark("T")
            ROW = c.fs_rows[0:1, 0:1024]
            p.v(lambda e: e.reciprocal(out=ROW, in_=c.PSA[0:1, 0:1024]),
                wait=[("T", mk_sum)])
            mk_rec = p.mark("V")
            for hi in range(2):
                p.t(lambda e, hi=hi: e.matmul(
                    c.PSB[:, 512 * hi:512 * hi + 512], lhsT=c.ONES[0:1, :],
                    rhs=ROW[:, 512 * hi:512 * hi + 512], start=True, stop=True),
                    wait=[("V", mk_rec), ("S", mk_exp[1])], inc=(hi == 1))
            mk_bc = p.mark("T")
            p.s(lambda e: e.copy(out=c.fs_rb[:, :], in_=c.PSB[:, :]),
                wait=[("T", mk_bc)])
            mk_rbe = p.mark("S")
            prev_norm = p.v(lambda e, h=h: e.tensor_tensor(
                out=bfa(c, BFA_OT + 1024 * h, 128, 1024), in0=c.PSC[:, :],
                in1=c.fs_rb[:, :], op=ALU.mult),
                wait=[("S", mk_rbe), ("T", max(mk_av, mk_ot_free[h]))])
            prev_rec = mk_rec
        mk_norm = p.mark("V")

        yslot_free = [0, 0]
        for dc in range(NC):
            buf = dc % 2
            p.g(lambda e, dc=dc, buf=buf: e.dma_start(
                out=bfa(c, POTBUF + 1024 * buf, 128, 1024).rearrange("p (h j) -> p h j", h=8),
                in_=POT[:, :, 128 * dc:128 * (dc + 1)].rearrange("h p j -> p h j")),
                wait=[("T", max(pot_mk[buf], c.mk_xb_free))], dma="DG2")
            mk_dma = p.mark("DG2")
            for qs in range(2):
                si = (dc * 2 + qs) % 2
                ps_t = c.PSA if si == 0 else c.PSB
                for h in range(8):
                    p.t(lambda e, dc=dc, qs=qs, h=h, buf=buf, ps_t=ps_t: e.matmul(
                        ps_t[:, 0:512],
                        lhsT=bfa(c, POTBUF + 1024 * buf + 128 * h, 128, 128),
                        rhs=bfa(c, BFA_OT + 1024 * h + 512 * qs, 128, 512),
                        start=(h == 0), stop=(h == 7)),
                        wait=[("V", max(mk_norm, yslot_free[si])), ("DG2", mk_dma)]
                        if h == 0 else [],
                        inc=(h == 7))
                mkt = p.mark("T")
                if qs == 1:
                    pot_mk[buf] = mkt
                yslot_free[si] = p.v(
                    lambda e, dc=dc, qs=qs, qlo=qlo, ps_t=ps_t: e.tensor_tensor(
                        out=c.X[dc][:, qlo + 512 * qs:qlo + 512 * qs + 512],
                        in0=c.X[dc][:, qlo + 512 * qs:qlo + 512 * qs + 512],
                        in1=ps_t[:, 0:512], op=ALU.add),
                    wait=[("T", mkt)])
        mk_y = p.mark("T")
        for h in range(8):
            mk_ot_free[h] = mk_y


def emit_moe(c):
    p = c.p
    mkln = c.mk_ln_done
    MSD = c.din["MS"]
    GWTB = bfa(c, BFA_MTMP, 128, 64)
    GTOK = c.fs_tmp[:, 0:128]
    GROW = c.fs_rows[0:1, 0:2048]
    SC = c.fs_mrb

    p.g(lambda e: e.dma_start(out=GWTB.rearrange("p (k e) -> p k e", k=8),
                              in_=c.din["GWT"].rearrange("k p e -> p k e")),
        wait=[("S", p.mark("S"))], dma="DG3")
    mk_gw = p.mark("DG3")

    ev_mks = [0, 0]
    for tc in range(16):
        psl = slice(512 * (tc % 2), 512 * (tc % 2) + 8)
        tsl = slice(128 * tc, 128 * (tc + 1))
        for kc in range(NC):
            p.t(lambda e, kc=kc, tsl=tsl, psl=psl: e.matmul(
                c.PSA[:, psl], lhsT=XBc(c, kc, tsl),
                rhs=GWTB[:, 8 * kc:8 * kc + 8],
                start=(kc == 0), stop=(kc == NC - 1)),
                wait=([("V", mkln), ("DG3", mk_gw), ("S", ev_mks[tc % 2])]
                      if kc == 0 else []),
                inc=(kc == NC - 1))
        mkt = p.mark("T")
        ev_mks[tc % 2] = p.s(lambda e, tc=tc, psl=psl: e.copy(
            out=GTOK[:, 8 * tc:8 * tc + 8], in_=c.PSA[:, psl]), wait=[("T", mkt)])
    mk_glog = p.mark("S")

    M1N = SC[:, 0:1]
    M1P = SC[:, 1:2]
    M2N = SC[:, 2:3]
    M2P = SC[:, 3:4]
    ZS = SC[:, 4:5]
    RS = SC[:, 5:6]
    SEXP = SC[:, 8:16]
    L2 = SC[:, 16:24]
    KEEP = SC[:, 24:32]
    for tc in range(16):
        G8 = GTOK[:, 8 * tc:8 * tc + 8]
        mv = p.v(lambda e, G8=G8: e.tensor_reduce(out=M1N, in_=G8, axis=AX.X,
                                                  op=ALU.max, negate=True),
                 wait=[("S", mk_glog)])
        p.s(lambda e, G8=G8: e.activation(out=SEXP, in_=G8, func=AF.Exp,
                                          bias=M1N), wait=[("V", mv)])
        mv = p.v(lambda e: e.tensor_scalar_mul(out=M1P, in0=M1N, scalar1=-1.0),
                 wait=[("V", mv)])
        mv = p.v(lambda e, G8=G8: e.tensor_scalar(out=KEEP, in0=G8, scalar1=M1P,
                                                  scalar2=None, op0=ALU.is_lt),
                 wait=[("V", mv)])
        mv = p.v(lambda e: e.tensor_scalar(out=L2, in0=KEEP, scalar1=1e30,
                                           scalar2=-1e30, op0=ALU.mult, op1=ALU.add),
                 wait=[("V", mv)])
        mv = p.v(lambda e, G8=G8: e.tensor_tensor(out=L2, in0=L2, in1=G8, op=ALU.add),
                 wait=[("V", mv)])
        mv = p.v(lambda e: e.tensor_reduce(out=M2N, in_=L2, axis=AX.X, op=ALU.max,
                                           negate=True), wait=[("V", mv)])
        mv = p.v(lambda e: e.tensor_scalar_mul(out=M2P, in0=M2N, scalar1=-1.0),
                 wait=[("V", mv)])
        mv = p.v(lambda e, G8=G8: e.tensor_scalar(out=KEEP, in0=G8, scalar1=M2P,
                                                  scalar2=None, op0=ALU.is_ge),
                 wait=[("V", mv)])
        mks = p.mark("S")
        mv = p.v(lambda e: e.tensor_tensor(out=SEXP, in0=SEXP, in1=KEEP, op=ALU.mult),
                 wait=[("S", mks), ("V", mv)])
        mv = p.v(lambda e: e.tensor_reduce(out=ZS, in_=SEXP, axis=AX.X, op=ALU.add),
                 wait=[("V", mv)])
        mv = p.v(lambda e: e.reciprocal(out=RS, in_=ZS), wait=[("V", mv)])
        p.v(lambda e, G8=G8: e.tensor_scalar(out=G8, in0=SEXP, scalar1=RS,
                                             scalar2=0.5, op0=ALU.mult, op1=ALU.mult),
            wait=[("V", mv)])
    mk_gtok = p.mark("V")

    w13_mk = [0, 0]
    w2r_mk = 0
    mk_geb_read = 0
    mk_hb_free = 0
    for ex in range(EXPERTS):
        mk_ps_a = p.mark("S")
        mk_vv = p.mark("V")
        tr_ev = [mk_ps_a, mk_ps_a]
        for tc in range(16):
            po = 128 * (tc % 2)
            po = 512 * (tc % 2)
            p.t(lambda e, ex=ex, tc=tc, po=po: e.transpose(
                c.PSA[0:1, po:po + 128],
                in_=GTOK[:, 8 * tc + ex:8 * tc + ex + 1], identity=c.IDENT[:, :]),
                wait=[("V", max(mk_gtok, mk_vv)), ("S", tr_ev[tc % 2])], inc=True)
            mkt = p.mark("T")
            tr_ev[tc % 2] = p.s(lambda e, tc=tc, po=po: e.copy(
                out=GROW[:, 128 * tc:128 * (tc + 1)],
                in_=c.PSA[0:1, po:po + 128]), wait=[("T", mkt)])
        mk_grow = p.mark("S")
        bc_ev = [mk_grow, mk_grow]
        for hi in range(4):
            p.t(lambda e, hi=hi: e.matmul(
                c.PSA[:, 512 * (hi % 2):512 * (hi % 2) + 512], lhsT=c.ONES[0:1, :],
                rhs=GROW[:, 512 * hi:512 * hi + 512], start=True, stop=True),
                wait=[("S", max(mk_grow, bc_ev[hi % 2]))], inc=True)
            mkt = p.mark("T")
            bc_ev[hi % 2] = p.s(lambda e, hi=hi: e.copy(
                out=bfa(c, BFA_GEB + 512 * hi, 128, 512),
                in_=c.PSA[:, 512 * (hi % 2):512 * (hi % 2) + 512]),
                wait=[("T", mkt), ("V", mk_geb_read)])
        mk_geb = p.mark("S")

        for half in range(2):
            hcs = list(range(11 * half, 11 * half + 11))
            silu_mks = [0, 0]
            t3_mks = [0, 0]
            mtmp_v = [0, 0, 0, 0]
            pv0 = p.mark("V")
            sidx = 0
            for i, hc in enumerate(hcs):
                buf = hc % 2
                p.g(lambda e, ex=ex, hc=hc, buf=buf: e.dma_start(
                    out=bfa(c, BFA_W13 + 2048 * buf, 128, 2048).rearrange(
                        "p (m f) -> p m f", m=2),
                    in_=MSD[ex, hc, 0:2].rearrange("m p f -> p m f")),
                    wait=[("T", w13_mk[buf])], dma="DG3")
                mk_dma = p.mark("DG3")
                p.g(lambda e, ex=ex, hc=hc, i=i: e.dma_start(
                    out=bfa(c, BFA_W2R + 1024 * i, 128, 1024),
                    in_=MSD[ex, hc, 2]),
                    wait=[("T", w2r_mk)], dma="DG4")
                for sl4 in range(4):
                    sl = slice(512 * sl4, 512 * (sl4 + 1))
                    par = sidx % 2
                    psl = slice(512 * par, 512 * par + 512)
                    for kc in range(NC):
                        p.t(lambda e, buf=buf, kc=kc, sl=sl, psl=psl: e.matmul(
                            c.PSA[:, psl], lhsT=bfa(c, BFA_W13 + 2048 * buf + 128 * kc, 128, 128),
                            rhs=XBc(c, kc, sl), start=(kc == 0), stop=(kc == NC - 1)),
                            wait=([("DG3", mk_dma), ("S", silu_mks[par]), ("V", mkln)]
                                  if kc == 0 else []))
                    for kc in range(NC):
                        p.t(lambda e, buf=buf, kc=kc, sl=sl, psl=psl: e.matmul(
                            c.PSB[:, psl],
                            lhsT=bfa(c, BFA_W13 + 2048 * buf + 1024 + 128 * kc, 128, 128),
                            rhs=XBc(c, kc, sl), start=(kc == 0), stop=(kc == NC - 1)),
                            wait=[("V", max(t3_mks[par], pv0))] if kc == 0 else [],
                            inc=(kc == NC - 1))
                    mkt = p.mark("T")
                    ms = sidx % 4
                    silu_mks[par] = p.s(lambda e, psl=psl: e.activation(
                        out=bfa(c, BFA_MTMP, 128, 512), in_=c.PSA[:, psl],
                        func=AF.Silu), wait=[("T", mkt), ("V", mtmp_v[0])])
                    t3_mks[par] = p.v(lambda e, sl4=sl4, psl=psl: e.tensor_tensor(
                        out=bfa(c, BFA_MTMP + 512, 128, 512), in0=c.PSB[:, psl],
                        in1=bfa(c, BFA_GEB + 512 * sl4, 128, 512), op=ALU.mult),
                        wait=[("T", mkt), ("S", mk_geb)])
                    mtmp_v[0] = p.v(lambda e, i=i, sl4=sl4: e.tensor_tensor(
                        out=bfa(c, BFA_HB + 2048 * i + 512 * sl4, 128, 512),
                        in0=bfa(c, BFA_MTMP, 128, 512),
                        in1=bfa(c, BFA_MTMP + 512, 128, 512), op=ALU.mult),
                        wait=[("S", silu_mks[par]), ("T", mk_hb_free)])
                    sidx += 1
                w13_mk[buf] = p.mark("T")
            mk_w2dma = p.mark("DG4")
            mk_h = p.mark("V")

            oslot_mk = [0, 0]
            ti = 0
            for dc in range(NC):
                for qh in range(2):
                    ps_t = c.PSB if ti % 2 == 0 else c.PSC
                    for i, hc in enumerate(hcs):
                        for s2 in range(2):
                            p.t(lambda e, dc=dc, qh=qh, i=i, s2=s2, ps_t=ps_t: e.matmul(
                                ps_t[:, 512 * s2:512 * s2 + 512],
                                lhsT=bfa(c, BFA_W2R + 1024 * i + 128 * dc, 128, 128),
                                rhs=bfa(c, BFA_HB + 2048 * i + 1024 * qh + 512 * s2,
                                        128, 512),
                                start=(i == 0), stop=(i == 10)),
                                wait=([("DG4", mk_w2dma), ("V", max(mk_h, oslot_mk[ti % 2]))]
                                      if (i == 0 and s2 == 0) else []),
                                inc=(i == 10 and s2 == 1))
                    mkt = p.mark("T")
                    oslot_mk[ti % 2] = p.v(
                        lambda e, dc=dc, qh=qh, ps_t=ps_t: e.tensor_tensor(
                            out=c.X[dc][:, 1024 * qh:1024 * qh + 1024],
                            in0=c.X[dc][:, 1024 * qh:1024 * qh + 1024],
                            in1=ps_t[:, :], op=ALU.add),
                        wait=[("T", mkt)])
                    ti += 1
            w2r_mk = p.mark("T")
            mk_hb_free = w2r_mk
        mk_geb_read = max(t3_mks)
    c.mk_xb_free = p.mark("T")


ATT1_SCALE = 0.35355339059327373


def emit_attn1(c):
    from concourse.bass_core_rust import IndirectOffsetOnAxis
    p = c.p
    mkln = c.mk_ln_done
    CE = c.CEXTRA
    BQ_, BK_ = CE[:, 0:8], CE[:, 8:16]
    BQS, BKS = CE[:, 16:24], CE[:, 24:32]
    MX = CE[0:64, 32:48]
    Z3 = CE[0:64, 48:64]
    MXG = CE[0:64, 64:80]
    NEGM = CE[0:64, 96:112]
    Z3R = CE[0:64, 112:128]
    ZPART = CE[0:64, 116 + 12:116 + 14]
    ZPART = CE[0:64, 126:128]
    K1F = CE[:, 128:192]
    K1N = CE[:, 192:256]
    K1ROW = CE[:, 256:260]
    NM64 = CE[0:64, 260:262]
    SM64 = CE[0:64, 262:264]
    RC64 = CE[0:64, 264:266]
    CS64 = CE[0:64, 266:268]
    RSC = CE[0:64, 268:270]

    WQ_BUF = [49152, 50176]
    VSTG = [22528, 21504]
    CONVB = [55296, 55680]
    HALO_L, HALO_R = 49152, 50176

    p.y(lambda e: e.dma_start(out=BQ_, in_=c.din["BQ"]))
    p.y(lambda e: e.dma_start(out=BK_, in_=c.din["BK"]))
    p.y(lambda e: e.dma_start(out=c.fs_rows[0:1, 0:1024], in_=c.din["BV"]),
        wait=[("V", mkln)])
    p.y(lambda e: e.dma_start(out=c.HIDX[:, :], in_=c.din["HALOIDX"]))
    mk_bload = p.mark("DY")
    p.g(lambda e: e.dma_start(out=c.BFA[0:64, 51200:54272], in_=c.din["EYES"]),
        wait=[("T", c.mk_xb_free)], dma="DG5")
    mk_eyesb = p.mark("DG5")
    p.s(lambda e: e.activation(out=BQS, in_=BQ_, func=AF.Copy, scale=ATT1_SCALE),
        wait=[("DY", mk_bload)])
    p.s(lambda e: e.activation(out=BKS, in_=BK_, func=AF.Copy, scale=ATT1_SCALE))
    mk_bscale = p.mark("S")
    for kc in range(NC):
        p.g(lambda e, kc=kc: e.dma_start(out=bfa(c, BFA_WVR + 1024 * kc, 128, 1024),
                                         in_=c.din["WV"][kc]), dma="DG5")
    mk_wv = p.mark("DG5")
    for hi in range(2):
        p.t(lambda e, hi=hi: e.matmul(
            c.PSA[:, 512 * hi:512 * hi + 512], lhsT=c.ONES[0:1, :],
            rhs=c.fs_rows[0:1, 512 * hi:512 * hi + 512], start=True, stop=True),
            wait=[("DY", mk_bload), ("S", p.mark("S"))], inc=(hi == 1))
    mk_bvb = p.mark("T")
    p.s(lambda e: e.copy(out=bfa(c, BFA_BVB, 128, 1024), in_=c.PSA[:, :]),
        wait=[("T", mk_bvb)])
    mk_bvbe = p.mark("S")

    vstg_mk = [0, 0]
    for tc in range(16):
        tsl = slice(128 * tc, 128 * (tc + 1))
        ps_t = c.PSB if tc % 2 == 0 else c.PSC
        for jh in range(2):
            for kc in range(NC):
                p.t(lambda e, kc=kc, tsl=tsl, jh=jh, ps_t=ps_t: e.matmul(
                    ps_t[:, 512 * jh:512 * jh + 512], lhsT=XBc(c, kc, tsl),
                    rhs=bfa(c, BFA_WVR + 1024 * kc + 512 * jh, 128, 512),
                    start=(kc == 0), stop=(kc == NC - 1)),
                    wait=([("V", max(mkln, vstg_mk[tc % 2])), ("DG5", mk_wv)]
                          if kc == 0 else []),
                    inc=(kc == NC - 1 and jh == 1))
        mkt = p.mark("T")
        vstg_mk[tc % 2] = p.v(lambda e, tc=tc, ps_t=ps_t: e.tensor_tensor(
            out=bfa(c, VSTG[tc % 2], 128, 1024), in0=ps_t[:, :],
            in1=bfa(c, BFA_BVB, 128, 1024), op=ALU.add),
            wait=[("T", mkt), ("S", mk_bvbe), ("DY", p.mark("DY"))])
        mkv = p.mark("V")
        p.y(lambda e, tc=tc: e.dma_start(out=c.dram["v_dram"].ap()[tc],
                                         in_=bfa(c, VSTG[tc % 2], 128, 1024)),
            wait=[("V", mkv)])
        if tc == 0:
            p.y(lambda e: e.dma_start(out=c.dram["vh_in"].ap()[0:128, :],
                                      in_=bfa(c, VSTG[0], 128, 1024)))
        if tc == 15:
            p.y(lambda e: e.dma_start(out=c.dram["vh_in"].ap()[128:256, :],
                                      in_=bfa(c, VSTG[1], 128, 1024)))
    mk_vdma = p.mark("DY")
    p.emit("g", lambda e: e.collective_compute(
        "AllGather", ALU.bypass, replica_groups=GROUPS_RG,
        ins=[c.dram["vh_in"].ap().opt()], outs=[c.dram["vh_out"].ap().opt()]),
        wait=[("DY", mk_vdma)], inc=[("CC4", 1)])

    wbuf_mk = [0, 0]
    for z, (WJ, BS, base) in enumerate(
            [("WQJ", BQS, BFA_QT1), ("WKJ", BKS, BFA_KT1)]):
        for jc in range(NC):
            buf = jc % 2
            p.g(lambda e, WJ=WJ, jc=jc, buf=buf: e.dma_start(
                out=bfa(c, WQ_BUF[buf], 128, 1024).rearrange("p (k j) -> p k j", k=8),
                in_=c.din[WJ][jc].rearrange("k p j -> p k j")),
                wait=[("T", wbuf_mk[buf])], dma="DG5")
            mk_dma = p.mark("DG5")
            for qs in range(QS):
                sl = slice(512 * qs, 512 * (qs + 1))
                ps_t = c.PSA if qs % 2 == 0 else c.PSB
                for kc in range(NC):
                    p.t(lambda e, buf=buf, kc=kc, sl=sl, ps_t=ps_t: e.matmul(
                        ps_t[:, 0:512], lhsT=bfa(c, WQ_BUF[buf] + 128 * kc, 128, 128),
                        rhs=XBc(c, kc, sl), start=(kc == 0), stop=(kc == NC - 1)),
                        wait=([("DG5", mk_dma), ("S", p.mark("S")), ("V", mkln)]
                              if kc == 0 else []),
                        inc=(kc == NC - 1))
                mkt = p.mark("T")
                if qs == QS - 1:
                    wbuf_mk[buf] = mkt
                p.s(lambda e, base=base, jc=jc, sl=sl, ps_t=ps_t, BS=BS: e.activation(
                    out=c.BFA[0:128, base + 2048 * jc + sl.start:base + 2048 * jc + sl.stop],
                    in_=ps_t[:, 0:512], func=AF.Identity, scale=ATT1_SCALE,
                    bias=BS[:, jc:jc + 1]),
                    wait=[("T", mkt), ("DY", mk_vdma if z == 0 else 0),
                          ("S", mk_bscale)])
    c.mk_xb_free = p.mark("T")
    mk_qk = p.mark("S")

    for tc in range(16):
        p.y(lambda e, tc=tc: e.dma_start(out=bfa(c, BFA_V1 + 1024 * tc, 128, 1024),
                                         in_=c.dram["v_dram"].ap()[tc]),
            wait=[("T", c.mk_xb_free), ("S", mk_qk), ("DY", mk_vdma)])
    p.v(lambda e: e.memset(bfa(c, HALO_L, 128, 1024), 0.0),
        wait=[("T", c.mk_xb_free)])
    p.v(lambda e: e.memset(bfa(c, HALO_R, 128, 1024), 0.0))
    mk_hz = p.mark("V")
    p.g(lambda e: e.indirect_dma_start(
        out=bfa(c, HALO_L, 128, 1024), out_offset=None,
        in_=c.dram["vh_out"].ap(),
        in_offset=IndirectOffsetOnAxis(c.HIDX[:, 0:1], 0),
        bounds_check=1023, oob_is_err=False),
        wait=[("CC4", 1), ("V", mk_hz), ("DY", mk_bload)], dma="DG5")
    p.g(lambda e: e.indirect_dma_start(
        out=bfa(c, HALO_R, 128, 1024), out_offset=None,
        in_=c.dram["vh_out"].ap(),
        in_offset=IndirectOffsetOnAxis(c.HIDX[:, 1:2], 0),
        bounds_check=1023, oob_is_err=False), dma="DG5")
    mk_v1 = p.mark("DY")
    mk_halo = p.mark("DG5")

    mv = 0
    for z, base in enumerate([BFA_QT1, BFA_KT1]):
        for dc in range(NC):
            mv = p.v(lambda e, z=z, dc=dc, base=base: e.tensor_reduce(
                out=c.QKL[:, 128 * z + 16 * dc:128 * z + 16 * dc + 16],
                in_=c.BFA[0:128, base + 2048 * dc:
                          base + 2048 * dc + 2048].rearrange(
                    "p (s j) -> p s j", s=16),
                axis=AX.X, op=ALU.add),
                wait=[("S", mk_qk)])
    p.s(lambda e: e.activation(out=c.QKL2[:, 0:256], in_=c.QKL[:, 0:256], func=AF.Copy,
                               scale=1.0 / 128.0), wait=[("V", mv)])
    mk_l = p.mark("S")
    p.y(lambda e: e.dma_start(
        out=c.dram["lm_in"].ap().rearrange("(p z c) l -> p z c l", p=128, z=2, c=8),
        in_=c.QKL2[:, 0:256].rearrange("p (z c l) -> p z c l", z=2, c=8)),
        wait=[("S", mk_l)])
    mk_lmin = p.mark("DY")
    p.emit("g", lambda e: e.collective_compute(
        "AllGather", ALU.bypass, replica_groups=GROUPS_RG,
        ins=[c.dram["lm_in"].ap().opt()], outs=[c.dram["lm_out"].ap().opt()]),
        wait=[("DY", mk_lmin)], inc=[("CC1", 1)])
    for r in range(4):
        p.y(lambda e, r=r: e.dma_start(
            out=c.QLF[:, :].rearrange("p (z c l) -> p z c l", z=2, c=8)[
                :, :, :, 16 * r:16 * (r + 1)],
            in_=c.dram["lm_out"].ap().rearrange(
                "(r p z c) l -> r p z c l", r=4, p=128, z=2, c=8)[r]),
            wait=[("CC1", 1)])
    mk_qlf = p.mark("DY")
    p.s(lambda e: e.copy(out=bfa(c, BFA_QLB, 128, 1024), in_=c.QLF[:, :]),
        wait=[("DY", mk_qlf)])
    mk_qlb = p.mark("S")

    l3_done_s = 0
    for h in range(ATT1_H):
        dc, pb = h // 2, 64 * (h % 2)
        for qs in range(QS):
            ps_t = c.PSB if qs < 2 else c.PSC
            p.t(lambda e, h=h, dc=dc, pb=pb, qs=qs, ps_t=ps_t: e.matmul(
                ps_t[0:64, 512 * (qs % 2):512 * (qs % 2) + 512],
                lhsT=c.BFA[pb:pb + 64, BFA_QLB + 64 * dc:BFA_QLB + 64 * dc + 64],
                rhs=c.BFA[pb:pb + 64, BFA_KT1 + 2048 * dc + 512 * qs:
                          BFA_KT1 + 2048 * dc + 512 * qs + 512],
                start=True, stop=True),
                wait=([("S", max(mk_qlb, l3_done_s)), ("V", p.mark("V"))]
                      if qs == 0 else []),
                inc=(qs == 3))
        mkt = p.mark("T")
        mv = p.v(lambda e: e.tensor_reduce(out=ZPART[:, 0:1], in_=c.PSB[0:64, :],
                                           axis=AX.X, op=ALU.max), wait=[("T", mkt)])
        mv = p.v(lambda e: e.tensor_reduce(out=ZPART[:, 1:2], in_=c.PSC[0:64, :],
                                           axis=AX.X, op=ALU.max), wait=[("V", mv)])
        mv = p.v(lambda e, h=h: e.tensor_tensor(out=MX[:, h:h + 1], in0=ZPART[:, 0:1],
                                                in1=ZPART[:, 1:2], op=ALU.max),
                 wait=[("V", mv)])
        l3_done_s = p.s(lambda e: e.copy(out=ZPART[:, 0:1], in_=ZPART[:, 0:1]),
                        wait=[("V", mv)])
    p.y(lambda e: e.dma_start(out=c.dram["mx_in"].ap(), in_=MX), wait=[("S", l3_done_s)])
    p.emit("g", lambda e: e.collective_compute(
        "AllReduce", ALU.max, replica_groups=GROUPS_RG,
        ins=[c.dram["mx_in"].ap().opt()], outs=[c.dram["mx_out"].ap().opt()]),
        wait=[("DY", p.mark("DY"))], inc=[("CC2", 1)])
    p.y(lambda e: e.dma_start(out=MXG, in_=c.dram["mx_out"].ap()), wait=[("CC2", 1)])
    mk_mxg = p.mark("DY")
    mv = p.v(lambda e: e.tensor_scalar_mul(out=NEGM, in0=MXG, scalar1=-1.0),
             wait=[("DY", mk_mxg)])
    mk_negm_v = mv

    TB_K2E = c.UBUF[0:64, 0:1024]
    TB_K2N = c.fs_tmp[0:64, 0:1024]
    TB_K2NT = c.fs_rb[0:64, 0:1024]
    TB_V = c.fs_mrb[0:64, 0:1024]
    TB_KVS = c.fs_sq[0:64, 0:1024]
    TB_KVT = c.UBUF[0:64, 0:1024]
    TB_A = c.INV2[0:64, 0:1024]
    NMB = CE[0:64, 320:336]
    SMB = CE[0:64, 336:352]
    RCB = CE[0:64, 352:368]
    RSCB = CE[0:64, 368:384]
    CSB = CE[0:1, 384:400]
    EYE_B = [c.BFA[0:64, 51200 + 1024 * k:51200 + 1024 * (k + 1)] for k in range(3)]

    def seg3(ap):
        return ap.rearrange("p (s j) -> p s j", s=16)

    mkt = 0
    HORD = [h for h in range(ATT1_H) if h % 2 == 0] + \
        [h for h in range(ATT1_H) if h % 2 == 1]
    for hi, h in enumerate(HORD):
        dc, pb = h // 2, 64 * (h % 2)
        mkt = p.t(lambda e, h=h, dc=dc, pb=pb: e.matmul(
            c.PSA[0:64, 64 * h:64 * h + 64],
            lhsT=c.QLF[pb:pb + 64, 64 * dc:64 * dc + 64],
            rhs=c.QLF[pb:pb + 64, 512 + 64 * dc:512 + 64 * dc + 64],
            start=True, stop=True),
            wait=([("DY", mk_qlf), ("S", p.mark("S"))] if hi == 0 else []),
            inc=(hi == ATT1_H - 1))
    v_nmb = p.v(lambda e: e.tensor_reduce(
        out=NMB, in_=seg3(c.PSA[0:64, 0:1024]), axis=AX.X, op=ALU.max, negate=True),
        wait=[("T", mkt)])
    mks = 0
    for h in range(ATT1_H):
        mks = p.s(lambda e, h=h: e.activation(
            out=TB_K2E[:, 64 * h:64 * h + 64], in_=c.PSA[0:64, 64 * h:64 * h + 64],
            func=AF.Exp, bias=NMB[:, h:h + 1]),
            wait=[("V", v_nmb)] if h == 0 else [])
    mv = p.v(lambda e: e.tensor_reduce(
        out=SMB, in_=seg3(TB_K2E), axis=AX.X, op=ALU.add), wait=[("S", mks)])
    mv = p.v(lambda e: e.reciprocal(out=RCB, in_=SMB), wait=[("V", mv)])
    for h in range(ATT1_H):
        mv = p.v(lambda e, h=h: e.tensor_scalar(
            out=TB_K2N[:, 64 * h:64 * h + 64], in0=TB_K2E[:, 64 * h:64 * h + 64],
            scalar1=RCB[:, h:h + 1], scalar2=None, op0=ALU.mult),
            wait=[("V", mv)])
    for hi in range(2):
        mkt = p.t(lambda e, hi=hi: e.matmul(
            c.PSB[0:1, 512 * hi:512 * hi + 512], lhsT=c.ONES[0:64, 0:1],
            rhs=TB_K2N[:, 512 * hi:512 * hi + 512], start=True, stop=True),
            wait=[("V", mv)] if hi == 0 else [], inc=(hi == 1))
    mv2 = p.v(lambda e: e.tensor_reduce(
        out=CSB, in_=seg3(c.PSB[0:1, 0:1024]), axis=AX.X, op=ALU.max),
        wait=[("T", mkt)])
    mv2 = p.v(lambda e: e.reciprocal(out=CSB, in_=CSB), wait=[("V", mv2)])
    mkt = p.t(lambda e: e.matmul(c.PSA[0:64, 0:16], lhsT=c.ONES[0:1, 0:64], rhs=CSB,
                                 start=True, stop=True),
              wait=[("V", mv2), ("S", mks)], inc=True)
    ev = p.s(lambda e: e.copy(out=RSCB, in_=c.PSA[0:64, 0:16]), wait=[("T", mkt)])
    for h in range(ATT1_H):
        mkt = p.t(lambda e, h=h: e.transpose(
            c.PSA[0:64, 64 * h:64 * h + 64], in_=TB_K2N[:, 64 * h:64 * h + 64],
            identity=c.IDENT[0:64, 0:64]),
            wait=[("S", ev)] if h == 0 else [], inc=(h == ATT1_H - 1))
    ev = p.s(lambda e: e.copy(out=TB_K2NT, in_=c.PSA[0:64, 0:1024]), wait=[("T", mkt)])
    mv = 0
    for h in range(ATT1_H):
        mv = p.v(lambda e, h=h: e.tensor_scalar(
            out=TB_V[:, 64 * h:64 * h + 64], in0=TB_K2NT[:, 64 * h:64 * h + 64],
            scalar1=RSCB[:, h:h + 1], scalar2=None, op0=ALU.mult),
            wait=[("S", ev)] if h == 0 else [])
    v_in = mv
    ev_i = ev

    def batch_pe(dst_ps, lhs_t, rhs_t, first_wait, trans=False):
        mkt = 0
        for h in range(ATT1_H):
            sl = slice(64 * h, 64 * h + 64)
            if trans:
                mkt = p.t(lambda e, sl=sl, dst_ps=dst_ps, lhs_t=lhs_t: e.transpose(
                    dst_ps[0:64, sl], in_=lhs_t[:, sl],
                    identity=c.IDENT[0:64, 0:64]),
                    wait=first_wait if h == 0 else [], inc=(h == ATT1_H - 1))
            else:
                mkt = p.t(lambda e, sl=sl, dst_ps=dst_ps, lhs_t=lhs_t, rhs_t=rhs_t:
                          e.matmul(dst_ps[0:64, sl], lhsT=lhs_t[:, sl],
                                   rhs=rhs_t[:, sl], start=True, stop=True),
                          wait=first_wait if h == 0 else [],
                          inc=(h == ATT1_H - 1))
        return mkt

    for it in range(6):
        mkt = batch_pe(c.PSA, TB_K2NT, TB_V, [("S", ev_i), ("V", v_in)])
        ev_a = p.s(lambda e: e.copy(out=TB_KVS, in_=c.PSA[0:64, 0:1024]),
                   wait=[("T", mkt)])
        v_c = p.v(lambda e: e.tensor_tensor(out=TB_A, in0=EYE_B[0], in1=TB_KVS,
                                            op=ALU.subtract),
                  wait=[("S", ev_a), ("DG5", mk_eyesb)])
        mkt = batch_pe(c.PSB, TB_KVS, None, [("S", ev_a)], trans=True)
        ev_b = p.s(lambda e: e.copy(out=TB_KVT, in_=c.PSB[0:64, 0:1024]),
                   wait=[("T", mkt)])
        mkt = batch_pe(c.PSA, TB_KVT, TB_A, [("S", ev_b), ("V", v_c)])
        v_e = p.v(lambda e: e.tensor_tensor(out=TB_A, in0=EYE_B[1],
                                            in1=c.PSA[0:64, 0:1024],
                                            op=ALU.subtract), wait=[("T", mkt)])
        mkt = batch_pe(c.PSB, TB_V, None, [("S", ev_b)], trans=True)
        ev_vt = p.s(lambda e: e.copy(out=TB_KVS, in_=c.PSB[0:64, 0:1024]),
                    wait=[("T", mkt), ("V", v_c)])
        mkt = batch_pe(c.PSA, TB_KVT, TB_A, [("V", v_e)])
        v_g = p.v(lambda e: e.tensor_tensor(out=TB_A, in0=EYE_B[2],
                                            in1=c.PSA[0:64, 0:1024],
                                            op=ALU.subtract), wait=[("T", mkt)])
        mkt = batch_pe(c.PSB, TB_KVS, TB_A, [("S", ev_vt), ("V", v_g)])
        dst = c.INV2[0:64, 0:1024] if it == 5 else TB_V
        ev_i = p.s(lambda e, dst=dst: e.activation(
            out=dst, in_=c.PSB[0:64, 0:1024], func=AF.Copy, scale=0.25),
            wait=[("T", mkt)])
        v_in = v_g
    mk_inv = p.mark("S")

    ub_s = 0
    for h in range(ATT1_H):
        dc, pb = h // 2, 64 * (h % 2)
        for qs in range(QS):
            ps_t = c.PSB if qs < 2 else c.PSC
            p.t(lambda e, dc=dc, pb=pb, qs=qs, ps_t=ps_t: e.matmul(
                ps_t[0:64, 512 * (qs % 2):512 * (qs % 2) + 512],
                lhsT=c.BFA[pb:pb + 64, BFA_QLB + 64 * dc:BFA_QLB + 64 * dc + 64],
                rhs=c.BFA[pb:pb + 64, BFA_KT1 + 2048 * dc + 512 * qs:
                          BFA_KT1 + 2048 * dc + 512 * qs + 512],
                start=True, stop=True),
                wait=([("S", p.mark("S")), ("V", mk_negm_v)] if qs == 0 else []),
                inc=(qs == 3))
        mkt = p.mark("T")
        mks = p.s(lambda e, h=h: e.activation(
            out=bfa(c, BFA_E3, 64, 1024), in_=c.PSB[0:64, :], func=AF.Exp,
            bias=NEGM[:, h:h + 1], accum_out=ZPART[:, 0:1]), wait=[("T", mkt)])
        mks = p.s(lambda e, h=h: e.activation(
            out=bfa(c, BFA_E3 + 1024, 64, 1024), in_=c.PSC[0:64, :], func=AF.Exp,
            bias=NEGM[:, h:h + 1], accum_out=ZPART[:, 1:2]))
        mv = p.v(lambda e, h=h: e.tensor_tensor(out=Z3[:, h:h + 1], in0=ZPART[:, 0:1],
                                                in1=ZPART[:, 1:2], op=ALU.add),
                 wait=[("S", mks)])
        tr_s = [mks, mks]
        for grp in range(2):
            po = 1024 * grp
            mkt = 0
            for j in range(8):
                s16 = 8 * grp + j
                mkt = p.t(lambda e, s16=s16, po=po, j=j: e.transpose(
                    c.PSE[:, po + 64 * j:po + 64 * j + 64],
                    in_=c.BFA[0:64, BFA_E3 + 128 * s16:BFA_E3 + 128 * s16 + 128],
                    identity=c.IDENTB[0:64, 0:64]),
                    wait=[("S", tr_s[grp])] if j == 0 else [], inc=(j == 7))
            tr_s[grp] = p.s(lambda e, grp=grp, po=po: e.copy(
                out=bfa(c, BFA_E3T + 512 * grp, 128, 512),
                in_=c.PSE[:, po:po + 512]),
                wait=[("T", mkt)])
        mk_e3t = p.mark("S")
        for s16 in range(16):
            p.t(lambda e, s16=s16, h=h: e.matmul(
                c.PSA[0:64, 0:64], lhsT=bfa(c, BFA_E3T + 64 * s16, 128, 64),
                rhs=bfa(c, BFA_V1 + 1024 * s16 + 64 * h, 128, 64),
                start=(s16 == 0), stop=(s16 == 15)),
                wait=([("S", max(mk_e3t, ub_s)), ("DY", mk_v1)] if s16 == 0 else []),
                inc=(s16 == 15))
        mkt = p.mark("T")
        ub_s = p.s(lambda e, h=h: e.copy(out=c.UBUF[0:64, 64 * h:64 * h + 64],
                                         in_=c.PSA[0:64, 0:64]), wait=[("T", mkt)])
    mk_uz_v = p.mark("V")
    p.y(lambda e: e.dma_start(out=c.dram["uz_in"].ap()[:, 0:1024], in_=c.UBUF[:, :]),
        wait=[("S", ub_s), ("V", mk_uz_v)])
    p.y(lambda e: e.dma_start(out=c.dram["uz_in"].ap()[:, 1024:1040], in_=Z3))
    mk_uzin = p.mark("DY")
    p.emit("g", lambda e: e.collective_compute(
        "AllReduce", ALU.add, replica_groups=GROUPS_RG,
        ins=[c.dram["uz_in"].ap().opt()], outs=[c.dram["uz_out"].ap().opt()]),
        wait=[("DY", mk_uzin)], inc=[("CC3", 1)])
    p.y(lambda e: e.dma_start(out=c.UBUF[:, :], in_=c.dram["uz_out"].ap()[:, 0:1024]),
        wait=[("CC3", 1)])
    p.y(lambda e: e.dma_start(out=Z3, in_=c.dram["uz_out"].ap()[:, 1024:1040]))
    mk_uzout = p.mark("DY")
    mv = p.v(lambda e: e.reciprocal(out=Z3R, in_=Z3), wait=[("DY", mk_uzout)])
    for h in range(ATT1_H):
        mv = p.v(lambda e, h=h: e.tensor_scalar(
            out=c.UBUF[0:64, 64 * h:64 * h + 64],
            in0=c.UBUF[0:64, 64 * h:64 * h + 64],
            scalar1=Z3R[:, h:h + 1], scalar2=None, op0=ALU.mult), wait=[("V", mv)])
    mk_k3v = mv

    convb_mk = [0, 0]
    pc_s = p.mark("S")
    pc_v = p.mark("V")
    for h in range(ATT1_H):
        dc, pb = h // 2, 64 * (h % 2)
        cbuf = h % 2
        p.g(lambda e, h=h, cbuf=cbuf: e.dma_start(
            out=bfa(c, CONVB[cbuf], 128, 384).rearrange("p (b j) -> p b j", b=3),
            in_=c.din["CONVT"][h].rearrange("b p j -> p b j")),
            wait=[("T", convb_mk[cbuf])], dma="DG5")
        mk_cb = p.mark("DG5")
        NMX16 = CE[:, 256:272]
        RCP16 = CE[:, 288:304]
        K1FB = 51200
        K1NB = 52224
        for tc in range(16):
            p.t(lambda e, dc=dc, pb=pb, tc=tc: e.matmul(
                c.PSA[:, 64 * tc:64 * tc + 64],
                lhsT=c.BFA[pb:pb + 64, BFA_QT1 + 2048 * dc + 128 * tc:
                           BFA_QT1 + 2048 * dc + 128 * tc + 128],
                rhs=c.BFA[pb:pb + 64, BFA_QLB + 512 + 64 * dc:BFA_QLB + 512 + 64 * dc + 64],
                start=True, stop=True),
                wait=([("S", pc_s), ("V", pc_v)] if tc == 0 else []),
                inc=(tc == 15))
        mk_k1mm = p.mark("T")
        mk_exp1 = p.s(lambda e: e.activation(
            out=bfa(c, K1FB, 128, 1024), in_=c.PSA[:, :], func=AF.Exp),
            wait=[("T", mk_k1mm)])
        sum_v = p.v(lambda e: e.tensor_reduce(
            out=RCP16, in_=bfa(c, K1FB, 128, 1024).rearrange(
                "p (s j) -> p s j", s=16),
            axis=AX.X, op=ALU.add),
            wait=[("S", mk_exp1)])
        sum_v = p.v(lambda e: e.reciprocal(out=RCP16, in_=RCP16), wait=[("V", sum_v)])
        mult_vs = []
        for tc in range(16):
            sum_v = p.v(lambda e, tc=tc: e.tensor_scalar(
                out=bfa(c, K1NB + 64 * tc, 128, 64),
                in0=bfa(c, K1FB + 64 * tc, 128, 64),
                scalar1=RCP16[:, tc:tc + 1], scalar2=None, op0=ALU.mult),
                wait=[("V", sum_v)])
            mult_vs.append(sum_v)
        k1t_s = [pc_s, pc_s]
        for grp in range(2):
            pt = 1024 * grp
            mkt = 0
            for j in range(8):
                tc = 8 * grp + j
                mkt = p.t(lambda e, tc=tc, pt=pt, j=j: e.transpose(
                    c.PSE[0:64, pt + 128 * j:pt + 128 * j + 128],
                    in_=bfa(c, K1NB + 64 * tc, 128, 64),
                    identity=c.IDENTB[:, :]),
                    wait=[("V", mult_vs[tc])] + ([("S", k1t_s[grp])] if j == 0 else []),
                    inc=(j == 7))
            k1t_s[grp] = p.s(lambda e, grp=grp, pt=pt: e.copy(
                out=c.K1T[0:64, 1024 * grp:1024 * grp + 1024],
                in_=c.PSE[0:64, pt:pt + 1024]),
                wait=[("T", mkt)])
        mk_k1t = p.mark("S")
        for qs in range(QS):
            ps_t = c.PSB if qs < 2 else c.PSC
            p.t(lambda e, h=h, qs=qs, ps_t=ps_t: e.matmul(
                ps_t[0:64, 512 * (qs % 2):512 * (qs % 2) + 512],
                lhsT=c.INV2[0:64, 64 * h:64 * h + 64],
                rhs=c.K1T[0:64, 512 * qs:512 * qs + 512], start=True, stop=True),
                wait=([("S", max(mk_k1t, mk_inv)), ("V", pc_v)] if qs == 0 else []),
                inc=(qs == 3))
        mkt = p.mark("T")
        p.s(lambda e: e.copy(out=c.ATTNT[0:64, 0:1024], in_=c.PSB[0:64, :]),
            wait=[("T", mkt)])
        mks = p.s(lambda e: e.copy(out=c.ATTNT[0:64, 1024:2048], in_=c.PSC[0:64, :]))
        mk_at = p.mark("S")
        for qs in range(QS):
            ps_t = c.PSB if qs < 2 else c.PSC
            p.t(lambda e, h=h, pb=pb, qs=qs, ps_t=ps_t: e.matmul(
                ps_t[pb:pb + 64, 512 * (qs % 2):512 * (qs % 2) + 512],
                lhsT=c.UBUF[0:64, 64 * h:64 * h + 64],
                rhs=c.ATTNT[0:64, 512 * qs:512 * qs + 512], start=True, stop=False),
                wait=([("S", mk_at), ("V", max(mk_k3v, pc_v))] if qs == 0 else []))
        for tc in range(16):
            ps_t = c.PSB if tc < 8 else c.PSC
            pco = 512 * ((tc // 4) % 2) + 128 * (tc % 4)
            for b, delta in enumerate((-1, 0, 1)):
                src = tc + delta
                if src < 0:
                    lhs = bfa(c, HALO_L + 64 * h, 128, 64)
                elif src > 15:
                    lhs = bfa(c, HALO_R + 64 * h, 128, 64)
                else:
                    lhs = bfa(c, BFA_V1 + 1024 * src + 64 * h, 128, 64)
                p.t(lambda e, lhs=lhs, b=b, cbuf=cbuf, pb=pb, ps_t=ps_t, pco=pco: e.matmul(
                    ps_t[pb:pb + 64, pco:pco + 128], lhsT=lhs,
                    rhs=bfa(c, CONVB[cbuf] + 128 * b, 128, 128),
                    start=False, stop=(b == 2)),
                    wait=([("DG5", max(mk_cb, mk_halo)), ("DY", mk_v1)]
                          if (tc == 0 and b == 0) else []),
                    inc=(tc == 15 and b == 2))
        mkt = p.mark("T")
        if h == ATT1_H - 1:
            convb_mk[cbuf] = mkt
        else:
            convb_mk[cbuf] = mkt
        for qs in range(QS):
            ps_t = c.PSB if qs < 2 else c.PSC
            mv = p.v(lambda e, dc=dc, pb=pb, qs=qs, ps_t=ps_t: e.tensor_tensor(
                out=c.X[dc][pb:pb + 64, 512 * qs:512 * qs + 512],
                in0=c.X[dc][pb:pb + 64, 512 * qs:512 * qs + 512],
                in1=ps_t[pb:pb + 64, 512 * (qs % 2):512 * (qs % 2) + 512],
                op=ALU.add), wait=[("T", mkt)])
        pc_v = p.mark("V")
        pc_s = p.mark("S")
    c.mk_xb_free = max(c.mk_xb_free, p.mark("T"))


def host_prep(inputs, sublayers=(0, 1, 2, 3, 4, 5)):
    rep = {}
    consts = np.zeros((128, 384), dtype=np.float32)
    consts[:, 0:128] = 1.0
    consts[:, 128:256] = np.eye(128, dtype=np.float32)
    consts[:, 256:264] = -1e30
    consts[:, 264] = 1e-5
    consts[:, 265] = -ATT0_SHIFT
    rep["CONSTS"] = consts
    need_ffn0 = 0 in sublayers or 2 in sublayers
    need_attn0 = 1 in sublayers
    need_moe = 3 in sublayers or 5 in sublayers
    need_attn1 = 4 in sublayers
    if need_ffn0:
        w = np.asarray(inputs["L0_ffn_weight"], dtype=np.float32)
        rep["W0T"] = np.ascontiguousarray(w.T).reshape(NC, 128, D)
        rep["DISHA"] = np.asarray(inputs["L0_ffn_disha"], dtype=np.float32)
        ssel = np.zeros((128, 64), dtype=np.float32)
        for pp in range(128):
            ssel[pp, pp % 64] = 1.0
        rep["SSEL"] = ssel
    if need_attn0:
        pq = np.asarray(inputs["L0_attn_proj_q"], dtype=np.float32)
        rep["PQ"] = np.ascontiguousarray(pq.reshape(8, NC, 128, 128))
        rep["PK"] = np.ascontiguousarray(
            np.asarray(inputs["L0_attn_proj_k"], dtype=np.float32).reshape(NC, 128, 128))
        rep["PV"] = np.ascontiguousarray(
            np.asarray(inputs["L0_attn_proj_v"], dtype=np.float32).reshape(NC, 128, 128))
        po = np.asarray(inputs["L0_attn_proj_o"], dtype=np.float32)
        rep["POT"] = np.ascontiguousarray(po.transpose(0, 2, 1))
    if need_moe:
        w1 = np.asarray(inputs["L1_ffn_w1"], dtype=np.float32)
        w3 = np.asarray(inputs["L1_ffn_w3"], dtype=np.float32)
        w2 = np.asarray(inputs["L1_ffn_w2"], dtype=np.float32)
        p1 = w1.reshape(EXPERTS, HC, 128, NC, 128).transpose(0, 1, 4, 3, 2).reshape(
            EXPERTS, HC, 128, D)
        p3 = w3.reshape(EXPERTS, HC, 128, NC, 128).transpose(0, 1, 4, 3, 2).reshape(
            EXPERTS, HC, 128, D)
        p2 = w2.reshape(EXPERTS, NC, 128, HC, 128).transpose(0, 3, 4, 1, 2).reshape(
            EXPERTS, HC, 128, D)
        rep["MS"] = np.ascontiguousarray(np.stack([p1, p3, p2], axis=2))
        rep["GWT"] = np.ascontiguousarray(
            np.asarray(inputs["L1_ffn_gate_w"], dtype=np.float32).T.reshape(NC, 128, 8))
    if need_attn1:
        wq = np.asarray(inputs["L1_attn_Wq"], dtype=np.float32)
        wk = np.asarray(inputs["L1_attn_Wk"], dtype=np.float32)
        wv = np.asarray(inputs["L1_attn_Wv"], dtype=np.float32)
        rep["WQJ"] = np.ascontiguousarray(wq.reshape(NC, 128, NC, 128).transpose(2, 0, 1, 3))
        rep["WKJ"] = np.ascontiguousarray(wk.reshape(NC, 128, NC, 128).transpose(2, 0, 1, 3))
        rep["WV"] = np.ascontiguousarray(wv.reshape(NC, 128, D))
        rep["BQ"] = np.ascontiguousarray(
            np.asarray(inputs["L1_attn_bq"], dtype=np.float32).reshape(8, 128).T)
        rep["BK"] = np.ascontiguousarray(
            np.asarray(inputs["L1_attn_bk"], dtype=np.float32).reshape(8, 128).T)
        rep["BV"] = np.asarray(inputs["L1_attn_bv"], dtype=np.float32).reshape(1, D)
        cw = np.asarray(inputs["L1_attn_conv_w"], dtype=np.float32).reshape(ATT1_H, 65)
        convt = np.zeros((ATT1_H, 3, 128, 128), dtype=np.float32)
        jj = np.arange(128)
        for h in range(ATT1_H):
            for b, delta in enumerate((-1, 0, 1)):
                for pp in range(128):
                    t = delta * 128 + pp - jj + 32
                    m = (t >= 0) & (t < 65)
                    convt[h, b, pp, m] = cw[h, t[m]]
        rep["CONVT"] = convt
        eyes = np.zeros((64, 3072), dtype=np.float32)
        for k, v in enumerate((7.0, 15.0, 13.0)):
            for h in range(ATT1_H):
                eyes[:, 1024 * k + 64 * h:1024 * k + 64 * h + 64] = v * np.eye(64)
        rep["EYES"] = eyes

    x = np.asarray(inputs["L0_attn_x"], dtype=np.float32)
    per_core = []
    for core in range(8):
        g, r = core // 4, core % 4
        d = dict(rep)
        d["xin"] = np.ascontiguousarray(x[g, Q * r:Q * (r + 1), :].T)
        if need_attn1:
            hi = np.full((128, 2), 99999, dtype=np.int32)
            if r > 0:
                hi[:, 0] = 256 * (r - 1) + 128 + np.arange(128)
            if r < 3:
                hi[:, 1] = 256 * (r + 1) + np.arange(128)
            d["HALOIDX"] = hi
        per_core.append(d)
    return per_core


def run_launch(sublayers, per_core_inputs):
    nc = build_program(tuple(sublayers))
    res = run_bass_kernel_spmd(nc, per_core_inputs, core_ids=list(range(8)))
    return [res.results[cc]["xout"] for cc in range(8)]


LAUNCH_GROUPS = [(0, 1, 2, 3, 4, 5)]


def kernel(**inputs):
    x = np.asarray(inputs["L0_attn_x"], dtype=np.float32)
    cur = [None] * 8
    for gi, grp in enumerate(LAUNCH_GROUPS):
        per_core = host_prep(inputs, grp)
        if gi > 0:
            for core in range(8):
                per_core[core]["xin"] = cur[core]
        outs = run_launch(grp, per_core)
        cur = [np.ascontiguousarray(o) for o in outs]
    y = np.empty((2, S_FULL, D), dtype=np.float32)
    for core in range(8):
        g, r = core // 4, core % 4
        y[g, Q * r:Q * (r + 1), :] = cur[core].T
    return y
